# version 06
# baseline (speedup 1.0000x reference)
import contextlib
import numpy as np
import concourse.bass as bass
import concourse.mybir as mybir
from concourse.bass_utils import run_bass_kernel_spmd

F32 = mybir.dt.float32
AF = mybir.ActivationFunctionType
ALU = mybir.AluOpType
AX = mybir.AxisListType

B_LOC, T, D, H, DH, DFF = 2, 2048, 1024, 16, 64, 4096
NT, NF = T // 128, D // 128
NPAIR = H // 2
EPS = 1e-5
NEGM = -1.0e9
STAGE = 4
PAIR_LIMIT = None
DBG = False



class Buf:
    __slots__ = ("ws", "rs", "name")

    def __init__(self, name=""):
        self.ws = []
        self.rs = []
        self.name = name


class Op:
    __slots__ = ("chan", "fn", "waits", "kind", "used", "val", "fwaits", "dstream")

    def __init__(self, chan, fn, waits, kind, dstream=0):
        self.chan = chan
        self.fn = fn
        self.waits = waits
        self.kind = kind
        self.used = False
        self.val = 0
        self.fwaits = ()
        self.dstream = dstream


NDSTREAM = 2


class Chan:
    def __init__(self, name):
        self.name = name
        self.last = None
        self.sem_c = None
        self.sem_d = [None] * NDSTREAM
        self.pending = []


class K:
    def __init__(self, nc):
        self.nc = nc
        self.ops = []
        self.chans = {n: Chan(n) for n in ("pe", "dve", "act", "gp", "sd")}

    def emit(self, chan, fn, reads=(), writes=(), kind="c", dstream=0):
        ch = self.chans[chan]
        waits, seen = [], set()

        def add(op):
            if op is not None and id(op) not in seen:
                seen.add(id(op))
                waits.append(op)

        for o in ch.pending:
            add(o)
        ch.pending = []
        for b in reads:
            for w in b.ws:
                add(w)
        for b in writes:
            for r in b.rs:
                add(r)
            for w in b.ws:
                add(w)
        op = Op(ch, fn, waits, kind, dstream)
        self.ops.append(op)
        ch.last = op
        for b in reads:
            b.rs.append(op)
        for b in writes:
            b.ws = [op]
            b.rs = []
        return op

    def barrier(self):
        lasts = [c.last for c in self.chans.values() if c.last is not None]
        for ch in self.chans.values():
            for o in lasts:
                if o.chan is not ch and o not in ch.pending:
                    ch.pending.append(o)

    def finish(self):
        lasts = [c.last for c in self.chans.values() if c.last is not None]
        ch = self.chans["sd"]
        op = Op(ch, None, [o for o in lasts if o.chan is not ch], "c")
        self.ops.append(op)

    def finalize(self):
        for op in self.ops:
            for w in op.waits:
                w.used = True
        for ch in self.chans.values():
            ch._cc = 0
            ch._cd = [0] * NDSTREAM
            ch._wmax = {}
        cd_snap = []
        run = {(n, s): 0 for n in self.chans for s in range(NDSTREAM)}
        for op in self.ops:
            cd_snap.append(dict(run))
            if op.kind == "d":
                run[(op.chan.name, op.dstream)] += 16
                op.chan._cd[op.dstream] += 16
                op.val = op.chan._cd[op.dstream]
            elif op.used:
                op.chan._cc += 1
                op.val = op.chan._cc
        for i, op in enumerate(self.ops):
            fw = []
            for w in op.waits:
                if w.chan is op.chan and w.kind == "c":
                    continue
                if w.kind == "d":
                    sem = w.chan.sem_d[w.dstream]
                    val = cd_snap[i][(w.chan.name, w.dstream)]
                    if val < w.val:
                        val = w.val
                else:
                    sem = w.chan.sem_c
                    val = w.val
                key = id(sem)
                if op.chan._wmax.get(key, -1) >= val:
                    continue
                op.chan._wmax[key] = val
                fw.append((sem, val))
            op.fwaits = fw

    def run(self, eng, name):
        ch = self.chans[name]
        for op in self.ops:
            if op.chan is not ch:
                continue
            for sem, val in op.fwaits:
                eng.wait_ge(sem, val)
            if op.fn is not None:
                ins = op.fn(eng)
                if ins is not None:
                    if op.kind == "d":
                        ins.then_inc(ch.sem_d[op.dstream], 16)
                    elif op.used:
                        ins.then_inc(ch.sem_c, 1)





def build_program(stage=STAGE):
    nc = bass.Bass(trn_type="TRN2", target_bir_lowering=False,
                   detect_race_conditions=False, dynamic_dma_scratch_size=8192)
    k = K(nc)

    def din(name, shape):
        return nc.dram_tensor(name, shape, F32, kind="ExternalInput").ap()

    x_in = din("x", [B_LOC, T, D])
    wqkv = din("wqkv", [3 * D, D])
    wo_a = din("wo_a", [D, D])
    wq_b, wk_b, wv_b, wo_b = (din(n, [D, D]) for n in ("wq_b", "wk_b", "wv_b", "wo_b"))
    w1 = din("w1", [D, DFF])
    w2 = din("w2", [DFF, D])
    cb = din("cb", [D, D])
    ident_d = din("identc", [128, 128])
    p2_d = din("p2c", [128, 128])
    ones_d = din("onesc", [128, 128])
    cm_d = din("cmc", [128, 1152])
    sm0_d = din("sm0c", [128, 512])
    c64_d = din("c64c", [128, T])
    s64_d = din("s64c", [128, T])
    bqh_d = din("bqhc", [128, 8])
    bkh_d = din("bkhc", [128, 8])
    bvf_d = din("bvfc", [64, 16])
    b1h_d = din("b1hc", [128, 32])
    b2r_d = din("b2rc", [1, D])
    bor_d = din("borc", [1, D])
    fbr_d = din("fbrc", [1, D])
    scb_d = din("scbc", [128, D])
    g1r_d = din("g1rc", [128, D])
    b1r_d = din("b1rc", [128, D])
    g2r_d = din("g2rc", [128, D])
    b2lr_d = din("b2lrc", [128, D])
    dsc_d = din("dscc", [1, 1])
    eps_d = din("epsc", [128, 1])
    dbt_d = din("dbtc", [128, 28])

    y_out = nc.dram_tensor("y", [B_LOC, T, D], F32, kind="ExternalOutput").ap()
    dbg_out = nc.dram_tensor("dbg", [128, 4 * 512], F32, kind="ExternalOutput").ap() if DBG \
        else None
    dbg2_out = nc.dram_tensor("dbg2", [64, 2 * 32 * 512], F32,
                              kind="ExternalOutput").ap() if DBG else None
    dbg3_out = nc.dram_tensor("dbg3", [128, 8192], F32,
                              kind="ExternalOutput").ap() if DBG else None
    k.dbg3 = dbg3_out
    xspill = nc.dram_tensor("xspill", [T, D], F32, kind="Internal")

    gstack = contextlib.ExitStack()
    _uid = [0]

    def mk(stk, name, shape, dt=F32, side="left"):
        _uid[0] += 1
        h = stk.enter_context(nc.sbuf_tensor(f"{name}_u{_uid[0]}", shape, dt,
                                             side=side))
        return h, Buf(name)

    ident, b_ident = mk(gstack, "identc_s", [128, 128])
    ones, b_ones = mk(gstack, "onesc_s", [128, 128])
    smalls, b_smalls = mk(gstack, "smalls", [128, 64])
    stats, b_stats = mk(gstack, "stats", [128, 4 * NT])
    nrm, b_nrm = mk(gstack, "nrm", [1, 16])
    sqs, b_sqs = mk(gstack, "sqs", [128, 512])
    ptil, b_ptil = [None, None], [None, None]
    for i in range(2):
        ptil[i], b_ptil[i] = mk(gstack, f"ptil{i}", [128, 512])
    recbc, b_recbc = mk(gstack, "recbc", [128, 512])
    rcw, b_rcw = mk(gstack, "rcw", [128, 256])

    PS, b_PS = [], []
    for i in range(8):
        h = gstack.enter_context(nc.psum_tensor(f"ps{i}", [128, 512], F32))
        PS.append(h)
        b_PS.append(Buf(f"ps{i}"))

    def ld(dst, src, b):
        k.emit("sd", lambda e, d=dst, s=src: e.dma_start(out=d, in_=s),
               reads=(b,), writes=(b,), kind="d")

    ld(ident[:, :], ident_d, b_ident)
    ld(ones[:, :], ones_d, b_ones)
    ld(smalls[:, 0:8], bqh_d, b_smalls)
    ld(smalls[:, 8:16], bkh_d, b_smalls)
    ld(smalls[0:64, 16:32], bvf_d, b_smalls)
    ld(smalls[0:1, 33:34], dsc_d, b_smalls)
    ld(smalls[:, 35:36], eps_d, b_smalls)
    k.emit("dve", lambda e: e.memset(rcw[:, 0:128], 1.0), writes=(b_rcw,))
    k.emit("pe", lambda e: e.matmul(PS[7][:, 0:1], ones[0:1, 0:128],
                                    smalls[0:1, 33:34], start=True, stop=True),
           reads=(b_ones, b_smalls), writes=(b_PS[7],))
    k.emit("dve", lambda e: e.tensor_scalar_mul(smalls[:, 32:33], PS[7][:, 0:1], -1.0),
           reads=(b_PS[7],), writes=(b_smalls,))
    k.barrier()

    def layernorm(xt, b_xt, out_tiles, b_out, lnt=None, b_lnt=None,
                  gain=None, b_gain=None, bias=None, b_bias=None):
        for tt in range(NT):
            src = xt(tt)
            k.emit("dve", lambda e, s=src, c=tt: e.reduce_sum(
                stats[:, c:c + 1], s, axis=AX.X), reads=(b_xt,), writes=(b_stats,))
            k.emit("act", lambda e, s=src, c=tt: e.activation(
                PS[7][:, 0:512], s[:, 0:512], AF.Square,
                accum_out=stats[:, NT + c:NT + c + 1]),
                   reads=(b_xt,), writes=(b_PS[7], b_stats))
            k.emit("act", lambda e, s=src, c=tt: e.activation(
                PS[7][:, 0:512], s[:, 512:1024], AF.Square,
                accum_out=stats[:, 2 * NT + c:2 * NT + c + 1]),
                   reads=(b_xt,), writes=(b_PS[7], b_stats))
        k.emit("dve", lambda e: e.tensor_add(stats[:, NT:2 * NT], stats[:, NT:2 * NT],
                                             stats[:, 2 * NT:3 * NT]),
               reads=(b_stats,), writes=(b_stats,))
        k.emit("gp", lambda e: e.tensor_scalar_mul(stats[:, 0:NT], stats[:, 0:NT],
                                                   1.0 / D),
               reads=(b_stats,), writes=(b_stats,))
        k.emit("dve", lambda e: e.tensor_scalar_mul(stats[:, NT:2 * NT],
                                                    stats[:, NT:2 * NT], 1.0 / D),
               reads=(b_stats,), writes=(b_stats,))
        k.emit("gp", lambda e: e.tensor_mul(stats[:, 3 * NT:4 * NT],
                                            stats[:, 0:NT], stats[:, 0:NT]),
               reads=(b_stats,), writes=(b_stats,))
        k.emit("dve", lambda e: e.tensor_sub(stats[:, NT:2 * NT],
                                             stats[:, NT:2 * NT],
                                             stats[:, 3 * NT:4 * NT]),
               reads=(b_stats,), writes=(b_stats,))
        k.emit("gp", lambda e: e.tensor_scalar_add(rcw[:, 0:NT],
                                                    stats[:, NT:2 * NT], EPS),
               reads=(b_stats, b_rcw), writes=(b_rcw,))
        k.emit("act", lambda e: e.activation(rcw[:, 128:256], rcw[:, 0:128], AF.Ln),
               reads=(b_rcw,), writes=(b_rcw,))
        k.emit("act", lambda e: e.activation(rcw[:, 128:256], rcw[:, 128:256],
                                             AF.Exp, scale=-0.5),
               reads=(b_rcw,), writes=(b_rcw,))
        for _nr in range(2):
            k.emit("dve", lambda e: e.tensor_mul(sqs[:, 0:128], rcw[:, 128:256],
                                                 rcw[:, 128:256]),
                   reads=(b_rcw,), writes=(b_sqs,))
            k.emit("dve", lambda e: e.tensor_mul(sqs[:, 0:128], sqs[:, 0:128],
                                                 rcw[:, 0:128]),
                   reads=(b_rcw, b_sqs), writes=(b_sqs,))
            k.emit("dve", lambda e: e.tensor_scalar(sqs[:, 0:128], sqs[:, 0:128],
                                                    scalar1=-0.5, scalar2=1.5,
                                                    op0=ALU.mult, op1=ALU.add),
                   reads=(b_sqs,), writes=(b_sqs,))
            k.emit("dve", lambda e: e.tensor_mul(rcw[:, 128:256], rcw[:, 128:256],
                                                 sqs[:, 0:128]),
                   reads=(b_sqs, b_rcw), writes=(b_rcw,))
        for tt in range(NT):
            src = xt(tt)
            for half in range(2):
                if out_tiles is not None:
                    dsub = out_tiles(tt)[:, half * 512:(half + 1) * 512]
                    b_d = b_out
                else:
                    dsub = ptil[half][:, :]
                    b_d = b_ptil[half]
                k.emit("dve", lambda e, s=src, d=dsub, c=tt, hh=half: e.tensor_scalar(
                    d, s[:, hh * 512:(hh + 1) * 512], scalar1=stats[:, c:c + 1],
                    scalar2=rcw[:, 128 + c:129 + c],
                    op0=ALU.subtract, op1=ALU.mult),
                       reads=(b_xt, b_stats, b_rcw), writes=(b_d,))
                if gain is not None:
                    k.emit("dve", lambda e, d=dsub, hh=half: e.tensor_mul(
                        d, d, gain[:, hh * 512:(hh + 1) * 512]),
                           reads=(b_d, b_gain), writes=(b_d,))
                    k.emit("dve", lambda e, d=dsub, hh=half: e.tensor_add(
                        d, d, bias[:, hh * 512:(hh + 1) * 512]),
                           reads=(b_d, b_bias), writes=(b_d,))
            if lnt is None:
                continue
            for half in range(2):
                bank = half
                if out_tiles is not None:
                    srcs = [out_tiles(tt)[:, (half * 4 + j) * 128:
                                          (half * 4 + j + 1) * 128] for j in range(4)]
                    b_s = b_out
                else:
                    srcs = [ptil[half][:, j * 128:(j + 1) * 128] for j in range(4)]
                    b_s = b_ptil[half]
                for j in range(4):
                    k.emit("pe", lambda e, s=srcs[j], bk=bank, j=j: e.transpose(
                        PS[bk][:, j * 128:(j + 1) * 128], s, ident[:, :]),
                           reads=(b_s, b_ident), writes=(b_PS[bank],))
                out_ap = bass.AP(lnt, (half * 4) * T + tt * 128,
                                 [[NF * T, 128], [T, 4], [1, 128]])
                k.emit("dve", lambda e, o=out_ap, bk=bank: e.tensor_copy(
                    o, PS[bk][:, 0:512]),
                       reads=(b_PS[bank],), writes=(b_lnt,))

    def rank1(bank, row_base, col0, w, start, stop, b_rowsrc, rowsrc):
        src = rowsrc
        b_src = b_rowsrc
        k.emit("pe", lambda e: e.matmul(
            PS[bank][:, 0:w], ones[row_base:row_base + 1, 0:128],
            src[row_base:row_base + 1, col0:col0 + w], start=start, stop=stop),
               reads=(b_ones, b_src), writes=(b_PS[bank],))

    def bound_bcast(qsb, b_qsb, qbase, ksb, b_ksb, kcol):
        k.emit("act", lambda e: e.activation(sqs[qbase:qbase + 64, :],
                                             qsb[qbase:qbase + 64, 0:512], AF.Square),
               reads=(b_qsb,), writes=(b_sqs,))
        k.emit("pe", lambda e: e.matmul(PS[5][0:1, 0:512],
                                        ones[qbase:qbase + 64, 0:1],
                                        sqs[qbase:qbase + 64, :],
                                        start=True, stop=True),
               reads=(b_sqs, b_ones), writes=(b_PS[5],))
        k.emit("dve", lambda e: e.reduce_max(nrm[0:1, 10:11], PS[5][0:1, 0:512],
                                             axis=AX.X),
               reads=(b_PS[5],), writes=(b_nrm,))
        k.emit("dve", lambda e, kc=kcol: e.tensor_add(
            nrm[0:1, 11:12], nrm[0:1, 10:11], nrm[0:1, kc:kc + 1]),
               reads=(b_nrm,), writes=(b_nrm,))
        k.emit("pe", lambda e: e.matmul(PS[5][:, 0:1], ones[0:1, 0:128],
                                        nrm[0:1, 11:12], start=True, stop=True),
               reads=(b_ones, b_nrm), writes=(b_PS[5],))
        k.emit("dve", lambda e: e.tensor_scalar_mul(smalls[:, 34:35],
                                                    PS[5][:, 0:1], -1.0 / 16.0),
               reads=(b_PS[5],), writes=(b_smalls,))

    def knorm_pass(ksb, b_ksb, cols_a, cols_b):
        for c in range(4):
            k.emit("act", lambda e, c=c: e.activation(
                sqs[:, :], ksb[:, c * 512:(c + 1) * 512], AF.Square),
                   reads=(b_ksb,), writes=(b_sqs,))
            for base, col in ((0, c), (64, 4 + c)):
                k.emit("pe", lambda e, b=base: e.matmul(
                    PS[5][0:1, 0:512], ones[b:b + 64, 0:1],
                    sqs[b:b + 64, :], start=True, stop=True),
                       reads=(b_sqs, b_ones), writes=(b_PS[5],))
                k.emit("dve", lambda e, col=col: e.reduce_max(
                    nrm[0:1, col:col + 1], PS[5][0:1, 0:512], axis=AX.X),
                       reads=(b_PS[5],), writes=(b_nrm,))
        k.emit("dve", lambda e, ca=cols_a: e.reduce_max(
            nrm[0:1, ca:ca + 1], nrm[0:1, 0:4], axis=AX.X),
               reads=(b_nrm,), writes=(b_nrm,))
        k.emit("dve", lambda e, cb=cols_b: e.reduce_max(
            nrm[0:1, cb:cb + 1], nrm[0:1, 4:8], axis=AX.X),
               reads=(b_nrm,), writes=(b_nrm,))

    def pv_norm_evac(atn_ap, b_atn, pvb, bcb):
        k.emit("act", lambda e: e.activation(recbc[64:65, 0:512],
                                             PS[pvb][64:65, 0:512], AF.Ln),
               reads=(b_PS[pvb],), writes=(b_recbc,))
        k.emit("act", lambda e: e.activation(recbc[64:65, 0:512],
                                             recbc[64:65, 0:512],
                                             AF.Exp, scale=-1.0),
               reads=(b_recbc,), writes=(b_recbc,))
        k.emit("dve", lambda e: e.tensor_mul(sqs[64:65, 0:512],
                                             recbc[64:65, 0:512],
                                             PS[pvb][64:65, 0:512]),
               reads=(b_recbc, b_PS[pvb]), writes=(b_sqs,))
        k.emit("gp", lambda e: e.tensor_scalar(sqs[64:65, 0:512],
                                               sqs[64:65, 0:512],
                                               scalar1=-1.0, scalar2=2.0,
                                               op0=ALU.mult, op1=ALU.add),
               reads=(b_sqs,), writes=(b_sqs,))
        k.emit("dve", lambda e: e.tensor_mul(recbc[64:65, 0:512],
                                             recbc[64:65, 0:512],
                                             sqs[64:65, 0:512]),
               reads=(b_sqs, b_recbc), writes=(b_recbc,))
        k.emit("pe", lambda e: e.matmul(PS[bcb][0:64, 0:512], ones[64:65, 0:64],
                                        recbc[64:65, 0:512], start=True, stop=True),
               reads=(b_ones, b_recbc), writes=(b_PS[bcb],))
        k.emit("dve", lambda e: e.tensor_copy(recbc[0:64, 0:512], PS[bcb][0:64, 0:512]),
               reads=(b_PS[bcb],), writes=(b_recbc,))
        k.emit("dve", lambda e: e.tensor_mul(atn_ap, PS[pvb][0:64, 0:512],
                                             recbc[0:64, 0:512]),
               reads=(b_PS[pvb], b_recbc), writes=(b_atn,))

    def do_seq(s):
        xs = contextlib.ExitStack()
        x1stk = contextlib.ExitStack()
        X, b_X = mk(x1stk, f"X{s}", [128, NT * D], side="right")

        def xtile(tt, X=X):
            return X[:, tt * D:(tt + 1) * D]

        xv = bass.AP(x_in.tensor, s * T * D, [[D, 128], [128 * D, NT], [1, D]])
        k.emit("sd", lambda e, xv=xv, X=X: e.dma_start(out=X[:, :], in_=xv),
               writes=(b_X,), kind="d")
        k.barrier()

        LNT, b_LNT = mk(xs, f"LNT{s}", [128, NF * T])

        def lnt_mv(f, c0, w, LNT=LNT):
            return LNT[:, f * T + c0:f * T + c0 + w]

        layernorm(xtile, b_X, None, None, lnt=LNT, b_lnt=b_LNT)
        k.barrier()
        a1 = contextlib.ExitStack()
        p2m, b_p2m = mk(a1, "p2m", [128, 128], side="right")
        cm, b_cm = mk(a1, "cm", [128, 896], side="right")
        rtc, b_rtc = zip(*[mk(a1, f"rtc{i}", [128, 512], side="right") for i in range(2)])
        rts, b_rts = zip(*[mk(a1, f"rts{i}", [128, 512], side="right") for i in range(2)])
        kp, b_kp = mk(a1, "kp", [128, T], side="right")
        vp, b_vp = mk(a1, "vp", [128, NT * 130], side="right")
        qch, b_qch = zip(*[mk(a1, f"qch{i}", [128, 512], side="right") for i in range(2)])
        qraw, b_qraw = zip(*[mk(a1, f"qraw{i}", [128, 512], side="right") for i in range(2)])
        qswp, b_qswp = zip(*[mk(a1, f"qswp{i}", [128, 512], side="right") for i in range(2)])
        atn, b_atn = zip(*[mk(a1, f"atn{i}", [64, 512], side="right") for i in range(2)])
        wst, b_wst = zip(*[mk(a1, f"wst{i}", [128, D], side="right") for i in range(2)])
        wqkT, b_wqkT = mk(a1, "wqkT", [128, NF * 256], side="right")
        wvT, b_wvT = mk(a1, "wvT", [128, NF * 128], side="right")
        woTa, b_woTa = mk(a1, "woTa", [64, D], side="right")
        woTb, b_woTb = mk(a1, "woTb", [64, D], side="right")

        ld(p2m[:, :], p2_d, b_p2m)
        ld(cm[:, :], cm_d[:, 128:1024], b_cm)
        for tt in range(NT):
            k.emit("gp", lambda e, c=tt: e.memset(
                vp[:, c * 130 + 64:c * 130 + 65], 1.0), writes=(b_vp,))
            k.emit("gp", lambda e, c=tt: e.memset(
                vp[:, c * 130 + 129:c * 130 + 130], 1.0), writes=(b_vp,))

        ruse = [0]

        def rope_chunk(dst_ap, b_dst, bank, c):
            ri = ruse[0] % 2
            ruse[0] += 1
            qr, b_qr = qraw[ri], b_qraw[ri]
            qs, b_qs = qswp[ri], b_qswp[ri]
            k.emit("dve", lambda e, bk=bank, qr=qr: e.tensor_copy(
                qr[:, :], PS[bk][:, 0:512]),
                   reads=(b_PS[bank],), writes=(b_qr,))
            for sl in range(4):
                srcl = [32, 0, 96, 64][sl]
                k.emit("sd", lambda e, sl=sl, srcl=srcl, qr=qr, qs=qs: e.dma_start(
                    out=qs[sl * 32:(sl + 1) * 32, :], in_=qr[srcl:srcl + 32, :]),
                       reads=(b_qr,), writes=(b_qs,), kind="d", dstream=1)
            k.emit("dve", lambda e, qr=qr, ri=ri: e.tensor_mul(
                qr[:, :], qr[:, :], rtc[ri][:, :]),
                   reads=(b_qr, b_rtc[ri]), writes=(b_qr,))
            k.emit("dve", lambda e, qs=qs, ri=ri: e.tensor_mul(
                qs[:, :], qs[:, :], rts[ri][:, :]),
                   reads=(b_qs, b_rts[ri]), writes=(b_qs,))
            k.emit("dve", lambda e, d=dst_ap, qr=qr, qs=qs: e.tensor_add(
                d, qr[:, :], qs[:, :]),
                   reads=(b_qr, b_qs), writes=(b_dst,))

        def rope_tables(c):
            ri = ruse[0] % 2
            k.emit("sd", lambda e, c=c, ri=ri: e.dma_start(
                out=rtc[ri][:, :], in_=c64_d[:, c * 512:(c + 1) * 512]),
                   reads=(b_rtc[ri],), writes=(b_rtc[ri],), kind="d", dstream=1)
            k.emit("sd", lambda e, c=c, ri=ri: e.dma_start(
                out=rts[ri][:, :], in_=s64_d[:, c * 512:(c + 1) * 512]),
                   reads=(b_rts[ri],), writes=(b_rts[ri],), kind="d", dstream=1)

        nplim = PAIR_LIMIT if PAIR_LIMIT is not None else NPAIR
        wring = [0]

        for pr in range(nplim):
            def wload(off, p=pr):
                w = wring[0] % 2
                wring[0] += 1
                k.emit("sd", lambda e, p=p, off=off, w=w: e.dma_start(
                    out=wst[w][:, :],
                    in_=(wqkv[off + p * 128:off + (p + 1) * 128, :] if off >= 0 else
                         bass.AP(wo_a.tensor, p * 128,
                                 [[D, 128], [128 * D, 8], [1, 128]]))),
                       reads=(b_wst[w],), writes=(b_wst[w],), kind="d", dstream=w)
                return w
            w = wload(0)
            for f in range(NF):
                k.emit("pe", lambda e, f=f, w=w: e.transpose(
                    PS[0][:, 0:128], wst[w][:, f * 128:(f + 1) * 128], p2m[:, :]),
                       reads=(b_wst[w], b_p2m), writes=(b_PS[0],))
                k.emit("dve", lambda e, f=f: e.tensor_copy(
                    wqkT[:, f * 256:f * 256 + 128], PS[0][:, 0:128]),
                       reads=(b_PS[0],), writes=(b_wqkT,))
            w = wload(D)
            for f in range(NF):
                k.emit("pe", lambda e, f=f, w=w: e.transpose(
                    PS[1][:, 0:128], wst[w][:, f * 128:(f + 1) * 128], p2m[:, :]),
                       reads=(b_wst[w], b_p2m), writes=(b_PS[1],))
                k.emit("dve", lambda e, f=f: e.tensor_copy(
                    wqkT[:, f * 256 + 128:f * 256 + 256], PS[1][:, 0:128]),
                       reads=(b_PS[1],), writes=(b_wqkT,))
            w = wload(2 * D)
            for f in range(NF):
                k.emit("pe", lambda e, f=f, w=w: e.transpose(
                    PS[0][:, 0:128], wst[w][:, f * 128:(f + 1) * 128], ident[:, :]),
                       reads=(b_wst[w], b_ident), writes=(b_PS[0],))
                k.emit("dve", lambda e, f=f: e.tensor_copy(
                    wvT[:, f * 128:(f + 1) * 128], PS[0][:, 0:128]),
                       reads=(b_PS[0],), writes=(b_wvT,))
            w = wload(-1)
            for c in range(NF):
                k.emit("pe", lambda e, c=c, w=w: e.transpose(
                    PS[1][0:64, 0:128], wst[w][:, c * 128:c * 128 + 64], ident[:, :]),
                       reads=(b_wst[w], b_ident), writes=(b_PS[1],))
                k.emit("dve", lambda e, c=c: e.tensor_copy(
                    woTa[:, c * 128:(c + 1) * 128], PS[1][0:64, 0:128]),
                       reads=(b_PS[1],), writes=(b_woTa,))
                k.emit("pe", lambda e, c=c, w=w: e.transpose(
                    PS[1][0:64, 0:128], wst[w][:, c * 128 + 64:(c + 1) * 128],
                    ident[:, :]),
                       reads=(b_wst[w], b_ident), writes=(b_PS[1],))
                k.emit("dve", lambda e, c=c: e.tensor_copy(
                    woTb[:, c * 128:(c + 1) * 128], PS[1][0:64, 0:128]),
                       reads=(b_PS[1],), writes=(b_woTb,))

            for c in range(4):
                bank = c % 2
                rope_tables(c)
                for f in range(NF):
                    k.emit("pe", lambda e, f=f, c=c, bk=bank, w=w: e.matmul(
                        PS[bk][:, 0:512], wqkT[:, f * 256 + 128:f * 256 + 256],
                        lnt_mv(f, c * 512, 512),
                        start=(f == 0), stop=(f == NF - 1)),
                           reads=(b_wqkT, b_LNT), writes=(b_PS[bank],))
                rope_chunk(kp[:, c * 512:(c + 1) * 512], b_kp, bank, c)
            for c in range(4):
                bank = c % 2
                for f in range(NF):
                    k.emit("pe", lambda e, f=f, c=c, bk=bank, w=w: e.matmul(
                        PS[bk][:, 0:512], wvT[:, f * 128:(f + 1) * 128],
                        lnt_mv(f, c * 512, 512),
                        start=(f == 0), stop=(f == NF - 1)),
                           reads=(b_wvT, b_LNT), writes=(b_PS[bank],))
                vtb = 2 + (c % 4)
                k.emit("dve", lambda e, bk=bank, c=c: e.tensor_copy(
                    qraw[c % 2][:, :], PS[bk][:, 0:512]),
                       reads=(b_PS[bank],), writes=(b_qraw[c % 2],))
                for j in range(4):
                    tt = c * 4 + j
                    k.emit("pe", lambda e, j=j, c=c, vb=vtb: e.transpose(
                        PS[vb][:, 0:128], qraw[c % 2][:, j * 128:(j + 1) * 128],
                        ident[:, :]),
                           reads=(b_qraw[c % 2], b_ident), writes=(b_PS[vtb],))
                    k.emit("dve", lambda e, t=tt, vb=vtb: e.tensor_copy(
                        vp[:, t * 130:t * 130 + 64], PS[vb][:, 0:64]),
                           reads=(b_PS[vtb],), writes=(b_vp,))
                    k.emit("dve", lambda e, t=tt, vb=vtb: e.tensor_copy(
                        vp[:, t * 130 + 65:t * 130 + 129], PS[vb][:, 64:128]),
                           reads=(b_PS[vtb],), writes=(b_vp,))

            for qc in range(4):
                q0 = qc * 512
                qb = qc % 2
                bank = qb
                rope_tables(qc)
                for f in range(NF):
                    k.emit("pe", lambda e, f=f, qc=qc, bk=bank, w=w: e.matmul(
                        PS[bk][:, 0:512], wqkT[:, f * 256:f * 256 + 128],
                        lnt_mv(f, qc * 512, 512),
                        start=(f == 0), stop=(f == NF - 1)),
                           reads=(b_wqkT, b_LNT), writes=(b_PS[bank],))
                rope_chunk(qch[qb][:, :], b_qch[qb], bank, qc)
                ktmax = 4 * (qc + 1)
                for side in range(2):
                    base = side * 64
                    qh = qch[qb][base:base + 64, :]
                    kh = kp[base:base + 64, :]
                    pvb = 6 + side
                    first = True
                    for kt in range(ktmax):
                        k0 = kt * 128
                        sb = 2 + (kt % 4)
                        k.emit("pe", lambda e, kt=kt, sb=sb, qh=qh, kh=kh: e.matmul(
                            PS[sb][:, 0:512], kh[:, kt * 128:(kt + 1) * 128],
                            qh[:, :], start=True, stop=True),
                               reads=(b_qch[qb], b_kp), writes=(b_PS[sb],))
                        dbase = q0 - k0
                        if dbase < 127:
                            u0 = 384 + dbase
                            k.emit("dve", lambda e, sb=sb, u0=u0: e.tensor_add(
                                PS[sb][:, 0:512], PS[sb][:, 0:512], cm[:, u0:u0 + 512]),
                                   reads=(b_cm, b_PS[sb]), writes=(b_PS[sb],))
                        pt, bpt = ptil[kt % 2], b_ptil[kt % 2]
                        k.emit("act", lambda e, sb=sb, pt=pt: e.activation(
                            pt[:, :], PS[sb][:, 0:512], AF.Exp, scale=0.125),
                               reads=(b_PS[sb],), writes=(bpt,))
                        voff = 0 if side == 0 else 65
                        k.emit("pe", lambda e, kt=kt, pt=pt, fr=first, vo=voff,
                               pvb=pvb, lastk=(kt == ktmax - 1): e.matmul(
                            PS[pvb][0:65, 0:512],
                            vp[:, kt * 130 + vo:kt * 130 + vo + 65],
                            pt[:, :], start=fr, stop=lastk),
                               reads=(b_vp, bpt), writes=(b_PS[pvb],))
                        first = False
                    pv_norm_evac(atn[side][:, :], b_atn[side], pvb, 2 + (ktmax % 4))
                    woT = woTa if side == 0 else woTb
                    b_woT = b_woTa if side == 0 else b_woTb
                    for j in range(4):
                        tt = qc * 4 + j
                        obk = j % 2
                        for oc in range(2):
                            k.emit("pe", lambda e, j=j, oc=oc, bk=obk, woT=woT,
                                   side=side: e.matmul(
                                PS[bk][:, 0:512], atn[side][:, j * 128:(j + 1) * 128],
                                woT[:, oc * 512:(oc + 1) * 512],
                                start=True, stop=True),
                                   reads=(b_atn[side], b_woT), writes=(b_PS[obk],))
                            k.emit("dve", lambda e, tt=tt, oc=oc, bk=obk, X=X:
                                   e.tensor_add(
                                       X[:, tt * D + oc * 512:tt * D + (oc + 1) * 512],
                                       X[:, tt * D + oc * 512:tt * D + (oc + 1) * 512],
                                       PS[bk][:, 0:512]),
                                   reads=(b_PS[obk], b_X), writes=(b_X,))
        a1.close()
        k.barrier()

        if stage >= 2:
            emit_ffn(k, nc, mk, xs, X, b_X, xtile, LNT, b_LNT, lnt_mv, layernorm,
                     PS, b_PS, cb, scb_d, ident, b_ident, fbr_d, rank1)
            k.barrier()

        if stage >= 3:
            X, b_X = emit_attn_b(
                k, nc, mk, xs, s, X, b_X, x1stk, xtile, LNT, b_LNT, lnt_mv,
                layernorm, PS, b_PS, ident, b_ident, ones, b_ones,
                smalls, b_smalls, b1h_d, b2r_d, sm0_d, dbt_d,
                wq_b, wk_b, wv_b, wo_b, w1, w2, bor_d,
                g1r_d, b1r_d, g2r_d, b2lr_d,
                ptil, b_ptil, recbc, b_recbc, sqs, b_sqs,
                nrm, b_nrm, xspill, rank1, knorm_pass, bound_bcast,
                pv_norm_evac, ld)

            def xtile(tt, X=X):
                return X[:, tt * D:(tt + 1) * D]
            k.barrier()

        if stage >= 4:
            emit_ffn(k, nc, mk, xs, X, b_X, xtile, LNT, b_LNT, lnt_mv, layernorm,
                     PS, b_PS, cb, scb_d, ident, b_ident, fbr_d, rank1)
            k.barrier()

        yv = bass.AP(y_out.tensor, s * T * D, [[D, 128], [128 * D, NT], [1, D]])
        k.emit("sd", lambda e, yv=yv, X=X: e.dma_start(out=yv, in_=X[:, :]),
               reads=(b_X,), kind="d")
        k.barrier()
        if stage < 3:
            x1stk.close()
        xs.close()

    for s in range(B_LOC):
        do_seq(s)

    k.finish()
    nops = len(k.ops)

    with contextlib.ExitStack() as semstk:
        for name, ch in k.chans.items():
            ch.sem_c = semstk.enter_context(nc.semaphore(f"sc_{name}"))
            ch.sem_d = [semstk.enter_context(nc.semaphore(f"sd_{name}{i}"))
                        for i in range(NDSTREAM)]
        k.finalize()
        with nc.Block() as block:
            @block.tensor
            def _(eng):
                k.run(eng, "pe")

            @block.vector
            def _(eng):
                k.run(eng, "dve")

            @block.scalar
            def _(eng):
                k.run(eng, "act")

            @block.gpsimd
            def _(eng):
                k.run(eng, "gp")

            @block.sync
            def _(eng):
                k.run(eng, "sd")

    return nc, nops


def emit_ffn(k, nc, mk, xs, X, b_X, xtile, LNT, b_LNT, lnt_mv, layernorm,
             PS, b_PS, cb, scb_d, ident, b_ident, fbr_d, rank1,
             dbgbufs=None):
    layernorm(xtile, b_X, None, None, lnt=LNT, b_lnt=b_LNT)
    if DBG and dbgbufs is not None:
        stats, b_stats, rcw, b_rcw, Xb = dbgbufs
        k.emit("gp", lambda e: e.dma_start(
            out=k.dbg3[0:128, 2048:2112], in_=stats[:, :]),
               reads=(b_stats,), kind="d")
        k.emit("gp", lambda e: e.dma_start(
            out=k.dbg3[0:128, 2112:2240], in_=rcw[:, 128:256]),
               reads=(b_rcw,), kind="d")
        k.emit("gp", lambda e: e.dma_start(
            out=k.dbg3[0:128, 6144:7168], in_=Xb[:, 3 * D:4 * D]),
               reads=(b_X,), kind="d")
        k.emit("gp", lambda e: e.dma_start(
            out=k.dbg3[0:128, 4096:5120], in_=LNT[:, 0:1024]),
               reads=(b_LNT,), kind="d")
        k.barrier()
    k.barrier()
    f1 = contextlib.ExitStack()
    cbt, b_cbt = mk(f1, "cbt", [128, NF * D], side="right")
    cst, b_cst = mk(f1, "cst", [128, D], side="right")
    scb, b_scb = mk(f1, "scb", [128, D], side="right")
    rows, b_rows = mk(f1, "rowsf", [128, D], side="right")
    k.emit("sd", lambda e: e.dma_start(out=rows[64:65, :], in_=fbr_d),
           reads=(b_rows,), writes=(b_rows,), kind="d")
    k.emit("sd", lambda e: e.dma_start(out=scb[:, :], in_=scb_d),
           reads=(b_scb,), writes=(b_scb,), kind="d")
    k.emit("dve", lambda e: e.tensor_scalar_mul(scb[:, :], scb[:, :], 1.0 / 127.0),
           reads=(b_scb,), writes=(b_scb,))
    for ntile in range(NF):
        k.emit("sd", lambda e, n=ntile: e.dma_start(
            out=cst[:, :], in_=cb[n * 128:(n + 1) * 128, :]),
               reads=(b_cst,), writes=(b_cst,), kind="d")
        for f in range(NF):
            bank = f % 2
            k.emit("pe", lambda e, f=f, bk=bank: e.transpose(
                PS[bk][:, 0:128], cst[:, f * 128:(f + 1) * 128], ident[:, :]),
                   reads=(b_cst, b_ident), writes=(b_PS[bank],))
            k.emit("dve", lambda e, f=f, n=ntile, bk=bank: e.tensor_mul(
                cbt[:, f * D + n * 128:f * D + (n + 1) * 128], PS[bk][:, 0:128],
                scb[:, n * 128:(n + 1) * 128]),
                   reads=(b_PS[bank], b_scb), writes=(b_cbt,))
    for tt in range(NT):
        for oc in range(2):
            bank = 2 + ((tt * 2 + oc) % 2)
            for f in range(NF):
                k.emit("pe", lambda e, tt=tt, oc=oc, f=f, bk=bank: e.matmul(
                    PS[bk][:, 0:512], lnt_mv(f, tt * 128, 128),
                    cbt[:, f * D + oc * 512:f * D + (oc + 1) * 512],
                    start=(f == 0), stop=False),
                       reads=(b_LNT, b_cbt), writes=(b_PS[bank],))
            rank1(bank, 64, oc * 512, 512, False, True, b_rows, rows)
            k.emit("dve", lambda e, tt=tt, oc=oc, bk=bank: e.tensor_add(
                X[:, tt * D + oc * 512:tt * D + (oc + 1) * 512],
                X[:, tt * D + oc * 512:tt * D + (oc + 1) * 512], PS[bk][:, 0:512]),
                   reads=(b_PS[bank], b_X), writes=(b_X,))
    f1.close()


def emit_attn_b(k, nc, mk, xs, s, X, b_X, x1stk, xtile, LNT, b_LNT, lnt_mv,
                layernorm, PS, b_PS, ident, b_ident, ones, b_ones,
                smalls, b_smalls, b1h_d, b2r_d, sm0_d, dbt_d,
                wq_b, wk_b, wv_b, wo_b, w1, w2, bor_d,
                g1r_d, b1r_d, g2r_d, b2lr_d,
                ptil, b_ptil, recbc, b_recbc, sqs, b_sqs,
                nrm, b_nrm, xspill, rank1, knorm_pass, bound_bcast,
                pv_norm_evac, ld):
    xsp_out = bass.AP(xspill, 0, [[D, 128], [128 * D, NT], [1, D]])
    k.emit("sd", lambda e: e.dma_start(out=xsp_out, in_=X[:, :]),
           reads=(b_X,), kind="d")
    layernorm(xtile, b_X, xtile, b_X, lnt=LNT, b_lnt=b_LNT)
    TOK2, b_TOK2 = X, b_X

    def t2tile(tt, TOK2=TOK2):
        return TOK2[:, tt * D:(tt + 1) * D]

    k.barrier()

    b1 = contextlib.ExitStack()
    sm0, b_sm0 = mk(b1, "sm0", [128, 512], side="right")
    dbt, b_dbt = mk(b1, "dbt", [128, 28], side="right")
    ld(dbt[:, :], dbt_d, b_dbt)
    dtmp, b_dtmp = zip(*[mk(b1, f"dtmp{i}", [128, 512], side="right") for i in range(2)])
    kp, b_kp = mk(b1, "kpb", [128, T], side="right")
    vp, b_vp = mk(b1, "vpb", [128, NT * 130], side="right")
    qch, b_qch = zip(*[mk(b1, f"qchb{i}", [128, 512], side="right") for i in range(2)])
    atn, b_atn = zip(*[mk(b1, f"atnb{i}", [64, 512], side="right") for i in range(2)])
    wqs, b_wqs = mk(b1, "wqs", [128, D], side="right")
    wks, b_wks = mk(b1, "wks", [128, D], side="right")
    wvs, b_wvs = mk(b1, "wvs", [128, D], side="right")
    woa, b_woa = mk(b1, "woab", [64, D], side="right")
    wob, b_wob = mk(b1, "wobb", [64, D], side="right")
    crow, b_crow = mk(b1, "crow", [1, D], side="right")
    qraw, b_qraw = zip(*[mk(b1, f"qrawb{i}", [128, 512], side="right") for i in range(2)])

    ld(sm0[:, :], sm0_d, b_sm0)
    k.emit("sd", lambda e: e.dma_start(out=crow[0:1, :], in_=bor_d),
           reads=(b_crow,), writes=(b_crow,), kind="d")
    for tt in range(NT):
        k.emit("gp", lambda e, c=tt: e.memset(vp[:, c * 130 + 64:c * 130 + 65], 1.0),
               writes=(b_vp,))
        k.emit("gp", lambda e, c=tt: e.memset(vp[:, c * 130 + 129:c * 130 + 130], 1.0),
               writes=(b_vp,))

    def load_qkv(p):
        for wsb, b_wsb, wd in ((wqs, b_wqs, wq_b), (wks, b_wks, wk_b),
                               (wvs, b_wvs, wv_b)):
            k.emit("sd", lambda e, wsb=wsb, wd=wd, p=p: e.dma_start(
                out=wsb[:, :],
                in_=bass.AP(wd.tensor, p * 128, [[D, 128], [128 * D, 8], [1, 128]])),
                   reads=(b_wsb,), writes=(b_wsb,), kind="d", dstream=p % 2)

    def load_wo(p):
        k.emit("sd", lambda e, p=p: e.dma_start(
            out=woa[:, :], in_=wo_b[p * 128:p * 128 + 64, :]),
               reads=(b_woa,), writes=(b_woa,), kind="d", dstream=p % 2)
        k.emit("sd", lambda e, p=p: e.dma_start(
            out=wob[:, :], in_=wo_b[p * 128 + 64:p * 128 + 128, :]),
               reads=(b_wob,), writes=(b_wob,), kind="d", dstream=p % 2)

    load_qkv(0)
    load_wo(0)
    for pr in range(NPAIR):
        for side in range(2):
            h = pr * 2 + side
            wo_sb = woa if side == 0 else wob
            b_wo = b_woa if side == 0 else b_wob
            for oc in range(2):
                k.emit("pe", lambda e, h=h, oc=oc, wo_sb=wo_sb: e.matmul(
                    PS[0][0:1, 0:512], smalls[0:64, 16 + h:17 + h],
                    wo_sb[:, oc * 512:(oc + 1) * 512], start=True, stop=True),
                       reads=(b_smalls, b_wo), writes=(b_PS[0],))
                k.emit("dve", lambda e, oc=oc: e.tensor_add(
                    crow[0:1, oc * 512:(oc + 1) * 512],
                    crow[0:1, oc * 512:(oc + 1) * 512], PS[0][0:1, 0:512]),
                       reads=(b_PS[0],), writes=(b_crow,))

        for c in range(4):
            bank = c % 2
            for f in range(NF):
                k.emit("pe", lambda e, f=f, c=c, bk=bank: e.matmul(
                    PS[bk][:, 0:512], wks[:, f * 128:(f + 1) * 128],
                    lnt_mv(f, c * 512, 512), start=(f == 0), stop=(f == NF - 1)),
                       reads=(b_wks, b_LNT), writes=(b_PS[bank],))
            k.emit("dve", lambda e, c=c, p=pr, bk=bank: e.tensor_scalar(
                kp[:, c * 512:(c + 1) * 512], PS[bk][:, 0:512],
                scalar1=smalls[:, 8 + p:9 + p], scalar2=None, op0=ALU.add),
                   reads=(b_PS[bank], b_smalls), writes=(b_kp,))
        for c in range(4):
            bank = c % 2
            qr, b_qr = qraw[c % 2], b_qraw[c % 2]
            for f in range(NF):
                k.emit("pe", lambda e, f=f, c=c, bk=bank: e.matmul(
                    PS[bk][:, 0:512], wvs[:, f * 128:(f + 1) * 128],
                    lnt_mv(f, c * 512, 512), start=(f == 0), stop=(f == NF - 1)),
                       reads=(b_wvs, b_LNT), writes=(b_PS[bank],))
            k.emit("dve", lambda e, bk=bank, qr=qr: e.tensor_copy(
                qr[:, :], PS[bk][:, 0:512]),
                   reads=(b_PS[bank],), writes=(b_qr,))
            vtb = 2 + (c % 4)
            for j in range(4):
                tt = c * 4 + j
                k.emit("pe", lambda e, j=j, vb=vtb, qr=qr: e.transpose(
                    PS[vb][:, 0:128], qr[:, j * 128:(j + 1) * 128], ident[:, :]),
                       reads=(b_qr, b_ident), writes=(b_PS[vtb],))
                k.emit("dve", lambda e, t=tt, vb=vtb: e.tensor_copy(
                    vp[:, t * 130:t * 130 + 64], PS[vb][:, 0:64]),
                       reads=(b_PS[vtb],), writes=(b_vp,))
                k.emit("dve", lambda e, t=tt, vb=vtb: e.tensor_copy(
                    vp[:, t * 130 + 65:t * 130 + 129], PS[vb][:, 64:128]),
                       reads=(b_PS[vtb],), writes=(b_vp,))

        for qc in range(4):
            q0 = qc * 512
            qb = qc % 2
            bank = qb
            for f in range(NF):
                k.emit("pe", lambda e, f=f, qc=qc, bk=bank: e.matmul(
                    PS[bk][:, 0:512], wqs[:, f * 128:(f + 1) * 128],
                    lnt_mv(f, qc * 512, 512), start=(f == 0), stop=(f == NF - 1)),
                       reads=(b_wqs, b_LNT), writes=(b_PS[bank],))
            k.emit("dve", lambda e, p=pr, bk=bank, qb=qb: e.tensor_scalar(
                qch[qb][:, :], PS[bk][:, 0:512], scalar1=smalls[:, 0 + p:1 + p],
                scalar2=None, op0=ALU.add),
                   reads=(b_PS[bank], b_smalls), writes=(b_qch[qb],))
            if qc == 3 and pr + 1 < NPAIR:
                load_qkv(pr + 1)
            for side in range(2):
                h = pr * 2 + side
                base = side * 64
                qh = qch[qb][base:base + 64, :]
                kh = kp[base:base + 64, :]
                pvb = 6 + side
                for kt in range(NT):
                    k0 = kt * 128
                    sb = 2 + (kt % 4)
                    k.emit("pe", lambda e, kt=kt, sb=sb, qh=qh, kh=kh: e.matmul(
                        PS[sb][:, 0:512], kh[:, kt * 128:(kt + 1) * 128],
                        qh[:, :], start=True, stop=True),
                           reads=(b_qch[qb], b_kp), writes=(b_PS[sb],))
                    dt, bdt = dtmp[kt % 2], b_dtmp[kt % 2]
                    dbi = (q0 - k0 + 1920) // 128
                    k.emit("act", lambda e, dt=dt, dbi=dbi: e.activation(
                        dt[:, :], sm0[:, :], AF.Abs, bias=dbt[:, dbi:dbi + 1]),
                           reads=(b_sm0, b_dbt), writes=(bdt,))
                    k.emit("dve", lambda e, sb=sb, dt=dt: e.scalar_tensor_tensor(
                        PS[sb][:, 0:512], dt[:, :], smalls[:, 32:33],
                        PS[sb][:, 0:512], op0=ALU.mult, op1=ALU.add),
                           reads=(bdt, b_smalls, b_PS[sb]), writes=(b_PS[sb],))
                    if kt == 0 and h > 0:
                        k.emit("dve", lambda e, sb=sb, h=h: e.tensor_scalar_add(
                            PS[sb][0:h, 0:512], PS[sb][0:h, 0:512], NEGM),
                               reads=(b_PS[sb],), writes=(b_PS[sb],))
                    pt, bpt = ptil[kt % 2], b_ptil[kt % 2]
                    k.emit("act", lambda e, sb=sb, pt=pt: e.activation(
                        pt[:, :], PS[sb][:, 0:512], AF.Exp, scale=0.125),
                           reads=(b_PS[sb],), writes=(bpt,))
                    voff = 0 if side == 0 else 65
                    k.emit("pe", lambda e, kt=kt, pt=pt, vo=voff, pvb=pvb: e.matmul(
                        PS[pvb][0:65, 0:512], vp[:, kt * 130 + vo:kt * 130 + vo + 65],
                        pt[:, :], start=(kt == 0), stop=(kt == NT - 1)),
                           reads=(b_vp, bpt), writes=(b_PS[pvb],))
                pv_norm_evac(atn[side][:, :], b_atn[side], pvb, 2 + (NT % 4))
                wo_sb = woa if side == 0 else wob
                b_wo = b_woa if side == 0 else b_wob
                last = (pr == NPAIR - 1 and side == 1)
                for j in range(4):
                    tt = qc * 4 + j
                    obk = j % 2
                    for oc in range(2):
                        k.emit("pe", lambda e, j=j, oc=oc, bk=obk, wo_sb=wo_sb,
                               side=side, lst=last: e.matmul(
                            PS[bk][:, 0:512], atn[side][:, j * 128:(j + 1) * 128],
                            wo_sb[:, oc * 512:(oc + 1) * 512],
                            start=True, stop=not lst),
                               reads=(b_atn[side], b_wo), writes=(b_PS[obk],))
                        if last:
                            rank1(obk, 0, oc * 512, 512, False, True,
                                  b_rowsrc=b_crow, rowsrc=crow)
                        k.emit("dve", lambda e, tt=tt, oc=oc, bk=obk: e.tensor_add(
                            TOK2[:, tt * D + oc * 512:tt * D + (oc + 1) * 512],
                            TOK2[:, tt * D + oc * 512:tt * D + (oc + 1) * 512],
                            PS[bk][:, 0:512]),
                               reads=(b_PS[obk], b_TOK2), writes=(b_TOK2,))
        if pr + 1 < NPAIR:
            load_wo(pr + 1)
    b1.close()
    k.barrier()

    ga2 = contextlib.ExitStack()
    g2r, b_g2r = mk(ga2, "g2r", [128, D], side="right")
    b2lr, b_b2lr = mk(ga2, "b2lr", [128, D], side="right")
    b1h, b_b1h = mk(ga2, "b1h", [128, 32], side="right")
    ga = contextlib.ExitStack()
    g1r, b_g1r = mk(ga, "g1r", [128, D], side="right")
    b1r, b_b1r = mk(ga, "b1r", [128, D], side="right")
    for t, d, b in ((g1r, g1r_d, b_g1r), (b1r, b1r_d, b_b1r),
                    (g2r, g2r_d, b_g2r), (b2lr, b2lr_d, b_b2lr),
                    (b1h, b1h_d, b_b1h)):
        k.emit("sd", lambda e, t=t, d=d: e.dma_start(out=t[:, :], in_=d),
               reads=(b,), writes=(b,), kind="d")
    k.barrier()
    layernorm(t2tile, b_TOK2, t2tile, b_TOK2, lnt=LNT, b_lnt=b_LNT,
              gain=g1r, b_gain=b_g1r, bias=b1r, b_bias=b_b1r)
    ga.close()
    k.barrier()

    m1 = contextlib.ExitStack()
    w1c, b_w1c = [None, None], [None, None]
    w2c, b_w2c = [None, None], [None, None]
    for i in range(2):
        w1c[i], b_w1c[i] = mk(m1, f"w1c{i}", [128, NF * 256], side="right")
        w2c[i], b_w2c[i] = mk(m1, f"w2c{i}", [128, 2 * D], side="right")
    hT, b_hT = mk(m1, "hT", [128, 2 * T], side="right")
    rows, b_rows = mk(m1, "rowsm", [128, D], side="right")
    k.emit("sd", lambda e: e.dma_start(out=rows[32:33, :], in_=b2r_d),
           reads=(b_rows,), writes=(b_rows,), kind="d")
    NHC = DFF // 256
    for hc in range(NHC):
        sl = hc % 2
        k.emit("sd", lambda e, hc=hc, sl=sl: e.dma_start(
            out=w1c[sl][:, :],
            in_=bass.AP(w1.tensor, hc * 256, [[DFF, 128], [128 * DFF, NF], [1, 256]])),
               reads=(b_w1c[sl],), writes=(b_w1c[sl],), kind="d", dstream=sl)
        k.emit("sd", lambda e, hc=hc, sl=sl: e.dma_start(
            out=w2c[sl][:, :],
            in_=bass.AP(w2.tensor, hc * 256 * D, [[D, 128], [128 * D, 2], [1, D]])),
               reads=(b_w2c[sl],), writes=(b_w2c[sl],), kind="d", dstream=sl)
        for ht in range(2):
            for c in range(4):
                bank = c % 2
                for f in range(NF):
                    k.emit("pe", lambda e, f=f, c=c, ht=ht, sl=sl, bk=bank: e.matmul(
                        PS[bk][:, 0:512],
                        w1c[sl][:, f * 256 + ht * 128:f * 256 + (ht + 1) * 128],
                        lnt_mv(f, c * 512, 512), start=(f == 0), stop=(f == NF - 1)),
                           reads=(b_w1c[sl], b_LNT), writes=(b_PS[bank],))
                hcol = hc * 2 + ht
                k.emit("act", lambda e, c=c, ht=ht, hcol=hcol, bk=bank: e.activation(
                    hT[:, ht * T + c * 512:ht * T + (c + 1) * 512], PS[bk][:, 0:512],
                    AF.Relu, bias=b1h[:, hcol:hcol + 1]),
                       reads=(b_PS[bank], b_b1h), writes=(b_hT,))
        for tt in range(NT):
            bank = 2 + (tt % 4)
            for oc in range(2):
                for ht in range(2):
                    lastmm = (ht == 1)
                    k.emit("pe", lambda e, tt=tt, oc=oc, ht=ht, sl=sl, bk=bank,
                           lm=lastmm, lh=(hc == NHC - 1): e.matmul(
                        PS[bk][:, 0:512],
                        hT[:, ht * T + tt * 128:ht * T + (tt + 1) * 128],
                        w2c[sl][:, ht * D + oc * 512:ht * D + (oc + 1) * 512],
                        start=(ht == 0), stop=(lm and not lh)),
                           reads=(b_hT, b_w2c[sl]), writes=(b_PS[bank],))
                if hc == NHC - 1:
                    rank1(bank, 32, oc * 512, 512, False, True, b_rows, rows)
                k.emit("dve", lambda e, tt=tt, oc=oc, bk=bank: e.tensor_add(
                    TOK2[:, tt * D + oc * 512:tt * D + (oc + 1) * 512],
                    TOK2[:, tt * D + oc * 512:tt * D + (oc + 1) * 512],
                    PS[bk][:, 0:512]),
                       reads=(b_PS[bank], b_TOK2), writes=(b_TOK2,))
    m1.close()
    k.barrier()

    layernorm(t2tile, b_TOK2, t2tile, b_TOK2,
              gain=g2r, b_gain=b_g2r, bias=b2lr, b_bias=b_b2lr)
    ga2.close()
    k.barrier()

    k.emit("gp", lambda e: e.dma_start(out=xsp_out, in_=TOK2[:, :],
                                       accum_op=ALU.add),
           reads=(b_TOK2,), kind="d")
    k.barrier()
    x1stk.close()
    X2, b_X2 = mk(xs, f"X2_{s}", [128, NT * D])
    xsp_in = bass.AP(xspill, 0, [[D, 128], [128 * D, NT], [1, D]])
    k.emit("sd", lambda e: e.dma_start(out=X2[:, :], in_=xsp_in),
           writes=(b_X2,), kind="d")
    k.barrier()
    return X2, b_X2




def _host_constants():
    c = {}
    c["identc"] = np.eye(128, dtype=np.float32)
    perm64 = np.array([2 * j for j in range(32)] + [2 * j + 1 for j in range(32)])
    perm2 = np.concatenate([perm64, perm64 + 64])
    p2 = np.zeros((128, 128), np.float32)
    p2[perm2, np.arange(128)] = 1.0
    c["p2c"] = p2
    c["onesc"] = np.ones((128, 128), np.float32)
    u = np.arange(1152)[None, :]
    p = np.arange(128)[:, None]
    c["cmc"] = np.where(u < p + 512, np.float32(NEGM), np.float32(0.0)).astype(np.float32)
    j = np.arange(512)[None, :]
    c["sm0c"] = (j - p).astype(np.float32)
    i = np.arange(32, dtype=np.float64)
    inv = 10000.0 ** (2.0 * i / 64.0)
    ang = np.arange(T, dtype=np.float64)[None, :] / inv[:, None]
    cos = np.cos(ang).astype(np.float32)
    sin = np.sin(ang).astype(np.float32)
    c["c64c"] = np.concatenate([cos, cos, cos, cos], axis=0)
    c["s64c"] = np.concatenate([-sin, sin, -sin, sin], axis=0)
    return c


def kernel(L0_attn_a_x, L0_attn_a_W_qkv, L0_attn_a_W_o,
           L0_attn_b_Wq, L0_attn_b_bq, L0_attn_b_Wk, L0_attn_b_bk,
           L0_attn_b_Wv, L0_attn_b_bv, L0_attn_b_Wo, L0_attn_b_bo,
           L0_attn_b_dist_scale, L0_attn_b_W1, L0_attn_b_b1,
           L0_attn_b_W2, L0_attn_b_b2, L0_attn_b_ln1_g, L0_attn_b_ln1_b,
           L0_attn_b_ln2_g, L0_attn_b_ln2_b, L0_ffn_CB, L0_ffn_SCB, L0_ffn_bias):
    f32 = np.float32
    asn = lambda a: np.ascontiguousarray(np.asarray(a, dtype=f32))
    x = asn(L0_attn_a_x)
    rep = lambda v: np.ascontiguousarray(
        np.broadcast_to(asn(v).reshape(1, D), (128, D)))
    shared = {
        "wqkv": asn(L0_attn_a_W_qkv), "wo_a": asn(L0_attn_a_W_o),
        "wq_b": asn(L0_attn_b_Wq), "wk_b": asn(L0_attn_b_Wk),
        "wv_b": asn(L0_attn_b_Wv), "wo_b": asn(L0_attn_b_Wo),
        "w1": asn(L0_attn_b_W1), "w2": asn(L0_attn_b_W2), "cb": asn(L0_ffn_CB),
        "bqhc": np.ascontiguousarray(asn(L0_attn_b_bq).reshape(8, 128).T),
        "bkhc": np.ascontiguousarray(asn(L0_attn_b_bk).reshape(8, 128).T),
        "bvfc": np.ascontiguousarray(asn(L0_attn_b_bv).reshape(16, 64).T),
        "b1hc": np.ascontiguousarray(asn(L0_attn_b_b1).reshape(32, 128).T),
        "b2rc": asn(L0_attn_b_b2).reshape(1, D),
        "borc": asn(L0_attn_b_bo).reshape(1, D),
        "fbrc": asn(L0_ffn_bias).reshape(1, D),
        "scbc": rep(L0_ffn_SCB),
        "g1rc": rep(L0_attn_b_ln1_g), "b1rc": rep(L0_attn_b_ln1_b),
        "g2rc": rep(L0_attn_b_ln2_g), "b2lrc": rep(L0_attn_b_ln2_b),
        "dscc": asn(L0_attn_b_dist_scale).reshape(1, 1),
        "epsc": np.full((128, 1), EPS, dtype=f32),
        "dbtc": np.ascontiguousarray(np.broadcast_to(
            (np.arange(28, dtype=f32) * 128 - 1920)[None, :], (128, 28))),
    }
    shared.update(_host_constants())
    n = 8
    in_maps = []
    for c in range(n):
        im = dict(shared)
        im["x"] = np.ascontiguousarray(x[c * B_LOC:(c + 1) * B_LOC])
        in_maps.append(im)
    nc, _ = build_program(STAGE)
    res = run_bass_kernel_spmd(nc, in_maps, core_ids=list(range(n)))
    return np.concatenate([r["y"] for r in res.results], axis=0)


# version 07
# speedup vs baseline: 1.0088x; 1.0088x over previous
import contextlib
import numpy as np
import concourse.bass as bass
import concourse.mybir as mybir
from concourse.bass_utils import run_bass_kernel_spmd

F32 = mybir.dt.float32
AF = mybir.ActivationFunctionType
ALU = mybir.AluOpType
AX = mybir.AxisListType

B_LOC, T, D, H, DH, DFF = 2, 2048, 1024, 16, 64, 4096
NT, NF = T // 128, D // 128
NPAIR = H // 2
EPS = 1e-5
NEGM = -1.0e9
STAGE = 4
PAIR_LIMIT = None
DBG = False



class Buf:
    __slots__ = ("ws", "rs", "name")

    def __init__(self, name=""):
        self.ws = []
        self.rs = []
        self.name = name


class Op:
    __slots__ = ("chan", "fn", "waits", "kind", "used", "val", "fwaits", "dstream")

    def __init__(self, chan, fn, waits, kind, dstream=0):
        self.chan = chan
        self.fn = fn
        self.waits = waits
        self.kind = kind
        self.used = False
        self.val = 0
        self.fwaits = ()
        self.dstream = dstream


NDSTREAM = 5


class Chan:
    def __init__(self, name):
        self.name = name
        self.last = None
        self.sem_c = None
        self.sem_d = [None] * NDSTREAM
        self.pending = []


class K:
    def __init__(self, nc):
        self.nc = nc
        self.ops = []
        self.chans = {n: Chan(n) for n in ("pe", "dve", "act", "gp", "sd")}

    def emit(self, chan, fn, reads=(), writes=(), kind="c", dstream=0):
        ch = self.chans[chan]
        waits, seen = [], set()

        def add(op):
            if op is not None and id(op) not in seen:
                seen.add(id(op))
                waits.append(op)

        for o in ch.pending:
            add(o)
        ch.pending = []
        for b in reads:
            for w in b.ws:
                add(w)
        for b in writes:
            for r in b.rs:
                add(r)
            for w in b.ws:
                add(w)
        op = Op(ch, fn, waits, kind, dstream)
        self.ops.append(op)
        ch.last = op
        for b in reads:
            b.rs.append(op)
        for b in writes:
            b.ws = [op]
            b.rs = []
        return op

    def barrier(self):
        lasts = [c.last for c in self.chans.values() if c.last is not None]
        for ch in self.chans.values():
            for o in lasts:
                if o.chan is not ch and o not in ch.pending:
                    ch.pending.append(o)

    def finish(self):
        lasts = [c.last for c in self.chans.values() if c.last is not None]
        ldma = {}
        for op in self.ops:
            if op.kind == "d":
                ldma[(op.chan.name, op.dstream)] = op
        ch = self.chans["sd"]
        waits = [o for o in lasts if o.chan is not ch] + list(ldma.values())
        op = Op(ch, None, waits, "c")
        self.ops.append(op)

    def finalize(self):
        for op in self.ops:
            for w in op.waits:
                w.used = True
        for ch in self.chans.values():
            ch._cc = 0
            ch._cd = [0] * NDSTREAM
            ch._wmax = {}
        cd_snap = []
        run = {(n, s): 0 for n in self.chans for s in range(NDSTREAM)}
        for op in self.ops:
            cd_snap.append(dict(run))
            if op.kind == "d":
                run[(op.chan.name, op.dstream)] += 16
                op.chan._cd[op.dstream] += 16
                op.val = op.chan._cd[op.dstream]
            elif op.used:
                op.chan._cc += 1
                op.val = op.chan._cc
        for i, op in enumerate(self.ops):
            fw = []
            for w in op.waits:
                if w.chan is op.chan and w.kind == "c":
                    continue
                if w.kind == "d":
                    sem = w.chan.sem_d[w.dstream]
                    val = cd_snap[i][(w.chan.name, w.dstream)]
                    if val < w.val:
                        val = w.val
                else:
                    sem = w.chan.sem_c
                    val = w.val
                key = id(sem)
                if op.chan._wmax.get(key, -1) >= val:
                    continue
                op.chan._wmax[key] = val
                fw.append((sem, val))
            op.fwaits = fw

    def run(self, eng, name):
        ch = self.chans[name]
        for op in self.ops:
            if op.chan is not ch:
                continue
            for sem, val in op.fwaits:
                eng.wait_ge(sem, val)
            if op.fn is not None:
                ins = op.fn(eng)
                if ins is not None:
                    if op.kind == "d":
                        ins.then_inc(ch.sem_d[op.dstream], 16)
                    elif op.used:
                        ins.then_inc(ch.sem_c, 1)





def build_program(stage=STAGE):
    nc = bass.Bass(trn_type="TRN2", target_bir_lowering=False,
                   detect_race_conditions=False, dynamic_dma_scratch_size=8192)
    k = K(nc)

    def din(name, shape):
        return nc.dram_tensor(name, shape, F32, kind="ExternalInput").ap()

    x_in = din("x", [B_LOC, T, D])
    wqkv = din("wqkv", [3 * D, D])
    wo_a = din("wo_a", [D, D])
    wq_b, wk_b, wv_b, wo_b = (din(n, [D, D]) for n in ("wq_b", "wk_b", "wv_b", "wo_b"))
    w1 = din("w1", [D, DFF])
    w2 = din("w2", [DFF, D])
    cb = din("cb", [D, D])
    ident_d = din("identc", [128, 128])
    p2_d = din("p2c", [128, 128])
    ones_d = din("onesc", [128, 128])
    cm_d = din("cmc", [128, 1152])
    sm0_d = din("sm0c", [128, 512])
    c64_d = din("c64c", [128, T])
    s64_d = din("s64c", [128, T])
    bqh_d = din("bqhc", [128, 8])
    bkh_d = din("bkhc", [128, 8])
    bvf_d = din("bvfc", [64, 16])
    b1h_d = din("b1hc", [128, 32])
    b2r_d = din("b2rc", [1, D])
    bor_d = din("borc", [1, D])
    fbr_d = din("fbrc", [1, D])
    scb_d = din("scbc", [128, D])
    g1r_d = din("g1rc", [128, D])
    b1r_d = din("b1rc", [128, D])
    g2r_d = din("g2rc", [128, D])
    b2lr_d = din("b2lrc", [128, D])
    dsc_d = din("dscc", [1, 1])
    eps_d = din("epsc", [128, 1])
    dbt_d = din("dbtc", [128, 28])

    y_out = nc.dram_tensor("y", [B_LOC, T, D], F32, kind="ExternalOutput").ap()
    dbg_out = nc.dram_tensor("dbg", [128, 4 * 512], F32, kind="ExternalOutput").ap() if DBG \
        else None
    dbg2_out = nc.dram_tensor("dbg2", [64, 2 * 32 * 512], F32,
                              kind="ExternalOutput").ap() if DBG else None
    dbg3_out = nc.dram_tensor("dbg3", [128, 8192], F32,
                              kind="ExternalOutput").ap() if DBG else None
    k.dbg3 = dbg3_out
    xspill = nc.dram_tensor("xspill", [T, D], F32, kind="Internal")

    gstack = contextlib.ExitStack()
    _uid = [0]

    def mk(stk, name, shape, dt=F32, side="left"):
        _uid[0] += 1
        h = stk.enter_context(nc.sbuf_tensor(f"{name}_u{_uid[0]}", shape, dt,
                                             side=side))
        return h, Buf(name)

    ident, b_ident = mk(gstack, "identc_s", [128, 128])
    ones, b_ones = mk(gstack, "onesc_s", [128, 128])
    smalls, b_smalls = mk(gstack, "smalls", [128, 64])
    stats, b_stats = mk(gstack, "stats", [128, 4 * NT])
    nrm, b_nrm = mk(gstack, "nrm", [1, 16])
    sqs, b_sqs = mk(gstack, "sqs", [128, 512])
    ptil, b_ptil = [None, None], [None, None]
    for i in range(2):
        ptil[i], b_ptil[i] = mk(gstack, f"ptil{i}", [128, 512])
    recbc, b_recbc = mk(gstack, "recbc", [128, 512])
    rcw, b_rcw = mk(gstack, "rcw", [128, 256])

    PS, b_PS = [], []
    for i in range(8):
        h = gstack.enter_context(nc.psum_tensor(f"ps{i}", [128, 512], F32))
        PS.append(h)
        b_PS.append(Buf(f"ps{i}"))

    def ld(dst, src, b):
        k.emit("sd", lambda e, d=dst, s=src: e.dma_start(out=d, in_=s),
               reads=(b,), writes=(b,), kind="d")

    ld(ident[:, :], ident_d, b_ident)
    ld(ones[:, :], ones_d, b_ones)
    ld(smalls[:, 0:8], bqh_d, b_smalls)
    ld(smalls[:, 8:16], bkh_d, b_smalls)
    ld(smalls[0:64, 16:32], bvf_d, b_smalls)
    ld(smalls[0:1, 33:34], dsc_d, b_smalls)
    ld(smalls[:, 35:36], eps_d, b_smalls)
    k.emit("dve", lambda e: e.memset(rcw[:, 0:128], 1.0), writes=(b_rcw,))
    k.emit("pe", lambda e: e.matmul(PS[7][:, 0:1], ones[0:1, 0:128],
                                    smalls[0:1, 33:34], start=True, stop=True),
           reads=(b_ones, b_smalls), writes=(b_PS[7],))
    k.emit("dve", lambda e: e.tensor_scalar_mul(smalls[:, 32:33], PS[7][:, 0:1], -1.0),
           reads=(b_PS[7],), writes=(b_smalls,))
    k.barrier()

    def layernorm(xt, b_xt, out_tiles, b_out, lnt=None, b_lnt=None,
                  gain=None, b_gain=None, bias=None, b_bias=None):
        for tt in range(NT):
            src = xt(tt)
            k.emit("dve", lambda e, s=src, c=tt: e.reduce_sum(
                stats[:, c:c + 1], s, axis=AX.X), reads=(b_xt,), writes=(b_stats,))
            k.emit("act", lambda e, s=src, c=tt: e.activation(
                PS[7][:, 0:512], s[:, 0:512], AF.Square,
                accum_out=stats[:, NT + c:NT + c + 1]),
                   reads=(b_xt,), writes=(b_PS[7], b_stats))
            k.emit("act", lambda e, s=src, c=tt: e.activation(
                PS[7][:, 0:512], s[:, 512:1024], AF.Square,
                accum_out=stats[:, 2 * NT + c:2 * NT + c + 1]),
                   reads=(b_xt,), writes=(b_PS[7], b_stats))
        k.emit("dve", lambda e: e.tensor_add(stats[:, NT:2 * NT], stats[:, NT:2 * NT],
                                             stats[:, 2 * NT:3 * NT]),
               reads=(b_stats,), writes=(b_stats,))
        k.emit("gp", lambda e: e.tensor_scalar_mul(stats[:, 0:NT], stats[:, 0:NT],
                                                   1.0 / D),
               reads=(b_stats,), writes=(b_stats,))
        k.emit("dve", lambda e: e.tensor_scalar_mul(stats[:, NT:2 * NT],
                                                    stats[:, NT:2 * NT], 1.0 / D),
               reads=(b_stats,), writes=(b_stats,))
        k.emit("gp", lambda e: e.tensor_mul(stats[:, 3 * NT:4 * NT],
                                            stats[:, 0:NT], stats[:, 0:NT]),
               reads=(b_stats,), writes=(b_stats,))
        k.emit("dve", lambda e: e.tensor_sub(stats[:, NT:2 * NT],
                                             stats[:, NT:2 * NT],
                                             stats[:, 3 * NT:4 * NT]),
               reads=(b_stats,), writes=(b_stats,))
        k.emit("gp", lambda e: e.tensor_scalar_add(rcw[:, 0:NT],
                                                    stats[:, NT:2 * NT], EPS),
               reads=(b_stats, b_rcw), writes=(b_rcw,))
        k.emit("act", lambda e: e.activation(rcw[:, 128:256], rcw[:, 0:128], AF.Ln),
               reads=(b_rcw,), writes=(b_rcw,))
        k.emit("act", lambda e: e.activation(rcw[:, 128:256], rcw[:, 128:256],
                                             AF.Exp, scale=-0.5),
               reads=(b_rcw,), writes=(b_rcw,))
        for _nr in range(2):
            k.emit("dve", lambda e: e.tensor_mul(sqs[:, 0:128], rcw[:, 128:256],
                                                 rcw[:, 128:256]),
                   reads=(b_rcw,), writes=(b_sqs,))
            k.emit("dve", lambda e: e.tensor_mul(sqs[:, 0:128], sqs[:, 0:128],
                                                 rcw[:, 0:128]),
                   reads=(b_rcw, b_sqs), writes=(b_sqs,))
            k.emit("dve", lambda e: e.tensor_scalar(sqs[:, 0:128], sqs[:, 0:128],
                                                    scalar1=-0.5, scalar2=1.5,
                                                    op0=ALU.mult, op1=ALU.add),
                   reads=(b_sqs,), writes=(b_sqs,))
            k.emit("dve", lambda e: e.tensor_mul(rcw[:, 128:256], rcw[:, 128:256],
                                                 sqs[:, 0:128]),
                   reads=(b_sqs, b_rcw), writes=(b_rcw,))
        for tt in range(NT):
            src = xt(tt)
            for half in range(2):
                if out_tiles is not None:
                    dsub = out_tiles(tt)[:, half * 512:(half + 1) * 512]
                    b_d = b_out
                else:
                    dsub = ptil[half][:, :]
                    b_d = b_ptil[half]
                k.emit("dve", lambda e, s=src, d=dsub, c=tt, hh=half: e.tensor_scalar(
                    d, s[:, hh * 512:(hh + 1) * 512], scalar1=stats[:, c:c + 1],
                    scalar2=rcw[:, 128 + c:129 + c],
                    op0=ALU.subtract, op1=ALU.mult),
                       reads=(b_xt, b_stats, b_rcw), writes=(b_d,))
                if gain is not None:
                    k.emit("dve", lambda e, d=dsub, hh=half: e.tensor_mul(
                        d, d, gain[:, hh * 512:(hh + 1) * 512]),
                           reads=(b_d, b_gain), writes=(b_d,))
                    k.emit("dve", lambda e, d=dsub, hh=half: e.tensor_add(
                        d, d, bias[:, hh * 512:(hh + 1) * 512]),
                           reads=(b_d, b_bias), writes=(b_d,))
            if lnt is None:
                continue
            for half in range(2):
                bank = half
                if out_tiles is not None:
                    srcs = [out_tiles(tt)[:, (half * 4 + j) * 128:
                                          (half * 4 + j + 1) * 128] for j in range(4)]
                    b_s = b_out
                else:
                    srcs = [ptil[half][:, j * 128:(j + 1) * 128] for j in range(4)]
                    b_s = b_ptil[half]
                for j in range(4):
                    k.emit("pe", lambda e, s=srcs[j], bk=bank, j=j: e.transpose(
                        PS[bk][:, j * 128:(j + 1) * 128], s, ident[:, :]),
                           reads=(b_s, b_ident), writes=(b_PS[bank],))
                out_ap = bass.AP(lnt, (half * 4) * T + tt * 128,
                                 [[NF * T, 128], [T, 4], [1, 128]])
                k.emit("dve", lambda e, o=out_ap, bk=bank: e.tensor_copy(
                    o, PS[bk][:, 0:512]),
                       reads=(b_PS[bank],), writes=(b_lnt,))

    def rank1(bank, row_base, col0, w, start, stop, b_rowsrc, rowsrc):
        src = rowsrc
        b_src = b_rowsrc
        k.emit("pe", lambda e: e.matmul(
            PS[bank][:, 0:w], ones[row_base:row_base + 1, 0:128],
            src[row_base:row_base + 1, col0:col0 + w], start=start, stop=stop),
               reads=(b_ones, b_src), writes=(b_PS[bank],))

    def bound_bcast(qsb, b_qsb, qbase, ksb, b_ksb, kcol):
        k.emit("act", lambda e: e.activation(sqs[qbase:qbase + 64, :],
                                             qsb[qbase:qbase + 64, 0:512], AF.Square),
               reads=(b_qsb,), writes=(b_sqs,))
        k.emit("pe", lambda e: e.matmul(PS[5][0:1, 0:512],
                                        ones[qbase:qbase + 64, 0:1],
                                        sqs[qbase:qbase + 64, :],
                                        start=True, stop=True),
               reads=(b_sqs, b_ones), writes=(b_PS[5],))
        k.emit("dve", lambda e: e.reduce_max(nrm[0:1, 10:11], PS[5][0:1, 0:512],
                                             axis=AX.X),
               reads=(b_PS[5],), writes=(b_nrm,))
        k.emit("dve", lambda e, kc=kcol: e.tensor_add(
            nrm[0:1, 11:12], nrm[0:1, 10:11], nrm[0:1, kc:kc + 1]),
               reads=(b_nrm,), writes=(b_nrm,))
        k.emit("pe", lambda e: e.matmul(PS[5][:, 0:1], ones[0:1, 0:128],
                                        nrm[0:1, 11:12], start=True, stop=True),
               reads=(b_ones, b_nrm), writes=(b_PS[5],))
        k.emit("dve", lambda e: e.tensor_scalar_mul(smalls[:, 34:35],
                                                    PS[5][:, 0:1], -1.0 / 16.0),
               reads=(b_PS[5],), writes=(b_smalls,))

    def knorm_pass(ksb, b_ksb, cols_a, cols_b):
        for c in range(4):
            k.emit("act", lambda e, c=c: e.activation(
                sqs[:, :], ksb[:, c * 512:(c + 1) * 512], AF.Square),
                   reads=(b_ksb,), writes=(b_sqs,))
            for base, col in ((0, c), (64, 4 + c)):
                k.emit("pe", lambda e, b=base: e.matmul(
                    PS[5][0:1, 0:512], ones[b:b + 64, 0:1],
                    sqs[b:b + 64, :], start=True, stop=True),
                       reads=(b_sqs, b_ones), writes=(b_PS[5],))
                k.emit("dve", lambda e, col=col: e.reduce_max(
                    nrm[0:1, col:col + 1], PS[5][0:1, 0:512], axis=AX.X),
                       reads=(b_PS[5],), writes=(b_nrm,))
        k.emit("dve", lambda e, ca=cols_a: e.reduce_max(
            nrm[0:1, ca:ca + 1], nrm[0:1, 0:4], axis=AX.X),
               reads=(b_nrm,), writes=(b_nrm,))
        k.emit("dve", lambda e, cb=cols_b: e.reduce_max(
            nrm[0:1, cb:cb + 1], nrm[0:1, 4:8], axis=AX.X),
               reads=(b_nrm,), writes=(b_nrm,))

    def pv_norm_evac(atn_ap, b_atn, pvb, bcb):
        k.emit("act", lambda e: e.activation(recbc[64:65, 0:512],
                                             PS[pvb][64:65, 0:512], AF.Ln),
               reads=(b_PS[pvb],), writes=(b_recbc,))
        k.emit("act", lambda e: e.activation(recbc[64:65, 0:512],
                                             recbc[64:65, 0:512],
                                             AF.Exp, scale=-1.0),
               reads=(b_recbc,), writes=(b_recbc,))
        k.emit("dve", lambda e: e.tensor_mul(sqs[64:65, 0:512],
                                             recbc[64:65, 0:512],
                                             PS[pvb][64:65, 0:512]),
               reads=(b_recbc, b_PS[pvb]), writes=(b_sqs,))
        k.emit("gp", lambda e: e.tensor_scalar(sqs[64:65, 0:512],
                                               sqs[64:65, 0:512],
                                               scalar1=-1.0, scalar2=2.0,
                                               op0=ALU.mult, op1=ALU.add),
               reads=(b_sqs,), writes=(b_sqs,))
        k.emit("dve", lambda e: e.tensor_mul(recbc[64:65, 0:512],
                                             recbc[64:65, 0:512],
                                             sqs[64:65, 0:512]),
               reads=(b_sqs, b_recbc), writes=(b_recbc,))
        k.emit("pe", lambda e: e.matmul(PS[bcb][0:64, 0:512], ones[64:65, 0:64],
                                        recbc[64:65, 0:512], start=True, stop=True),
               reads=(b_ones, b_recbc), writes=(b_PS[bcb],))
        k.emit("dve", lambda e: e.tensor_copy(recbc[0:64, 0:512], PS[bcb][0:64, 0:512]),
               reads=(b_PS[bcb],), writes=(b_recbc,))
        k.emit("dve", lambda e: e.tensor_mul(atn_ap, PS[pvb][0:64, 0:512],
                                             recbc[0:64, 0:512]),
               reads=(b_PS[pvb], b_recbc), writes=(b_atn,))

    def do_seq(s):
        xs = contextlib.ExitStack()
        x1stk = contextlib.ExitStack()
        X, b_X = mk(x1stk, f"X{s}", [128, NT * D], side="right")

        def xtile(tt, X=X):
            return X[:, tt * D:(tt + 1) * D]

        xv = bass.AP(x_in.tensor, s * T * D, [[D, 128], [128 * D, NT], [1, D]])
        k.emit("sd", lambda e, xv=xv, X=X: e.dma_start(out=X[:, :], in_=xv),
               writes=(b_X,), kind="d", dstream=2)
        k.barrier()

        LNT, b_LNT = mk(xs, f"LNT{s}", [128, NF * T])

        def lnt_mv(f, c0, w, LNT=LNT):
            return LNT[:, f * T + c0:f * T + c0 + w]

        layernorm(xtile, b_X, None, None, lnt=LNT, b_lnt=b_LNT)
        k.barrier()
        a1 = contextlib.ExitStack()
        p2m, b_p2m = mk(a1, "p2m", [128, 128], side="right")
        cm, b_cm = mk(a1, "cm", [128, 896], side="right")
        rtc, b_rtc = zip(*[mk(a1, f"rtc{i}", [128, 512], side="right") for i in range(2)])
        rts, b_rts = zip(*[mk(a1, f"rts{i}", [128, 512], side="right") for i in range(2)])
        kp, b_kp = mk(a1, "kp", [128, T], side="right")
        vp, b_vp = mk(a1, "vp", [128, NT * 130], side="right")
        qch, b_qch = zip(*[mk(a1, f"qch{i}", [128, 512], side="right") for i in range(2)])
        qraw, b_qraw = zip(*[mk(a1, f"qraw{i}", [128, 512], side="right") for i in range(2)])
        qswp, b_qswp = zip(*[mk(a1, f"qswp{i}", [128, 512], side="right") for i in range(2)])
        atn, b_atn = zip(*[mk(a1, f"atn{i}", [64, 512], side="right") for i in range(2)])
        wst, b_wst = zip(*[mk(a1, f"wst{i}", [128, D], side="right") for i in range(2)])
        wqkT, b_wqkT = mk(a1, "wqkT", [128, NF * 256], side="right")
        wvT, b_wvT = mk(a1, "wvT", [128, NF * 128], side="right")
        woTa, b_woTa = mk(a1, "woTa", [64, D], side="right")
        woTb, b_woTb = mk(a1, "woTb", [64, D], side="right")

        ld(p2m[:, :], p2_d, b_p2m)
        ld(cm[:, :], cm_d[:, 128:1024], b_cm)
        for tt in range(NT):
            k.emit("gp", lambda e, c=tt: e.memset(
                vp[:, c * 130 + 64:c * 130 + 65], 1.0), writes=(b_vp,))
            k.emit("gp", lambda e, c=tt: e.memset(
                vp[:, c * 130 + 129:c * 130 + 130], 1.0), writes=(b_vp,))

        ruse = [0]

        def rope_chunk(dst_ap, b_dst, bank, c):
            ri = ruse[0] % 2
            ruse[0] += 1
            qr, b_qr = qraw[ri], b_qraw[ri]
            qs, b_qs = qswp[ri], b_qswp[ri]
            k.emit("dve", lambda e, bk=bank, qr=qr: e.tensor_copy(
                qr[:, :], PS[bk][:, 0:512]),
                   reads=(b_PS[bank],), writes=(b_qr,))
            for sl in range(4):
                srcl = [32, 0, 96, 64][sl]
                k.emit("sd", lambda e, sl=sl, srcl=srcl, qr=qr, qs=qs: e.dma_start(
                    out=qs[sl * 32:(sl + 1) * 32, :], in_=qr[srcl:srcl + 32, :]),
                       reads=(b_qr,), writes=(b_qs,), kind="d", dstream=4)
            k.emit("dve", lambda e, qr=qr, ri=ri: e.tensor_mul(
                qr[:, :], qr[:, :], rtc[ri][:, :]),
                   reads=(b_qr, b_rtc[ri]), writes=(b_qr,))
            k.emit("dve", lambda e, qs=qs, ri=ri: e.tensor_mul(
                qs[:, :], qs[:, :], rts[ri][:, :]),
                   reads=(b_qs, b_rts[ri]), writes=(b_qs,))
            k.emit("dve", lambda e, d=dst_ap, qr=qr, qs=qs: e.tensor_add(
                d, qr[:, :], qs[:, :]),
                   reads=(b_qr, b_qs), writes=(b_dst,))

        def rope_tables(c):
            ri = ruse[0] % 2
            k.emit("sd", lambda e, c=c, ri=ri: e.dma_start(
                out=rtc[ri][:, :], in_=c64_d[:, c * 512:(c + 1) * 512]),
                   reads=(b_rtc[ri],), writes=(b_rtc[ri],), kind="d", dstream=3)
            k.emit("sd", lambda e, c=c, ri=ri: e.dma_start(
                out=rts[ri][:, :], in_=s64_d[:, c * 512:(c + 1) * 512]),
                   reads=(b_rts[ri],), writes=(b_rts[ri],), kind="d", dstream=3)

        nplim = PAIR_LIMIT if PAIR_LIMIT is not None else NPAIR
        wring = [0]

        for pr in range(nplim):
            def wload(off, p=pr):
                w = wring[0] % 2
                wring[0] += 1
                k.emit("sd", lambda e, p=p, off=off, w=w: e.dma_start(
                    out=wst[w][:, :],
                    in_=(wqkv[off + p * 128:off + (p + 1) * 128, :] if off >= 0 else
                         bass.AP(wo_a.tensor, p * 128,
                                 [[D, 128], [128 * D, 8], [1, 128]]))),
                       reads=(b_wst[w],), writes=(b_wst[w],), kind="d", dstream=w)
                return w
            w = wload(0)
            for f in range(NF):
                k.emit("pe", lambda e, f=f, w=w: e.transpose(
                    PS[0][:, 0:128], wst[w][:, f * 128:(f + 1) * 128], p2m[:, :]),
                       reads=(b_wst[w], b_p2m), writes=(b_PS[0],))
                k.emit("dve", lambda e, f=f: e.tensor_copy(
                    wqkT[:, f * 256:f * 256 + 128], PS[0][:, 0:128]),
                       reads=(b_PS[0],), writes=(b_wqkT,))
            w = wload(D)
            for f in range(NF):
                k.emit("pe", lambda e, f=f, w=w: e.transpose(
                    PS[1][:, 0:128], wst[w][:, f * 128:(f + 1) * 128], p2m[:, :]),
                       reads=(b_wst[w], b_p2m), writes=(b_PS[1],))
                k.emit("dve", lambda e, f=f: e.tensor_copy(
                    wqkT[:, f * 256 + 128:f * 256 + 256], PS[1][:, 0:128]),
                       reads=(b_PS[1],), writes=(b_wqkT,))
            w = wload(2 * D)
            for f in range(NF):
                k.emit("pe", lambda e, f=f, w=w: e.transpose(
                    PS[0][:, 0:128], wst[w][:, f * 128:(f + 1) * 128], ident[:, :]),
                       reads=(b_wst[w], b_ident), writes=(b_PS[0],))
                k.emit("dve", lambda e, f=f: e.tensor_copy(
                    wvT[:, f * 128:(f + 1) * 128], PS[0][:, 0:128]),
                       reads=(b_PS[0],), writes=(b_wvT,))
            w = wload(-1)
            for c in range(NF):
                k.emit("pe", lambda e, c=c, w=w: e.transpose(
                    PS[1][0:64, 0:128], wst[w][:, c * 128:c * 128 + 64], ident[:, :]),
                       reads=(b_wst[w], b_ident), writes=(b_PS[1],))
                k.emit("dve", lambda e, c=c: e.tensor_copy(
                    woTa[:, c * 128:(c + 1) * 128], PS[1][0:64, 0:128]),
                       reads=(b_PS[1],), writes=(b_woTa,))
                k.emit("pe", lambda e, c=c, w=w: e.transpose(
                    PS[1][0:64, 0:128], wst[w][:, c * 128 + 64:(c + 1) * 128],
                    ident[:, :]),
                       reads=(b_wst[w], b_ident), writes=(b_PS[1],))
                k.emit("dve", lambda e, c=c: e.tensor_copy(
                    woTb[:, c * 128:(c + 1) * 128], PS[1][0:64, 0:128]),
                       reads=(b_PS[1],), writes=(b_woTb,))

            for c in range(4):
                bank = c % 2
                rope_tables(c)
                for f in range(NF):
                    k.emit("pe", lambda e, f=f, c=c, bk=bank, w=w: e.matmul(
                        PS[bk][:, 0:512], wqkT[:, f * 256 + 128:f * 256 + 256],
                        lnt_mv(f, c * 512, 512),
                        start=(f == 0), stop=(f == NF - 1)),
                           reads=(b_wqkT, b_LNT), writes=(b_PS[bank],))
                rope_chunk(kp[:, c * 512:(c + 1) * 512], b_kp, bank, c)
            for c in range(4):
                bank = c % 2
                for f in range(NF):
                    k.emit("pe", lambda e, f=f, c=c, bk=bank, w=w: e.matmul(
                        PS[bk][:, 0:512], wvT[:, f * 128:(f + 1) * 128],
                        lnt_mv(f, c * 512, 512),
                        start=(f == 0), stop=(f == NF - 1)),
                           reads=(b_wvT, b_LNT), writes=(b_PS[bank],))
                vtb = 2 + (c % 4)
                k.emit("dve", lambda e, bk=bank, c=c: e.tensor_copy(
                    qraw[c % 2][:, :], PS[bk][:, 0:512]),
                       reads=(b_PS[bank],), writes=(b_qraw[c % 2],))
                for j in range(4):
                    tt = c * 4 + j
                    k.emit("pe", lambda e, j=j, c=c, vb=vtb: e.transpose(
                        PS[vb][:, 0:128], qraw[c % 2][:, j * 128:(j + 1) * 128],
                        ident[:, :]),
                           reads=(b_qraw[c % 2], b_ident), writes=(b_PS[vtb],))
                    k.emit("dve", lambda e, t=tt, vb=vtb: e.tensor_copy(
                        vp[:, t * 130:t * 130 + 64], PS[vb][:, 0:64]),
                           reads=(b_PS[vtb],), writes=(b_vp,))
                    k.emit("dve", lambda e, t=tt, vb=vtb: e.tensor_copy(
                        vp[:, t * 130 + 65:t * 130 + 129], PS[vb][:, 64:128]),
                           reads=(b_PS[vtb],), writes=(b_vp,))

            for qc in range(4):
                q0 = qc * 512
                qb = qc % 2
                bank = qb
                rope_tables(qc)
                for f in range(NF):
                    k.emit("pe", lambda e, f=f, qc=qc, bk=bank, w=w: e.matmul(
                        PS[bk][:, 0:512], wqkT[:, f * 256:f * 256 + 128],
                        lnt_mv(f, qc * 512, 512),
                        start=(f == 0), stop=(f == NF - 1)),
                           reads=(b_wqkT, b_LNT), writes=(b_PS[bank],))
                rope_chunk(qch[qb][:, :], b_qch[qb], bank, qc)
                ktmax = 4 * (qc + 1)
                for side in range(2):
                    base = side * 64
                    qh = qch[qb][base:base + 64, :]
                    kh = kp[base:base + 64, :]
                    pvb = 6 + side
                    first = True
                    for kt in range(ktmax):
                        k0 = kt * 128
                        sb = 2 + (kt % 4)
                        k.emit("pe", lambda e, kt=kt, sb=sb, qh=qh, kh=kh: e.matmul(
                            PS[sb][:, 0:512], kh[:, kt * 128:(kt + 1) * 128],
                            qh[:, :], start=True, stop=True),
                               reads=(b_qch[qb], b_kp), writes=(b_PS[sb],))
                        dbase = q0 - k0
                        if dbase < 127:
                            u0 = 384 + dbase
                            k.emit("dve", lambda e, sb=sb, u0=u0: e.tensor_add(
                                PS[sb][:, 0:512], PS[sb][:, 0:512], cm[:, u0:u0 + 512]),
                                   reads=(b_cm, b_PS[sb]), writes=(b_PS[sb],))
                        pt, bpt = ptil[kt % 2], b_ptil[kt % 2]
                        k.emit("act", lambda e, sb=sb, pt=pt: e.activation(
                            pt[:, :], PS[sb][:, 0:512], AF.Exp, scale=0.125),
                               reads=(b_PS[sb],), writes=(bpt,))
                        voff = 0 if side == 0 else 65
                        k.emit("pe", lambda e, kt=kt, pt=pt, fr=first, vo=voff,
                               pvb=pvb, lastk=(kt == ktmax - 1): e.matmul(
                            PS[pvb][0:65, 0:512],
                            vp[:, kt * 130 + vo:kt * 130 + vo + 65],
                            pt[:, :], start=fr, stop=lastk),
                               reads=(b_vp, bpt), writes=(b_PS[pvb],))
                        first = False
                    pv_norm_evac(atn[side][:, :], b_atn[side], pvb, 2 + (ktmax % 4))
                    woT = woTa if side == 0 else woTb
                    b_woT = b_woTa if side == 0 else b_woTb
                    for j in range(4):
                        tt = qc * 4 + j
                        obk = j % 2
                        for oc in range(2):
                            k.emit("pe", lambda e, j=j, oc=oc, bk=obk, woT=woT,
                                   side=side: e.matmul(
                                PS[bk][:, 0:512], atn[side][:, j * 128:(j + 1) * 128],
                                woT[:, oc * 512:(oc + 1) * 512],
                                start=True, stop=True),
                                   reads=(b_atn[side], b_woT), writes=(b_PS[obk],))
                            k.emit("dve", lambda e, tt=tt, oc=oc, bk=obk, X=X:
                                   e.tensor_add(
                                       X[:, tt * D + oc * 512:tt * D + (oc + 1) * 512],
                                       X[:, tt * D + oc * 512:tt * D + (oc + 1) * 512],
                                       PS[bk][:, 0:512]),
                                   reads=(b_PS[obk], b_X), writes=(b_X,))
        a1.close()
        k.barrier()

        if stage >= 2:
            emit_ffn(k, nc, mk, xs, X, b_X, xtile, LNT, b_LNT, lnt_mv, layernorm,
                     PS, b_PS, cb, scb_d, ident, b_ident, fbr_d, rank1)
            k.barrier()

        if stage >= 3:
            X, b_X = emit_attn_b(
                k, nc, mk, xs, s, X, b_X, x1stk, xtile, LNT, b_LNT, lnt_mv,
                layernorm, PS, b_PS, ident, b_ident, ones, b_ones,
                smalls, b_smalls, b1h_d, b2r_d, sm0_d, dbt_d,
                wq_b, wk_b, wv_b, wo_b, w1, w2, bor_d,
                g1r_d, b1r_d, g2r_d, b2lr_d,
                ptil, b_ptil, recbc, b_recbc, sqs, b_sqs,
                nrm, b_nrm, xspill, rank1, knorm_pass, bound_bcast,
                pv_norm_evac, ld)

            def xtile(tt, X=X):
                return X[:, tt * D:(tt + 1) * D]
            k.barrier()

        if stage >= 4:
            emit_ffn(k, nc, mk, xs, X, b_X, xtile, LNT, b_LNT, lnt_mv, layernorm,
                     PS, b_PS, cb, scb_d, ident, b_ident, fbr_d, rank1)
            k.barrier()

        yv = bass.AP(y_out.tensor, s * T * D, [[D, 128], [128 * D, NT], [1, D]])
        k.emit("sd", lambda e, yv=yv, X=X: e.dma_start(out=yv, in_=X[:, :]),
               reads=(b_X,), kind="d", dstream=2)
        k.barrier()
        if stage < 3:
            x1stk.close()
        xs.close()

    for s in range(B_LOC):
        do_seq(s)

    k.finish()
    nops = len(k.ops)

    with contextlib.ExitStack() as semstk:
        for name, ch in k.chans.items():
            ch.sem_c = semstk.enter_context(nc.semaphore(f"sc_{name}"))
            ch.sem_d = [semstk.enter_context(nc.semaphore(f"sd_{name}{i}"))
                        for i in range(NDSTREAM)]
        k.finalize()
        with nc.Block() as block:
            @block.tensor
            def _(eng):
                k.run(eng, "pe")

            @block.vector
            def _(eng):
                k.run(eng, "dve")

            @block.scalar
            def _(eng):
                k.run(eng, "act")

            @block.gpsimd
            def _(eng):
                k.run(eng, "gp")

            @block.sync
            def _(eng):
                k.run(eng, "sd")

    return nc, nops


def emit_ffn(k, nc, mk, xs, X, b_X, xtile, LNT, b_LNT, lnt_mv, layernorm,
             PS, b_PS, cb, scb_d, ident, b_ident, fbr_d, rank1,
             dbgbufs=None):
    layernorm(xtile, b_X, None, None, lnt=LNT, b_lnt=b_LNT)
    if DBG and dbgbufs is not None:
        stats, b_stats, rcw, b_rcw, Xb = dbgbufs
        k.emit("gp", lambda e: e.dma_start(
            out=k.dbg3[0:128, 2048:2112], in_=stats[:, :]),
               reads=(b_stats,), kind="d")
        k.emit("gp", lambda e: e.dma_start(
            out=k.dbg3[0:128, 2112:2240], in_=rcw[:, 128:256]),
               reads=(b_rcw,), kind="d")
        k.emit("gp", lambda e: e.dma_start(
            out=k.dbg3[0:128, 6144:7168], in_=Xb[:, 3 * D:4 * D]),
               reads=(b_X,), kind="d")
        k.emit("gp", lambda e: e.dma_start(
            out=k.dbg3[0:128, 4096:5120], in_=LNT[:, 0:1024]),
               reads=(b_LNT,), kind="d")
        k.barrier()
    k.barrier()
    f1 = contextlib.ExitStack()
    cbt, b_cbt = mk(f1, "cbt", [128, NF * D], side="right")
    cst, b_cst = mk(f1, "cst", [128, D], side="right")
    scb, b_scb = mk(f1, "scb", [128, D], side="right")
    rows, b_rows = mk(f1, "rowsf", [128, D], side="right")
    k.emit("sd", lambda e: e.dma_start(out=rows[64:65, :], in_=fbr_d),
           reads=(b_rows,), writes=(b_rows,), kind="d")
    k.emit("sd", lambda e: e.dma_start(out=scb[:, :], in_=scb_d),
           reads=(b_scb,), writes=(b_scb,), kind="d")
    k.emit("dve", lambda e: e.tensor_scalar_mul(scb[:, :], scb[:, :], 1.0 / 127.0),
           reads=(b_scb,), writes=(b_scb,))
    for ntile in range(NF):
        k.emit("sd", lambda e, n=ntile: e.dma_start(
            out=cst[:, :], in_=cb[n * 128:(n + 1) * 128, :]),
               reads=(b_cst,), writes=(b_cst,), kind="d")
        for f in range(NF):
            bank = f % 2
            k.emit("pe", lambda e, f=f, bk=bank: e.transpose(
                PS[bk][:, 0:128], cst[:, f * 128:(f + 1) * 128], ident[:, :]),
                   reads=(b_cst, b_ident), writes=(b_PS[bank],))
            k.emit("dve", lambda e, f=f, n=ntile, bk=bank: e.tensor_mul(
                cbt[:, f * D + n * 128:f * D + (n + 1) * 128], PS[bk][:, 0:128],
                scb[:, n * 128:(n + 1) * 128]),
                   reads=(b_PS[bank], b_scb), writes=(b_cbt,))
    for tt in range(NT):
        for oc in range(2):
            bank = 2 + ((tt * 2 + oc) % 2)
            for f in range(NF):
                k.emit("pe", lambda e, tt=tt, oc=oc, f=f, bk=bank: e.matmul(
                    PS[bk][:, 0:512], lnt_mv(f, tt * 128, 128),
                    cbt[:, f * D + oc * 512:f * D + (oc + 1) * 512],
                    start=(f == 0), stop=False),
                       reads=(b_LNT, b_cbt), writes=(b_PS[bank],))
            rank1(bank, 64, oc * 512, 512, False, True, b_rows, rows)
            k.emit("dve", lambda e, tt=tt, oc=oc, bk=bank: e.tensor_add(
                X[:, tt * D + oc * 512:tt * D + (oc + 1) * 512],
                X[:, tt * D + oc * 512:tt * D + (oc + 1) * 512], PS[bk][:, 0:512]),
                   reads=(b_PS[bank], b_X), writes=(b_X,))
    f1.close()


def emit_attn_b(k, nc, mk, xs, s, X, b_X, x1stk, xtile, LNT, b_LNT, lnt_mv,
                layernorm, PS, b_PS, ident, b_ident, ones, b_ones,
                smalls, b_smalls, b1h_d, b2r_d, sm0_d, dbt_d,
                wq_b, wk_b, wv_b, wo_b, w1, w2, bor_d,
                g1r_d, b1r_d, g2r_d, b2lr_d,
                ptil, b_ptil, recbc, b_recbc, sqs, b_sqs,
                nrm, b_nrm, xspill, rank1, knorm_pass, bound_bcast,
                pv_norm_evac, ld):
    xsp_out = bass.AP(xspill, 0, [[D, 128], [128 * D, NT], [1, D]])
    k.emit("sd", lambda e: e.dma_start(out=xsp_out, in_=X[:, :]),
           reads=(b_X,), kind="d", dstream=2)
    layernorm(xtile, b_X, xtile, b_X, lnt=LNT, b_lnt=b_LNT)
    TOK2, b_TOK2 = X, b_X

    def t2tile(tt, TOK2=TOK2):
        return TOK2[:, tt * D:(tt + 1) * D]

    k.barrier()

    b1 = contextlib.ExitStack()
    sm0, b_sm0 = mk(b1, "sm0", [128, 512], side="right")
    dbt, b_dbt = mk(b1, "dbt", [128, 28], side="right")
    ld(dbt[:, :], dbt_d, b_dbt)
    dtmp, b_dtmp = zip(*[mk(b1, f"dtmp{i}", [128, 512], side="right") for i in range(2)])
    kp, b_kp = mk(b1, "kpb", [128, T], side="right")
    vp, b_vp = mk(b1, "vpb", [128, NT * 130], side="right")
    qch, b_qch = zip(*[mk(b1, f"qchb{i}", [128, 512], side="right") for i in range(2)])
    atn, b_atn = zip(*[mk(b1, f"atnb{i}", [64, 512], side="right") for i in range(2)])
    wqs, b_wqs = mk(b1, "wqs", [128, D], side="right")
    wks, b_wks = mk(b1, "wks", [128, D], side="right")
    wvs, b_wvs = mk(b1, "wvs", [128, D], side="right")
    woa, b_woa = mk(b1, "woab", [64, D], side="right")
    wob, b_wob = mk(b1, "wobb", [64, D], side="right")
    crow, b_crow = mk(b1, "crow", [1, D], side="right")
    qraw, b_qraw = zip(*[mk(b1, f"qrawb{i}", [128, 512], side="right") for i in range(2)])

    ld(sm0[:, :], sm0_d, b_sm0)
    k.emit("sd", lambda e: e.dma_start(out=crow[0:1, :], in_=bor_d),
           reads=(b_crow,), writes=(b_crow,), kind="d")
    for tt in range(NT):
        k.emit("gp", lambda e, c=tt: e.memset(vp[:, c * 130 + 64:c * 130 + 65], 1.0),
               writes=(b_vp,))
        k.emit("gp", lambda e, c=tt: e.memset(vp[:, c * 130 + 129:c * 130 + 130], 1.0),
               writes=(b_vp,))

    def load_qkv(p):
        for wsb, b_wsb, wd in ((wqs, b_wqs, wq_b), (wks, b_wks, wk_b),
                               (wvs, b_wvs, wv_b)):
            k.emit("sd", lambda e, wsb=wsb, wd=wd, p=p: e.dma_start(
                out=wsb[:, :],
                in_=bass.AP(wd.tensor, p * 128, [[D, 128], [128 * D, 8], [1, 128]])),
                   reads=(b_wsb,), writes=(b_wsb,), kind="d", dstream=p % 2)

    def load_wo(p):
        k.emit("sd", lambda e, p=p: e.dma_start(
            out=woa[:, :], in_=wo_b[p * 128:p * 128 + 64, :]),
               reads=(b_woa,), writes=(b_woa,), kind="d", dstream=p % 2)
        k.emit("sd", lambda e, p=p: e.dma_start(
            out=wob[:, :], in_=wo_b[p * 128 + 64:p * 128 + 128, :]),
               reads=(b_wob,), writes=(b_wob,), kind="d", dstream=p % 2)

    load_qkv(0)
    load_wo(0)
    for pr in range(NPAIR):
        for side in range(2):
            h = pr * 2 + side
            wo_sb = woa if side == 0 else wob
            b_wo = b_woa if side == 0 else b_wob
            for oc in range(2):
                k.emit("pe", lambda e, h=h, oc=oc, wo_sb=wo_sb: e.matmul(
                    PS[0][0:1, 0:512], smalls[0:64, 16 + h:17 + h],
                    wo_sb[:, oc * 512:(oc + 1) * 512], start=True, stop=True),
                       reads=(b_smalls, b_wo), writes=(b_PS[0],))
                k.emit("dve", lambda e, oc=oc: e.tensor_add(
                    crow[0:1, oc * 512:(oc + 1) * 512],
                    crow[0:1, oc * 512:(oc + 1) * 512], PS[0][0:1, 0:512]),
                       reads=(b_PS[0],), writes=(b_crow,))

        for c in range(4):
            bank = c % 2
            for f in range(NF):
                k.emit("pe", lambda e, f=f, c=c, bk=bank: e.matmul(
                    PS[bk][:, 0:512], wks[:, f * 128:(f + 1) * 128],
                    lnt_mv(f, c * 512, 512), start=(f == 0), stop=(f == NF - 1)),
                       reads=(b_wks, b_LNT), writes=(b_PS[bank],))
            k.emit("dve", lambda e, c=c, p=pr, bk=bank: e.tensor_scalar(
                kp[:, c * 512:(c + 1) * 512], PS[bk][:, 0:512],
                scalar1=smalls[:, 8 + p:9 + p], scalar2=None, op0=ALU.add),
                   reads=(b_PS[bank], b_smalls), writes=(b_kp,))
        for c in range(4):
            bank = c % 2
            qr, b_qr = qraw[c % 2], b_qraw[c % 2]
            for f in range(NF):
                k.emit("pe", lambda e, f=f, c=c, bk=bank: e.matmul(
                    PS[bk][:, 0:512], wvs[:, f * 128:(f + 1) * 128],
                    lnt_mv(f, c * 512, 512), start=(f == 0), stop=(f == NF - 1)),
                       reads=(b_wvs, b_LNT), writes=(b_PS[bank],))
            k.emit("dve", lambda e, bk=bank, qr=qr: e.tensor_copy(
                qr[:, :], PS[bk][:, 0:512]),
                   reads=(b_PS[bank],), writes=(b_qr,))
            vtb = 2 + (c % 4)
            for j in range(4):
                tt = c * 4 + j
                k.emit("pe", lambda e, j=j, vb=vtb, qr=qr: e.transpose(
                    PS[vb][:, 0:128], qr[:, j * 128:(j + 1) * 128], ident[:, :]),
                       reads=(b_qr, b_ident), writes=(b_PS[vtb],))
                k.emit("dve", lambda e, t=tt, vb=vtb: e.tensor_copy(
                    vp[:, t * 130:t * 130 + 64], PS[vb][:, 0:64]),
                       reads=(b_PS[vtb],), writes=(b_vp,))
                k.emit("dve", lambda e, t=tt, vb=vtb: e.tensor_copy(
                    vp[:, t * 130 + 65:t * 130 + 129], PS[vb][:, 64:128]),
                       reads=(b_PS[vtb],), writes=(b_vp,))

        for qc in range(4):
            q0 = qc * 512
            qb = qc % 2
            bank = qb
            for f in range(NF):
                k.emit("pe", lambda e, f=f, qc=qc, bk=bank: e.matmul(
                    PS[bk][:, 0:512], wqs[:, f * 128:(f + 1) * 128],
                    lnt_mv(f, qc * 512, 512), start=(f == 0), stop=(f == NF - 1)),
                       reads=(b_wqs, b_LNT), writes=(b_PS[bank],))
            k.emit("dve", lambda e, p=pr, bk=bank, qb=qb: e.tensor_scalar(
                qch[qb][:, :], PS[bk][:, 0:512], scalar1=smalls[:, 0 + p:1 + p],
                scalar2=None, op0=ALU.add),
                   reads=(b_PS[bank], b_smalls), writes=(b_qch[qb],))
            if qc == 3 and pr + 1 < NPAIR:
                load_qkv(pr + 1)
            for side in range(2):
                h = pr * 2 + side
                base = side * 64
                qh = qch[qb][base:base + 64, :]
                kh = kp[base:base + 64, :]
                pvb = 6 + side
                for kt in range(NT):
                    k0 = kt * 128
                    sb = 2 + (kt % 4)
                    k.emit("pe", lambda e, kt=kt, sb=sb, qh=qh, kh=kh: e.matmul(
                        PS[sb][:, 0:512], kh[:, kt * 128:(kt + 1) * 128],
                        qh[:, :], start=True, stop=True),
                           reads=(b_qch[qb], b_kp), writes=(b_PS[sb],))
                    dt, bdt = dtmp[kt % 2], b_dtmp[kt % 2]
                    dbi = (q0 - k0 + 1920) // 128
                    k.emit("act", lambda e, dt=dt, dbi=dbi: e.activation(
                        dt[:, :], sm0[:, :], AF.Abs, bias=dbt[:, dbi:dbi + 1]),
                           reads=(b_sm0, b_dbt), writes=(bdt,))
                    k.emit("dve", lambda e, sb=sb, dt=dt: e.scalar_tensor_tensor(
                        PS[sb][:, 0:512], dt[:, :], smalls[:, 32:33],
                        PS[sb][:, 0:512], op0=ALU.mult, op1=ALU.add),
                           reads=(bdt, b_smalls, b_PS[sb]), writes=(b_PS[sb],))
                    if kt == 0 and h > 0:
                        k.emit("dve", lambda e, sb=sb, h=h: e.tensor_scalar_add(
                            PS[sb][0:h, 0:512], PS[sb][0:h, 0:512], NEGM),
                               reads=(b_PS[sb],), writes=(b_PS[sb],))
                    pt, bpt = ptil[kt % 2], b_ptil[kt % 2]
                    k.emit("act", lambda e, sb=sb, pt=pt: e.activation(
                        pt[:, :], PS[sb][:, 0:512], AF.Exp, scale=0.125),
                           reads=(b_PS[sb],), writes=(bpt,))
                    voff = 0 if side == 0 else 65
                    k.emit("pe", lambda e, kt=kt, pt=pt, vo=voff, pvb=pvb: e.matmul(
                        PS[pvb][0:65, 0:512], vp[:, kt * 130 + vo:kt * 130 + vo + 65],
                        pt[:, :], start=(kt == 0), stop=(kt == NT - 1)),
                           reads=(b_vp, bpt), writes=(b_PS[pvb],))
                pv_norm_evac(atn[side][:, :], b_atn[side], pvb, 2 + (NT % 4))
                wo_sb = woa if side == 0 else wob
                b_wo = b_woa if side == 0 else b_wob
                last = (pr == NPAIR - 1 and side == 1)
                for j in range(4):
                    tt = qc * 4 + j
                    obk = j % 2
                    for oc in range(2):
                        k.emit("pe", lambda e, j=j, oc=oc, bk=obk, wo_sb=wo_sb,
                               side=side, lst=last: e.matmul(
                            PS[bk][:, 0:512], atn[side][:, j * 128:(j + 1) * 128],
                            wo_sb[:, oc * 512:(oc + 1) * 512],
                            start=True, stop=not lst),
                               reads=(b_atn[side], b_wo), writes=(b_PS[obk],))
                        if last:
                            rank1(obk, 0, oc * 512, 512, False, True,
                                  b_rowsrc=b_crow, rowsrc=crow)
                        k.emit("dve", lambda e, tt=tt, oc=oc, bk=obk: e.tensor_add(
                            TOK2[:, tt * D + oc * 512:tt * D + (oc + 1) * 512],
                            TOK2[:, tt * D + oc * 512:tt * D + (oc + 1) * 512],
                            PS[bk][:, 0:512]),
                               reads=(b_PS[obk], b_TOK2), writes=(b_TOK2,))
        if pr + 1 < NPAIR:
            load_wo(pr + 1)
    b1.close()
    k.barrier()

    ga2 = contextlib.ExitStack()
    g2r, b_g2r = mk(ga2, "g2r", [128, D], side="right")
    b2lr, b_b2lr = mk(ga2, "b2lr", [128, D], side="right")
    b1h, b_b1h = mk(ga2, "b1h", [128, 32], side="right")
    ga = contextlib.ExitStack()
    g1r, b_g1r = mk(ga, "g1r", [128, D], side="right")
    b1r, b_b1r = mk(ga, "b1r", [128, D], side="right")
    for t, d, b in ((g1r, g1r_d, b_g1r), (b1r, b1r_d, b_b1r),
                    (g2r, g2r_d, b_g2r), (b2lr, b2lr_d, b_b2lr),
                    (b1h, b1h_d, b_b1h)):
        k.emit("sd", lambda e, t=t, d=d: e.dma_start(out=t[:, :], in_=d),
               reads=(b,), writes=(b,), kind="d")
    k.barrier()
    layernorm(t2tile, b_TOK2, t2tile, b_TOK2, lnt=LNT, b_lnt=b_LNT,
              gain=g1r, b_gain=b_g1r, bias=b1r, b_bias=b_b1r)
    ga.close()
    k.barrier()

    m1 = contextlib.ExitStack()
    w1c, b_w1c = [None, None], [None, None]
    w2c, b_w2c = [None, None], [None, None]
    for i in range(2):
        w1c[i], b_w1c[i] = mk(m1, f"w1c{i}", [128, NF * 256], side="right")
        w2c[i], b_w2c[i] = mk(m1, f"w2c{i}", [128, 2 * D], side="right")
    hT, b_hT = mk(m1, "hT", [128, 2 * T], side="right")
    rows, b_rows = mk(m1, "rowsm", [128, D], side="right")
    k.emit("sd", lambda e: e.dma_start(out=rows[32:33, :], in_=b2r_d),
           reads=(b_rows,), writes=(b_rows,), kind="d")
    NHC = DFF // 256
    for hc in range(NHC):
        sl = hc % 2
        k.emit("sd", lambda e, hc=hc, sl=sl: e.dma_start(
            out=w1c[sl][:, :],
            in_=bass.AP(w1.tensor, hc * 256, [[DFF, 128], [128 * DFF, NF], [1, 256]])),
               reads=(b_w1c[sl],), writes=(b_w1c[sl],), kind="d", dstream=sl)
        k.emit("sd", lambda e, hc=hc, sl=sl: e.dma_start(
            out=w2c[sl][:, :],
            in_=bass.AP(w2.tensor, hc * 256 * D, [[D, 128], [128 * D, 2], [1, D]])),
               reads=(b_w2c[sl],), writes=(b_w2c[sl],), kind="d", dstream=sl)
        for ht in range(2):
            for c in range(4):
                bank = c % 2
                for f in range(NF):
                    k.emit("pe", lambda e, f=f, c=c, ht=ht, sl=sl, bk=bank: e.matmul(
                        PS[bk][:, 0:512],
                        w1c[sl][:, f * 256 + ht * 128:f * 256 + (ht + 1) * 128],
                        lnt_mv(f, c * 512, 512), start=(f == 0), stop=(f == NF - 1)),
                           reads=(b_w1c[sl], b_LNT), writes=(b_PS[bank],))
                hcol = hc * 2 + ht
                k.emit("act", lambda e, c=c, ht=ht, hcol=hcol, bk=bank: e.activation(
                    hT[:, ht * T + c * 512:ht * T + (c + 1) * 512], PS[bk][:, 0:512],
                    AF.Relu, bias=b1h[:, hcol:hcol + 1]),
                       reads=(b_PS[bank], b_b1h), writes=(b_hT,))
        for tt in range(NT):
            bank = 2 + (tt % 4)
            for oc in range(2):
                for ht in range(2):
                    lastmm = (ht == 1)
                    k.emit("pe", lambda e, tt=tt, oc=oc, ht=ht, sl=sl, bk=bank,
                           lm=lastmm, lh=(hc == NHC - 1): e.matmul(
                        PS[bk][:, 0:512],
                        hT[:, ht * T + tt * 128:ht * T + (tt + 1) * 128],
                        w2c[sl][:, ht * D + oc * 512:ht * D + (oc + 1) * 512],
                        start=(ht == 0), stop=(lm and not lh)),
                           reads=(b_hT, b_w2c[sl]), writes=(b_PS[bank],))
                if hc == NHC - 1:
                    rank1(bank, 32, oc * 512, 512, False, True, b_rows, rows)
                k.emit("dve", lambda e, tt=tt, oc=oc, bk=bank: e.tensor_add(
                    TOK2[:, tt * D + oc * 512:tt * D + (oc + 1) * 512],
                    TOK2[:, tt * D + oc * 512:tt * D + (oc + 1) * 512],
                    PS[bk][:, 0:512]),
                       reads=(b_PS[bank], b_TOK2), writes=(b_TOK2,))
    m1.close()
    k.barrier()

    layernorm(t2tile, b_TOK2, t2tile, b_TOK2,
              gain=g2r, b_gain=b_g2r, bias=b2lr, b_bias=b_b2lr)
    ga2.close()
    k.barrier()

    k.emit("gp", lambda e: e.dma_start(out=xsp_out, in_=TOK2[:, :],
                                       accum_op=ALU.add),
           reads=(b_TOK2,), kind="d")
    k.barrier()
    x1stk.close()
    X2, b_X2 = mk(xs, f"X2_{s}", [128, NT * D])
    xsp_in = bass.AP(xspill, 0, [[D, 128], [128 * D, NT], [1, D]])
    k.emit("sd", lambda e: e.dma_start(out=X2[:, :], in_=xsp_in),
           writes=(b_X2,), kind="d", dstream=2)
    k.barrier()
    return X2, b_X2




def _host_constants():
    c = {}
    c["identc"] = np.eye(128, dtype=np.float32)
    perm64 = np.array([2 * j for j in range(32)] + [2 * j + 1 for j in range(32)])
    perm2 = np.concatenate([perm64, perm64 + 64])
    p2 = np.zeros((128, 128), np.float32)
    p2[perm2, np.arange(128)] = 1.0
    c["p2c"] = p2
    c["onesc"] = np.ones((128, 128), np.float32)
    u = np.arange(1152)[None, :]
    p = np.arange(128)[:, None]
    c["cmc"] = np.where(u < p + 512, np.float32(NEGM), np.float32(0.0)).astype(np.float32)
    j = np.arange(512)[None, :]
    c["sm0c"] = (j - p).astype(np.float32)
    i = np.arange(32, dtype=np.float64)
    inv = 10000.0 ** (2.0 * i / 64.0)
    ang = np.arange(T, dtype=np.float64)[None, :] / inv[:, None]
    cos = np.cos(ang).astype(np.float32)
    sin = np.sin(ang).astype(np.float32)
    c["c64c"] = np.concatenate([cos, cos, cos, cos], axis=0)
    c["s64c"] = np.concatenate([-sin, sin, -sin, sin], axis=0)
    return c


def kernel(L0_attn_a_x, L0_attn_a_W_qkv, L0_attn_a_W_o,
           L0_attn_b_Wq, L0_attn_b_bq, L0_attn_b_Wk, L0_attn_b_bk,
           L0_attn_b_Wv, L0_attn_b_bv, L0_attn_b_Wo, L0_attn_b_bo,
           L0_attn_b_dist_scale, L0_attn_b_W1, L0_attn_b_b1,
           L0_attn_b_W2, L0_attn_b_b2, L0_attn_b_ln1_g, L0_attn_b_ln1_b,
           L0_attn_b_ln2_g, L0_attn_b_ln2_b, L0_ffn_CB, L0_ffn_SCB, L0_ffn_bias):
    f32 = np.float32
    asn = lambda a: np.ascontiguousarray(np.asarray(a, dtype=f32))
    x = asn(L0_attn_a_x)
    rep = lambda v: np.ascontiguousarray(
        np.broadcast_to(asn(v).reshape(1, D), (128, D)))
    shared = {
        "wqkv": asn(L0_attn_a_W_qkv), "wo_a": asn(L0_attn_a_W_o),
        "wq_b": asn(L0_attn_b_Wq), "wk_b": asn(L0_attn_b_Wk),
        "wv_b": asn(L0_attn_b_Wv), "wo_b": asn(L0_attn_b_Wo),
        "w1": asn(L0_attn_b_W1), "w2": asn(L0_attn_b_W2), "cb": asn(L0_ffn_CB),
        "bqhc": np.ascontiguousarray(asn(L0_attn_b_bq).reshape(8, 128).T),
        "bkhc": np.ascontiguousarray(asn(L0_attn_b_bk).reshape(8, 128).T),
        "bvfc": np.ascontiguousarray(asn(L0_attn_b_bv).reshape(16, 64).T),
        "b1hc": np.ascontiguousarray(asn(L0_attn_b_b1).reshape(32, 128).T),
        "b2rc": asn(L0_attn_b_b2).reshape(1, D),
        "borc": asn(L0_attn_b_bo).reshape(1, D),
        "fbrc": asn(L0_ffn_bias).reshape(1, D),
        "scbc": rep(L0_ffn_SCB),
        "g1rc": rep(L0_attn_b_ln1_g), "b1rc": rep(L0_attn_b_ln1_b),
        "g2rc": rep(L0_attn_b_ln2_g), "b2lrc": rep(L0_attn_b_ln2_b),
        "dscc": asn(L0_attn_b_dist_scale).reshape(1, 1),
        "epsc": np.full((128, 1), EPS, dtype=f32),
        "dbtc": np.ascontiguousarray(np.broadcast_to(
            (np.arange(28, dtype=f32) * 128 - 1920)[None, :], (128, 28))),
    }
    shared.update(_host_constants())
    n = 8
    in_maps = []
    for c in range(n):
        im = dict(shared)
        im["x"] = np.ascontiguousarray(x[c * B_LOC:(c + 1) * B_LOC])
        in_maps.append(im)
    nc, _ = build_program(STAGE)
    res = run_bass_kernel_spmd(nc, in_maps, core_ids=list(range(n)))
    return np.concatenate([r["y"] for r in res.results], axis=0)


# version 08
# speedup vs baseline: 1.2871x; 1.2758x over previous
import contextlib
import numpy as np
import concourse.bass as bass
import concourse.mybir as mybir
from concourse.bass_utils import run_bass_kernel_spmd

F32 = mybir.dt.float32
AF = mybir.ActivationFunctionType
ALU = mybir.AluOpType
AX = mybir.AxisListType

B_LOC, T, D, H, DH, DFF = 2, 2048, 1024, 16, 64, 4096
NT, NF = T // 128, D // 128
NPAIR = H // 2
EPS = 1e-5
NEGM = -1.0e9
STAGE = 4
PAIR_LIMIT = None
DBG = False



class Buf:
    __slots__ = ("ws", "rs", "name")

    def __init__(self, name=""):
        self.ws = []
        self.rs = []
        self.name = name


class Op:
    __slots__ = ("chan", "fn", "waits", "kind", "used", "val", "fwaits", "dstream")

    def __init__(self, chan, fn, waits, kind, dstream=0):
        self.chan = chan
        self.fn = fn
        self.waits = waits
        self.kind = kind
        self.used = False
        self.val = 0
        self.fwaits = ()
        self.dstream = dstream


NDSTREAM = 5


class Chan:
    def __init__(self, name):
        self.name = name
        self.last = None
        self.sem_c = None
        self.sem_d = [None] * NDSTREAM
        self.pending = []


class K:
    def __init__(self, nc):
        self.nc = nc
        self.ops = []
        self.chans = {n: Chan(n) for n in ("pe", "dve", "act", "gp", "sd")}

    def emit(self, chan, fn, reads=(), writes=(), kind="c", dstream=0):
        ch = self.chans[chan]
        waits, seen = [], set()

        def add(op):
            if op is not None and id(op) not in seen:
                seen.add(id(op))
                waits.append(op)

        for o in ch.pending:
            add(o)
        ch.pending = []
        for b in reads:
            for w in b.ws:
                add(w)
        for b in writes:
            for r in b.rs:
                add(r)
            for w in b.ws:
                add(w)
        op = Op(ch, fn, waits, kind, dstream)
        self.ops.append(op)
        ch.last = op
        for b in reads:
            b.rs.append(op)
        for b in writes:
            b.ws = [op]
            b.rs = []
        return op

    def barrier(self):
        lasts = [c.last for c in self.chans.values() if c.last is not None]
        for ch in self.chans.values():
            for o in lasts:
                if o.chan is not ch and o not in ch.pending:
                    ch.pending.append(o)

    def finish(self):
        lasts = [c.last for c in self.chans.values() if c.last is not None]
        ldma = {}
        for op in self.ops:
            if op.kind == "d":
                ldma[(op.chan.name, op.dstream)] = op
        ch = self.chans["sd"]
        waits = [o for o in lasts if o.chan is not ch] + list(ldma.values())
        op = Op(ch, None, waits, "c")
        self.ops.append(op)

    def finalize(self):
        for op in self.ops:
            for w in op.waits:
                w.used = True
        for ch in self.chans.values():
            ch._cc = 0
            ch._cd = [0] * NDSTREAM
            ch._wmax = {}
        cd_snap = []
        run = {(n, s): 0 for n in self.chans for s in range(NDSTREAM)}
        for op in self.ops:
            cd_snap.append(dict(run))
            if op.kind == "d":
                run[(op.chan.name, op.dstream)] += 16
                op.chan._cd[op.dstream] += 16
                op.val = op.chan._cd[op.dstream]
            elif op.used:
                op.chan._cc += 1
                op.val = op.chan._cc
        for i, op in enumerate(self.ops):
            fw = []
            for w in op.waits:
                if w.chan is op.chan and w.kind == "c":
                    continue
                if w.kind == "d":
                    sem = w.chan.sem_d[w.dstream]
                    val = cd_snap[i][(w.chan.name, w.dstream)]
                    if val < w.val:
                        val = w.val
                else:
                    sem = w.chan.sem_c
                    val = w.val
                key = id(sem)
                if op.chan._wmax.get(key, -1) >= val:
                    continue
                op.chan._wmax[key] = val
                fw.append((sem, val))
            op.fwaits = fw

    def run(self, eng, name):
        ch = self.chans[name]
        for op in self.ops:
            if op.chan is not ch:
                continue
            for sem, val in op.fwaits:
                eng.wait_ge(sem, val)
            if op.fn is not None:
                ins = op.fn(eng)
                if ins is not None:
                    if op.kind == "d":
                        ins.then_inc(ch.sem_d[op.dstream], 16)
                    elif op.used:
                        ins.then_inc(ch.sem_c, 1)





def build_program(stage=STAGE):
    nc = bass.Bass(trn_type="TRN2", target_bir_lowering=False,
                   detect_race_conditions=False, dynamic_dma_scratch_size=8192)
    k = K(nc)

    def din(name, shape):
        return nc.dram_tensor(name, shape, F32, kind="ExternalInput").ap()

    x_in = din("x", [B_LOC, T, D])
    wqkv = din("wqkv", [3 * D, D])
    wo_a = din("wo_a", [D, D])
    wq_b, wk_b, wv_b, wo_b = (din(n, [D, D]) for n in ("wq_b", "wk_b", "wv_b", "wo_b"))
    w1 = din("w1", [D, DFF])
    w2 = din("w2", [DFF, D])
    cb = din("cb", [D, D])
    ident_d = din("identc", [128, 128])
    p2_d = din("p2c", [128, 128])
    ones_d = din("onesc", [128, 128])
    cm_d = din("cmc", [128, 1152])
    sm0_d = din("sm0c", [128, 512])
    c64_d = din("c64c", [128, T])
    s64_d = din("s64c", [128, T])
    bqh_d = din("bqhc", [128, 8])
    bkh_d = din("bkhc", [128, 8])
    bvf_d = din("bvfc", [64, 16])
    b1h_d = din("b1hc", [128, 32])
    b2r_d = din("b2rc", [1, D])
    bor_d = din("borc", [1, D])
    fbr_d = din("fbrc", [1, D])
    scb_d = din("scbc", [128, D])
    g1r_d = din("g1rc", [128, D])
    b1r_d = din("b1rc", [128, D])
    g2r_d = din("g2rc", [128, D])
    b2lr_d = din("b2lrc", [128, D])
    dsc_d = din("dscc", [1, 1])
    eps_d = din("epsc", [128, 1])
    dbt_d = din("dbtc", [128, 28])

    y_out = nc.dram_tensor("y", [B_LOC, T, D], F32, kind="ExternalOutput").ap()
    dbg_out = nc.dram_tensor("dbg", [128, 4 * 512], F32, kind="ExternalOutput").ap() if DBG \
        else None
    dbg2_out = nc.dram_tensor("dbg2", [64, 2 * 32 * 512], F32,
                              kind="ExternalOutput").ap() if DBG else None
    dbg3_out = nc.dram_tensor("dbg3", [128, 8192], F32,
                              kind="ExternalOutput").ap() if DBG else None
    k.dbg3 = dbg3_out
    xspill = nc.dram_tensor("xspill", [T, D], F32, kind="Internal")

    gstack = contextlib.ExitStack()
    _uid = [0]

    def mk(stk, name, shape, dt=F32, side="left"):
        _uid[0] += 1
        h = stk.enter_context(nc.sbuf_tensor(f"{name}_u{_uid[0]}", shape, dt,
                                             side=side))
        return h, Buf(name)

    ident, b_ident = mk(gstack, "identc_s", [128, 128])
    ones, b_ones = mk(gstack, "onesc_s", [128, 128])
    smalls, b_smalls = mk(gstack, "smalls", [128, 64])
    stats, b_stats = mk(gstack, "stats", [128, 4 * NT])
    nrm, b_nrm = mk(gstack, "nrm", [1, 16])
    sqs, b_sqs = mk(gstack, "sqs", [128, 512])
    ptil, b_ptil = [None, None], [None, None]
    for i in range(2):
        ptil[i], b_ptil[i] = mk(gstack, f"ptil{i}", [128, 512])
    recbc, b_recbc = mk(gstack, "recbc", [128, 512])
    rcw, b_rcw = mk(gstack, "rcw", [128, 256])

    PS, b_PS = [], []
    for i in range(8):
        h = gstack.enter_context(nc.psum_tensor(f"ps{i}", [128, 512], F32))
        PS.append(h)
        b_PS.append(Buf(f"ps{i}"))

    def ld(dst, src, b):
        k.emit("sd", lambda e, d=dst, s=src: e.dma_start(out=d, in_=s),
               reads=(b,), writes=(b,), kind="d")

    ld(ident[:, :], ident_d, b_ident)
    ld(ones[:, :], ones_d, b_ones)
    ld(smalls[:, 0:8], bqh_d, b_smalls)
    ld(smalls[:, 8:16], bkh_d, b_smalls)
    ld(smalls[0:64, 16:32], bvf_d, b_smalls)
    ld(smalls[0:1, 33:34], dsc_d, b_smalls)
    ld(smalls[:, 35:36], eps_d, b_smalls)
    k.emit("dve", lambda e: e.memset(rcw[:, 0:128], 1.0), writes=(b_rcw,))
    k.emit("pe", lambda e: e.matmul(PS[7][:, 0:1], ones[0:1, 0:128],
                                    smalls[0:1, 33:34], start=True, stop=True),
           reads=(b_ones, b_smalls), writes=(b_PS[7],))
    k.emit("dve", lambda e: e.tensor_scalar_mul(smalls[:, 32:33], PS[7][:, 0:1], -1.0),
           reads=(b_PS[7],), writes=(b_smalls,))
    k.barrier()

    def layernorm(xt, b_xt, out_tiles, b_out, lnt=None, b_lnt=None,
                  gain=None, b_gain=None, bias=None, b_bias=None):
        for tt in range(NT):
            src = xt(tt)
            k.emit("dve", lambda e, s=src, c=tt: e.reduce_sum(
                stats[:, c:c + 1], s, axis=AX.X), reads=(b_xt,), writes=(b_stats,))
            k.emit("act", lambda e, s=src, c=tt: e.activation(
                PS[7][:, 0:512], s[:, 0:512], AF.Square,
                accum_out=stats[:, NT + c:NT + c + 1]),
                   reads=(b_xt,), writes=(b_PS[7], b_stats))
            k.emit("act", lambda e, s=src, c=tt: e.activation(
                PS[7][:, 0:512], s[:, 512:1024], AF.Square,
                accum_out=stats[:, 2 * NT + c:2 * NT + c + 1]),
                   reads=(b_xt,), writes=(b_PS[7], b_stats))
        k.emit("dve", lambda e: e.tensor_add(stats[:, NT:2 * NT], stats[:, NT:2 * NT],
                                             stats[:, 2 * NT:3 * NT]),
               reads=(b_stats,), writes=(b_stats,))
        k.emit("gp", lambda e: e.tensor_scalar_mul(stats[:, 0:NT], stats[:, 0:NT],
                                                   1.0 / D),
               reads=(b_stats,), writes=(b_stats,))
        k.emit("dve", lambda e: e.tensor_scalar_mul(stats[:, NT:2 * NT],
                                                    stats[:, NT:2 * NT], 1.0 / D),
               reads=(b_stats,), writes=(b_stats,))
        k.emit("gp", lambda e: e.tensor_mul(stats[:, 3 * NT:4 * NT],
                                            stats[:, 0:NT], stats[:, 0:NT]),
               reads=(b_stats,), writes=(b_stats,))
        k.emit("dve", lambda e: e.tensor_sub(stats[:, NT:2 * NT],
                                             stats[:, NT:2 * NT],
                                             stats[:, 3 * NT:4 * NT]),
               reads=(b_stats,), writes=(b_stats,))
        k.emit("gp", lambda e: e.tensor_scalar_add(rcw[:, 0:NT],
                                                    stats[:, NT:2 * NT], EPS),
               reads=(b_stats, b_rcw), writes=(b_rcw,))
        k.emit("act", lambda e: e.activation(rcw[:, 128:256], rcw[:, 0:128], AF.Ln),
               reads=(b_rcw,), writes=(b_rcw,))
        k.emit("act", lambda e: e.activation(rcw[:, 128:256], rcw[:, 128:256],
                                             AF.Exp, scale=-0.5),
               reads=(b_rcw,), writes=(b_rcw,))
        for _nr in range(2):
            k.emit("dve", lambda e: e.tensor_mul(sqs[:, 0:128], rcw[:, 128:256],
                                                 rcw[:, 128:256]),
                   reads=(b_rcw,), writes=(b_sqs,))
            k.emit("dve", lambda e: e.tensor_mul(sqs[:, 0:128], sqs[:, 0:128],
                                                 rcw[:, 0:128]),
                   reads=(b_rcw, b_sqs), writes=(b_sqs,))
            k.emit("dve", lambda e: e.tensor_scalar(sqs[:, 0:128], sqs[:, 0:128],
                                                    scalar1=-0.5, scalar2=1.5,
                                                    op0=ALU.mult, op1=ALU.add),
                   reads=(b_sqs,), writes=(b_sqs,))
            k.emit("dve", lambda e: e.tensor_mul(rcw[:, 128:256], rcw[:, 128:256],
                                                 sqs[:, 0:128]),
                   reads=(b_sqs, b_rcw), writes=(b_rcw,))
        for tt in range(NT):
            src = xt(tt)
            for half in range(2):
                if out_tiles is not None:
                    dsub = out_tiles(tt)[:, half * 512:(half + 1) * 512]
                    b_d = b_out
                else:
                    dsub = ptil[half][:, :]
                    b_d = b_ptil[half]
                k.emit("dve", lambda e, s=src, d=dsub, c=tt, hh=half: e.tensor_scalar(
                    d, s[:, hh * 512:(hh + 1) * 512], scalar1=stats[:, c:c + 1],
                    scalar2=rcw[:, 128 + c:129 + c],
                    op0=ALU.subtract, op1=ALU.mult),
                       reads=(b_xt, b_stats, b_rcw), writes=(b_d,))
                if gain is not None:
                    k.emit("dve", lambda e, d=dsub, hh=half: e.tensor_mul(
                        d, d, gain[:, hh * 512:(hh + 1) * 512]),
                           reads=(b_d, b_gain), writes=(b_d,))
                    k.emit("dve", lambda e, d=dsub, hh=half: e.tensor_add(
                        d, d, bias[:, hh * 512:(hh + 1) * 512]),
                           reads=(b_d, b_bias), writes=(b_d,))
            if lnt is None:
                continue
            for half in range(2):
                bank = half
                if out_tiles is not None:
                    srcs = [out_tiles(tt)[:, (half * 4 + j) * 128:
                                          (half * 4 + j + 1) * 128] for j in range(4)]
                    b_s = b_out
                else:
                    srcs = [ptil[half][:, j * 128:(j + 1) * 128] for j in range(4)]
                    b_s = b_ptil[half]
                for j in range(4):
                    k.emit("pe", lambda e, s=srcs[j], bk=bank, j=j: e.transpose(
                        PS[bk][:, j * 128:(j + 1) * 128], s, ident[:, :]),
                           reads=(b_s, b_ident), writes=(b_PS[bank],))
                out_ap = bass.AP(lnt, (half * 4) * T + tt * 128,
                                 [[NF * T, 128], [T, 4], [1, 128]])
                k.emit("dve", lambda e, o=out_ap, bk=bank: e.tensor_copy(
                    o, PS[bk][:, 0:512]),
                       reads=(b_PS[bank],), writes=(b_lnt,))

    def rank1(bank, row_base, col0, w, start, stop, b_rowsrc, rowsrc):
        src = rowsrc
        b_src = b_rowsrc
        k.emit("pe", lambda e: e.matmul(
            PS[bank][:, 0:w], ones[row_base:row_base + 1, 0:128],
            src[row_base:row_base + 1, col0:col0 + w], start=start, stop=stop),
               reads=(b_ones, b_src), writes=(b_PS[bank],))

    def bound_bcast(qsb, b_qsb, qbase, ksb, b_ksb, kcol):
        k.emit("act", lambda e: e.activation(sqs[qbase:qbase + 64, :],
                                             qsb[qbase:qbase + 64, 0:512], AF.Square),
               reads=(b_qsb,), writes=(b_sqs,))
        k.emit("pe", lambda e: e.matmul(PS[5][0:1, 0:512],
                                        ones[qbase:qbase + 64, 0:1],
                                        sqs[qbase:qbase + 64, :],
                                        start=True, stop=True),
               reads=(b_sqs, b_ones), writes=(b_PS[5],))
        k.emit("dve", lambda e: e.reduce_max(nrm[0:1, 10:11], PS[5][0:1, 0:512],
                                             axis=AX.X),
               reads=(b_PS[5],), writes=(b_nrm,))
        k.emit("dve", lambda e, kc=kcol: e.tensor_add(
            nrm[0:1, 11:12], nrm[0:1, 10:11], nrm[0:1, kc:kc + 1]),
               reads=(b_nrm,), writes=(b_nrm,))
        k.emit("pe", lambda e: e.matmul(PS[5][:, 0:1], ones[0:1, 0:128],
                                        nrm[0:1, 11:12], start=True, stop=True),
               reads=(b_ones, b_nrm), writes=(b_PS[5],))
        k.emit("dve", lambda e: e.tensor_scalar_mul(smalls[:, 34:35],
                                                    PS[5][:, 0:1], -1.0 / 16.0),
               reads=(b_PS[5],), writes=(b_smalls,))

    def knorm_pass(ksb, b_ksb, cols_a, cols_b):
        for c in range(4):
            k.emit("act", lambda e, c=c: e.activation(
                sqs[:, :], ksb[:, c * 512:(c + 1) * 512], AF.Square),
                   reads=(b_ksb,), writes=(b_sqs,))
            for base, col in ((0, c), (64, 4 + c)):
                k.emit("pe", lambda e, b=base: e.matmul(
                    PS[5][0:1, 0:512], ones[b:b + 64, 0:1],
                    sqs[b:b + 64, :], start=True, stop=True),
                       reads=(b_sqs, b_ones), writes=(b_PS[5],))
                k.emit("dve", lambda e, col=col: e.reduce_max(
                    nrm[0:1, col:col + 1], PS[5][0:1, 0:512], axis=AX.X),
                       reads=(b_PS[5],), writes=(b_nrm,))
        k.emit("dve", lambda e, ca=cols_a: e.reduce_max(
            nrm[0:1, ca:ca + 1], nrm[0:1, 0:4], axis=AX.X),
               reads=(b_nrm,), writes=(b_nrm,))
        k.emit("dve", lambda e, cb=cols_b: e.reduce_max(
            nrm[0:1, cb:cb + 1], nrm[0:1, 4:8], axis=AX.X),
               reads=(b_nrm,), writes=(b_nrm,))

    def pv_recip(pvb):
        k.emit("act", lambda e: e.activation(recbc[64:65, 0:512],
                                             PS[pvb][64:65, 0:512], AF.Ln),
               reads=(b_PS[pvb],), writes=(b_recbc,))
        k.emit("act", lambda e: e.activation(recbc[64:65, 0:512],
                                             recbc[64:65, 0:512],
                                             AF.Exp, scale=-1.0),
               reads=(b_recbc,), writes=(b_recbc,))
        k.emit("dve", lambda e: e.tensor_mul(sqs[64:65, 0:512],
                                             recbc[64:65, 0:512],
                                             PS[pvb][64:65, 0:512]),
               reads=(b_recbc, b_PS[pvb]), writes=(b_sqs,))
        k.emit("gp", lambda e: e.tensor_scalar(sqs[64:65, 0:512],
                                               sqs[64:65, 0:512],
                                               scalar1=-1.0, scalar2=2.0,
                                               op0=ALU.mult, op1=ALU.add),
               reads=(b_sqs,), writes=(b_sqs,))
        k.emit("dve", lambda e: e.tensor_mul(recbc[64:65, 0:512],
                                             recbc[64:65, 0:512],
                                             sqs[64:65, 0:512]),
               reads=(b_sqs, b_recbc), writes=(b_recbc,))

    def pv_bcast_evac(atn_ap, b_atn, pvb, bcb):
        k.emit("pe", lambda e: e.matmul(PS[bcb][0:64, 0:512], ones[64:65, 0:64],
                                        recbc[64:65, 0:512], start=True, stop=True),
               reads=(b_ones, b_recbc), writes=(b_PS[bcb],))
        k.emit("dve", lambda e: e.tensor_copy(recbc[0:64, 0:512], PS[bcb][0:64, 0:512]),
               reads=(b_PS[bcb],), writes=(b_recbc,))
        k.emit("dve", lambda e: e.tensor_mul(atn_ap, PS[pvb][0:64, 0:512],
                                             recbc[0:64, 0:512]),
               reads=(b_PS[pvb], b_recbc), writes=(b_atn,))

    def pv_norm_evac(atn_ap, b_atn, pvb, bcb):
        pv_recip(pvb)
        pv_bcast_evac(atn_ap, b_atn, pvb, bcb)

    def do_seq(s):
        xs = contextlib.ExitStack()
        x1stk = contextlib.ExitStack()
        X, b_X = mk(x1stk, f"X{s}", [128, NT * D], side="right")

        def xtile(tt, X=X):
            return X[:, tt * D:(tt + 1) * D]

        xv = bass.AP(x_in.tensor, s * T * D, [[D, 128], [128 * D, NT], [1, D]])
        k.emit("sd", lambda e, xv=xv, X=X: e.dma_start(out=X[:, :], in_=xv),
               writes=(b_X,), kind="d", dstream=2)
        k.barrier()

        LNT, b_LNT = mk(xs, f"LNT{s}", [128, NF * T])

        def lnt_mv(f, c0, w, LNT=LNT):
            return LNT[:, f * T + c0:f * T + c0 + w]

        layernorm(xtile, b_X, None, None, lnt=LNT, b_lnt=b_LNT)
        k.barrier()
        a1 = contextlib.ExitStack()
        p2m, b_p2m = mk(a1, "p2m", [128, 128], side="right")
        cm, b_cm = mk(a1, "cm", [128, 896], side="right")
        rtc, b_rtc = zip(*[mk(a1, f"rtc{i}", [128, 512], side="right") for i in range(2)])
        rts, b_rts = zip(*[mk(a1, f"rts{i}", [128, 512], side="right") for i in range(2)])
        kp, b_kp = mk(a1, "kp", [128, T], side="right")
        vp, b_vp = mk(a1, "vp", [128, NT * 130], side="right")
        qch, b_qch = zip(*[mk(a1, f"qch{i}", [128, 512], side="right") for i in range(2)])
        qraw, b_qraw = zip(*[mk(a1, f"qraw{i}", [128, 512], side="right") for i in range(2)])
        qswp, b_qswp = zip(*[mk(a1, f"qswp{i}", [128, 512], side="right") for i in range(2)])
        atn, b_atn = zip(*[mk(a1, f"atn{i}", [64, 512], side="right") for i in range(2)])
        wst, b_wst = zip(*[mk(a1, f"wst{i}", [128, D], side="right") for i in range(2)])
        wqkT, b_wqkT = mk(a1, "wqkT", [128, NF * 256], side="right")
        wvT, b_wvT = mk(a1, "wvT", [128, NF * 128], side="right")
        woTa, b_woTa = mk(a1, "woTa", [64, D], side="right")
        woTb, b_woTb = mk(a1, "woTb", [64, D], side="right")

        ld(p2m[:, :], p2_d, b_p2m)
        ld(cm[:, :], cm_d[:, 128:1024], b_cm)
        for tt in range(NT):
            k.emit("gp", lambda e, c=tt: e.memset(
                vp[:, c * 130 + 64:c * 130 + 65], 1.0), writes=(b_vp,))
            k.emit("gp", lambda e, c=tt: e.memset(
                vp[:, c * 130 + 129:c * 130 + 130], 1.0), writes=(b_vp,))

        ruse = [0]

        def rope_chunk(dst_ap, b_dst, bank, c):
            ri = ruse[0] % 2
            ruse[0] += 1
            qr, b_qr = qraw[ri], b_qraw[ri]
            qs, b_qs = qswp[ri], b_qswp[ri]
            k.emit("dve", lambda e, bk=bank, qr=qr: e.tensor_copy(
                qr[:, :], PS[bk][:, 0:512]),
                   reads=(b_PS[bank],), writes=(b_qr,))
            for sl in range(4):
                srcl = [32, 0, 96, 64][sl]
                k.emit("sd", lambda e, sl=sl, srcl=srcl, qr=qr, qs=qs: e.dma_start(
                    out=qs[sl * 32:(sl + 1) * 32, :], in_=qr[srcl:srcl + 32, :]),
                       reads=(b_qr,), writes=(b_qs,), kind="d", dstream=4)
            k.emit("dve", lambda e, qr=qr, ri=ri: e.tensor_mul(
                qr[:, :], qr[:, :], rtc[ri][:, :]),
                   reads=(b_qr, b_rtc[ri]), writes=(b_qr,))
            k.emit("dve", lambda e, qs=qs, ri=ri: e.tensor_mul(
                qs[:, :], qs[:, :], rts[ri][:, :]),
                   reads=(b_qs, b_rts[ri]), writes=(b_qs,))
            k.emit("dve", lambda e, d=dst_ap, qr=qr, qs=qs: e.tensor_add(
                d, qr[:, :], qs[:, :]),
                   reads=(b_qr, b_qs), writes=(b_dst,))

        def rope_tables(c):
            ri = ruse[0] % 2
            k.emit("sd", lambda e, c=c, ri=ri: e.dma_start(
                out=rtc[ri][:, :], in_=c64_d[:, c * 512:(c + 1) * 512]),
                   reads=(b_rtc[ri],), writes=(b_rtc[ri],), kind="d", dstream=3)
            k.emit("sd", lambda e, c=c, ri=ri: e.dma_start(
                out=rts[ri][:, :], in_=s64_d[:, c * 512:(c + 1) * 512]),
                   reads=(b_rts[ri],), writes=(b_rts[ri],), kind="d", dstream=3)

        nplim = PAIR_LIMIT if PAIR_LIMIT is not None else NPAIR
        wring = [0]

        for pr in range(nplim):
            def wload(off, p=pr):
                w = wring[0] % 2
                wring[0] += 1
                k.emit("sd", lambda e, p=p, off=off, w=w: e.dma_start(
                    out=wst[w][:, :],
                    in_=(wqkv[off + p * 128:off + (p + 1) * 128, :] if off >= 0 else
                         bass.AP(wo_a.tensor, p * 128,
                                 [[D, 128], [128 * D, 8], [1, 128]]))),
                       reads=(b_wst[w],), writes=(b_wst[w],), kind="d", dstream=w)
                return w
            w = wload(0)
            for f in range(NF):
                k.emit("pe", lambda e, f=f, w=w: e.transpose(
                    PS[0][:, 0:128], wst[w][:, f * 128:(f + 1) * 128], p2m[:, :]),
                       reads=(b_wst[w], b_p2m), writes=(b_PS[0],))
                k.emit("dve", lambda e, f=f: e.tensor_copy(
                    wqkT[:, f * 256:f * 256 + 128], PS[0][:, 0:128]),
                       reads=(b_PS[0],), writes=(b_wqkT,))
            w = wload(D)
            for f in range(NF):
                k.emit("pe", lambda e, f=f, w=w: e.transpose(
                    PS[1][:, 0:128], wst[w][:, f * 128:(f + 1) * 128], p2m[:, :]),
                       reads=(b_wst[w], b_p2m), writes=(b_PS[1],))
                k.emit("dve", lambda e, f=f: e.tensor_copy(
                    wqkT[:, f * 256 + 128:f * 256 + 256], PS[1][:, 0:128]),
                       reads=(b_PS[1],), writes=(b_wqkT,))
            w = wload(2 * D)
            for f in range(NF):
                k.emit("pe", lambda e, f=f, w=w: e.transpose(
                    PS[0][:, 0:128], wst[w][:, f * 128:(f + 1) * 128], ident[:, :]),
                       reads=(b_wst[w], b_ident), writes=(b_PS[0],))
                k.emit("dve", lambda e, f=f: e.tensor_copy(
                    wvT[:, f * 128:(f + 1) * 128], PS[0][:, 0:128]),
                       reads=(b_PS[0],), writes=(b_wvT,))
            w = wload(-1)
            for c in range(NF):
                k.emit("pe", lambda e, c=c, w=w: e.transpose(
                    PS[1][0:64, 0:128], wst[w][:, c * 128:c * 128 + 64], ident[:, :]),
                       reads=(b_wst[w], b_ident), writes=(b_PS[1],))
                k.emit("dve", lambda e, c=c: e.tensor_copy(
                    woTa[:, c * 128:(c + 1) * 128], PS[1][0:64, 0:128]),
                       reads=(b_PS[1],), writes=(b_woTa,))
                k.emit("pe", lambda e, c=c, w=w: e.transpose(
                    PS[1][0:64, 0:128], wst[w][:, c * 128 + 64:(c + 1) * 128],
                    ident[:, :]),
                       reads=(b_wst[w], b_ident), writes=(b_PS[1],))
                k.emit("dve", lambda e, c=c: e.tensor_copy(
                    woTb[:, c * 128:(c + 1) * 128], PS[1][0:64, 0:128]),
                       reads=(b_PS[1],), writes=(b_woTb,))

            for c in range(4):
                bank = c % 2
                rope_tables(c)
                for f in range(NF):
                    k.emit("pe", lambda e, f=f, c=c, bk=bank, w=w: e.matmul(
                        PS[bk][:, 0:512], wqkT[:, f * 256 + 128:f * 256 + 256],
                        lnt_mv(f, c * 512, 512),
                        start=(f == 0), stop=(f == NF - 1)),
                           reads=(b_wqkT, b_LNT), writes=(b_PS[bank],))
                rope_chunk(kp[:, c * 512:(c + 1) * 512], b_kp, bank, c)
            for c in range(4):
                bank = c % 2
                for f in range(NF):
                    k.emit("pe", lambda e, f=f, c=c, bk=bank, w=w: e.matmul(
                        PS[bk][:, 0:512], wvT[:, f * 128:(f + 1) * 128],
                        lnt_mv(f, c * 512, 512),
                        start=(f == 0), stop=(f == NF - 1)),
                           reads=(b_wvT, b_LNT), writes=(b_PS[bank],))
                vtb = 2 + (c % 4)
                k.emit("dve", lambda e, bk=bank, c=c: e.tensor_copy(
                    qraw[c % 2][:, :], PS[bk][:, 0:512]),
                       reads=(b_PS[bank],), writes=(b_qraw[c % 2],))
                for j in range(4):
                    tt = c * 4 + j
                    k.emit("pe", lambda e, j=j, c=c, vb=vtb: e.transpose(
                        PS[vb][:, 0:128], qraw[c % 2][:, j * 128:(j + 1) * 128],
                        ident[:, :]),
                           reads=(b_qraw[c % 2], b_ident), writes=(b_PS[vtb],))
                    k.emit("dve", lambda e, t=tt, vb=vtb: e.tensor_copy(
                        vp[:, t * 130:t * 130 + 64], PS[vb][:, 0:64]),
                           reads=(b_PS[vtb],), writes=(b_vp,))
                    k.emit("dve", lambda e, t=tt, vb=vtb: e.tensor_copy(
                        vp[:, t * 130 + 65:t * 130 + 129], PS[vb][:, 64:128]),
                           reads=(b_PS[vtb],), writes=(b_vp,))

            for qc in range(4):
                q0 = qc * 512
                qb = qc % 2
                bank = qb
                rope_tables(qc)
                for f in range(NF):
                    k.emit("pe", lambda e, f=f, qc=qc, bk=bank, w=w: e.matmul(
                        PS[bk][:, 0:512], wqkT[:, f * 256:f * 256 + 128],
                        lnt_mv(f, qc * 512, 512),
                        start=(f == 0), stop=(f == NF - 1)),
                           reads=(b_wqkT, b_LNT), writes=(b_PS[bank],))
                rope_chunk(qch[qb][:, :], b_qch[qb], bank, qc)
                ktmax = 4 * (qc + 1)

                prb = (ptil[0], ptil[1], qraw[0])
                b_prb = (b_ptil[0], b_ptil[1], b_qraw[0])

                def st_exp(side, kt, qb=qb, q0=q0):
                    base = side * 64
                    qh = qch[qb][base:base + 64, :]
                    kh = kp[base:base + 64, :]
                    sb = 2 + (kt % 4)
                    k.emit("pe", lambda e, kt=kt, sb=sb, qh=qh, kh=kh: e.matmul(
                        PS[sb][:, 0:512], kh[:, kt * 128:(kt + 1) * 128],
                        qh[:, :], start=True, stop=True),
                           reads=(b_qch[qb], b_kp), writes=(b_PS[sb],))
                    dbase = q0 - kt * 128
                    if dbase < 127:
                        u0 = 384 + dbase
                        k.emit("dve", lambda e, sb=sb, u0=u0: e.tensor_add(
                            PS[sb][:, 0:512], PS[sb][:, 0:512], cm[:, u0:u0 + 512]),
                               reads=(b_cm, b_PS[sb]), writes=(b_PS[sb],))
                    pt, bpt = prb[kt % 3], b_prb[kt % 3]
                    k.emit("act", lambda e, sb=sb, pt=pt: e.activation(
                        pt[:, :], PS[sb][:, 0:512], AF.Exp, scale=0.125),
                           reads=(b_PS[sb],), writes=(bpt,))

                def pv(side, kt, ktmax=ktmax):
                    pvb = 6 + side
                    voff = 0 if side == 0 else 65
                    pt, bpt = prb[kt % 3], b_prb[kt % 3]
                    k.emit("pe", lambda e, kt=kt, pt=pt, vo=voff, pvb=pvb,
                           fr=(kt == 0), lastk=(kt == ktmax - 1): e.matmul(
                        PS[pvb][0:65, 0:512],
                        vp[:, kt * 130 + vo:kt * 130 + vo + 65],
                        pt[:, :], start=fr, stop=lastk),
                           reads=(b_vp, bpt), writes=(b_PS[pvb],))

                def pipeline(side, ktmax=ktmax, LOOK=2):
                    for kt in range(ktmax + LOOK):
                        if kt < ktmax:
                            st_exp(side, kt)
                        if kt >= LOOK:
                            pv(side, kt - LOOK)

                def o_proj(side, qc=qc):
                    woT = woTa if side == 0 else woTb
                    b_woT = b_woTa if side == 0 else b_woTb
                    for j in range(4):
                        tt = qc * 4 + j
                        obk = j % 2
                        for oc in range(2):
                            k.emit("pe", lambda e, j=j, oc=oc, bk=obk, woT=woT,
                                   side=side: e.matmul(
                                PS[bk][:, 0:512], atn[side][:, j * 128:(j + 1) * 128],
                                woT[:, oc * 512:(oc + 1) * 512],
                                start=True, stop=True),
                                   reads=(b_atn[side], b_woT), writes=(b_PS[obk],))
                            k.emit("dve", lambda e, tt=tt, oc=oc, bk=obk, X=X:
                                   e.tensor_add(
                                       X[:, tt * D + oc * 512:tt * D + (oc + 1) * 512],
                                       X[:, tt * D + oc * 512:tt * D + (oc + 1) * 512],
                                       PS[bk][:, 0:512]),
                                   reads=(b_PS[obk], b_X), writes=(b_X,))

                pipeline(0)
                pv_recip(6)
                pipeline(1)
                pv_bcast_evac(atn[0][:, :], b_atn[0], 6, 2 + (ktmax % 4))
                pv_recip(7)
                o_proj(0)
                pv_bcast_evac(atn[1][:, :], b_atn[1], 7, 2 + (ktmax % 4))
                o_proj(1)
        a1.close()
        k.barrier()

        if stage >= 2:
            emit_ffn(k, nc, mk, xs, X, b_X, xtile, LNT, b_LNT, lnt_mv, layernorm,
                     PS, b_PS, cb, scb_d, ident, b_ident, fbr_d, rank1)
            k.barrier()

        if stage >= 3:
            X, b_X = emit_attn_b(
                k, nc, mk, xs, s, X, b_X, x1stk, xtile, LNT, b_LNT, lnt_mv,
                layernorm, PS, b_PS, ident, b_ident, ones, b_ones,
                smalls, b_smalls, b1h_d, b2r_d, sm0_d, dbt_d,
                wq_b, wk_b, wv_b, wo_b, w1, w2, bor_d,
                g1r_d, b1r_d, g2r_d, b2lr_d,
                ptil, b_ptil, recbc, b_recbc, sqs, b_sqs,
                nrm, b_nrm, xspill, rank1, knorm_pass, bound_bcast,
                pv_norm_evac, pv_recip, pv_bcast_evac, ld)

            def xtile(tt, X=X):
                return X[:, tt * D:(tt + 1) * D]
            k.barrier()

        if stage >= 4:
            emit_ffn(k, nc, mk, xs, X, b_X, xtile, LNT, b_LNT, lnt_mv, layernorm,
                     PS, b_PS, cb, scb_d, ident, b_ident, fbr_d, rank1)
            k.barrier()

        yv = bass.AP(y_out.tensor, s * T * D, [[D, 128], [128 * D, NT], [1, D]])
        k.emit("sd", lambda e, yv=yv, X=X: e.dma_start(out=yv, in_=X[:, :]),
               reads=(b_X,), kind="d", dstream=2)
        k.barrier()
        if stage < 3:
            x1stk.close()
        xs.close()

    for s in range(B_LOC):
        do_seq(s)

    k.finish()
    nops = len(k.ops)

    with contextlib.ExitStack() as semstk:
        for name, ch in k.chans.items():
            ch.sem_c = semstk.enter_context(nc.semaphore(f"sc_{name}"))
            ch.sem_d = [semstk.enter_context(nc.semaphore(f"sd_{name}{i}"))
                        for i in range(NDSTREAM)]
        k.finalize()
        with nc.Block() as block:
            @block.tensor
            def _(eng):
                k.run(eng, "pe")

            @block.vector
            def _(eng):
                k.run(eng, "dve")

            @block.scalar
            def _(eng):
                k.run(eng, "act")

            @block.gpsimd
            def _(eng):
                k.run(eng, "gp")

            @block.sync
            def _(eng):
                k.run(eng, "sd")

    return nc, nops


def emit_ffn(k, nc, mk, xs, X, b_X, xtile, LNT, b_LNT, lnt_mv, layernorm,
             PS, b_PS, cb, scb_d, ident, b_ident, fbr_d, rank1,
             dbgbufs=None):
    layernorm(xtile, b_X, None, None, lnt=LNT, b_lnt=b_LNT)
    if DBG and dbgbufs is not None:
        stats, b_stats, rcw, b_rcw, Xb = dbgbufs
        k.emit("gp", lambda e: e.dma_start(
            out=k.dbg3[0:128, 2048:2112], in_=stats[:, :]),
               reads=(b_stats,), kind="d")
        k.emit("gp", lambda e: e.dma_start(
            out=k.dbg3[0:128, 2112:2240], in_=rcw[:, 128:256]),
               reads=(b_rcw,), kind="d")
        k.emit("gp", lambda e: e.dma_start(
            out=k.dbg3[0:128, 6144:7168], in_=Xb[:, 3 * D:4 * D]),
               reads=(b_X,), kind="d")
        k.emit("gp", lambda e: e.dma_start(
            out=k.dbg3[0:128, 4096:5120], in_=LNT[:, 0:1024]),
               reads=(b_LNT,), kind="d")
        k.barrier()
    k.barrier()
    f1 = contextlib.ExitStack()
    cbt, b_cbt = mk(f1, "cbt", [128, NF * D], side="right")
    cst, b_cst = mk(f1, "cst", [128, D], side="right")
    scb, b_scb = mk(f1, "scb", [128, D], side="right")
    rows, b_rows = mk(f1, "rowsf", [128, D], side="right")
    k.emit("sd", lambda e: e.dma_start(out=rows[64:65, :], in_=fbr_d),
           reads=(b_rows,), writes=(b_rows,), kind="d")
    k.emit("sd", lambda e: e.dma_start(out=scb[:, :], in_=scb_d),
           reads=(b_scb,), writes=(b_scb,), kind="d")
    k.emit("dve", lambda e: e.tensor_scalar_mul(scb[:, :], scb[:, :], 1.0 / 127.0),
           reads=(b_scb,), writes=(b_scb,))
    for ntile in range(NF):
        k.emit("sd", lambda e, n=ntile: e.dma_start(
            out=cst[:, :], in_=cb[n * 128:(n + 1) * 128, :]),
               reads=(b_cst,), writes=(b_cst,), kind="d")
        for f in range(NF):
            bank = f % 2
            k.emit("pe", lambda e, f=f, bk=bank: e.transpose(
                PS[bk][:, 0:128], cst[:, f * 128:(f + 1) * 128], ident[:, :]),
                   reads=(b_cst, b_ident), writes=(b_PS[bank],))
            k.emit("dve", lambda e, f=f, n=ntile, bk=bank: e.tensor_mul(
                cbt[:, f * D + n * 128:f * D + (n + 1) * 128], PS[bk][:, 0:128],
                scb[:, n * 128:(n + 1) * 128]),
                   reads=(b_PS[bank], b_scb), writes=(b_cbt,))
    for tt in range(NT):
        for oc in range(2):
            bank = 2 + ((tt * 2 + oc) % 2)
            for f in range(NF):
                k.emit("pe", lambda e, tt=tt, oc=oc, f=f, bk=bank: e.matmul(
                    PS[bk][:, 0:512], lnt_mv(f, tt * 128, 128),
                    cbt[:, f * D + oc * 512:f * D + (oc + 1) * 512],
                    start=(f == 0), stop=False),
                       reads=(b_LNT, b_cbt), writes=(b_PS[bank],))
            rank1(bank, 64, oc * 512, 512, False, True, b_rows, rows)
            k.emit("dve", lambda e, tt=tt, oc=oc, bk=bank: e.tensor_add(
                X[:, tt * D + oc * 512:tt * D + (oc + 1) * 512],
                X[:, tt * D + oc * 512:tt * D + (oc + 1) * 512], PS[bk][:, 0:512]),
                   reads=(b_PS[bank], b_X), writes=(b_X,))
    f1.close()


def emit_attn_b(k, nc, mk, xs, s, X, b_X, x1stk, xtile, LNT, b_LNT, lnt_mv,
                layernorm, PS, b_PS, ident, b_ident, ones, b_ones,
                smalls, b_smalls, b1h_d, b2r_d, sm0_d, dbt_d,
                wq_b, wk_b, wv_b, wo_b, w1, w2, bor_d,
                g1r_d, b1r_d, g2r_d, b2lr_d,
                ptil, b_ptil, recbc, b_recbc, sqs, b_sqs,
                nrm, b_nrm, xspill, rank1, knorm_pass, bound_bcast,
                pv_norm_evac, pv_recip, pv_bcast_evac, ld):
    xsp_out = bass.AP(xspill, 0, [[D, 128], [128 * D, NT], [1, D]])
    k.emit("sd", lambda e: e.dma_start(out=xsp_out, in_=X[:, :]),
           reads=(b_X,), kind="d", dstream=2)
    layernorm(xtile, b_X, xtile, b_X, lnt=LNT, b_lnt=b_LNT)
    TOK2, b_TOK2 = X, b_X

    def t2tile(tt, TOK2=TOK2):
        return TOK2[:, tt * D:(tt + 1) * D]

    k.barrier()

    b1 = contextlib.ExitStack()
    sm0, b_sm0 = mk(b1, "sm0", [128, 512], side="right")
    dbt, b_dbt = mk(b1, "dbt", [128, 28], side="right")
    ld(dbt[:, :], dbt_d, b_dbt)
    dtmp, b_dtmp = zip(*[mk(b1, f"dtmp{i}", [128, 512], side="right") for i in range(2)])
    kp, b_kp = mk(b1, "kpb", [128, T], side="right")
    vp, b_vp = mk(b1, "vpb", [128, NT * 130], side="right")
    qch, b_qch = zip(*[mk(b1, f"qchb{i}", [128, 512], side="right") for i in range(2)])
    atn, b_atn = zip(*[mk(b1, f"atnb{i}", [64, 512], side="right") for i in range(2)])
    wqs, b_wqs = mk(b1, "wqs", [128, D], side="right")
    wks, b_wks = mk(b1, "wks", [128, D], side="right")
    wvs, b_wvs = mk(b1, "wvs", [128, D], side="right")
    woa, b_woa = mk(b1, "woab", [64, D], side="right")
    wob, b_wob = mk(b1, "wobb", [64, D], side="right")
    crow, b_crow = mk(b1, "crow", [1, D], side="right")
    qraw, b_qraw = zip(*[mk(b1, f"qrawb{i}", [128, 512], side="right") for i in range(2)])

    ld(sm0[:, :], sm0_d, b_sm0)
    k.emit("sd", lambda e: e.dma_start(out=crow[0:1, :], in_=bor_d),
           reads=(b_crow,), writes=(b_crow,), kind="d")
    for tt in range(NT):
        k.emit("gp", lambda e, c=tt: e.memset(vp[:, c * 130 + 64:c * 130 + 65], 1.0),
               writes=(b_vp,))
        k.emit("gp", lambda e, c=tt: e.memset(vp[:, c * 130 + 129:c * 130 + 130], 1.0),
               writes=(b_vp,))

    def load_qkv(p):
        for wsb, b_wsb, wd in ((wqs, b_wqs, wq_b), (wks, b_wks, wk_b),
                               (wvs, b_wvs, wv_b)):
            k.emit("sd", lambda e, wsb=wsb, wd=wd, p=p: e.dma_start(
                out=wsb[:, :],
                in_=bass.AP(wd.tensor, p * 128, [[D, 128], [128 * D, 8], [1, 128]])),
                   reads=(b_wsb,), writes=(b_wsb,), kind="d", dstream=p % 2)

    def load_wo(p):
        k.emit("sd", lambda e, p=p: e.dma_start(
            out=woa[:, :], in_=wo_b[p * 128:p * 128 + 64, :]),
               reads=(b_woa,), writes=(b_woa,), kind="d", dstream=p % 2)
        k.emit("sd", lambda e, p=p: e.dma_start(
            out=wob[:, :], in_=wo_b[p * 128 + 64:p * 128 + 128, :]),
               reads=(b_wob,), writes=(b_wob,), kind="d", dstream=p % 2)

    load_qkv(0)
    load_wo(0)
    for pr in range(NPAIR):
        for side in range(2):
            h = pr * 2 + side
            wo_sb = woa if side == 0 else wob
            b_wo = b_woa if side == 0 else b_wob
            for oc in range(2):
                k.emit("pe", lambda e, h=h, oc=oc, wo_sb=wo_sb: e.matmul(
                    PS[0][0:1, 0:512], smalls[0:64, 16 + h:17 + h],
                    wo_sb[:, oc * 512:(oc + 1) * 512], start=True, stop=True),
                       reads=(b_smalls, b_wo), writes=(b_PS[0],))
                k.emit("dve", lambda e, oc=oc: e.tensor_add(
                    crow[0:1, oc * 512:(oc + 1) * 512],
                    crow[0:1, oc * 512:(oc + 1) * 512], PS[0][0:1, 0:512]),
                       reads=(b_PS[0],), writes=(b_crow,))

        for c in range(4):
            bank = c % 2
            for f in range(NF):
                k.emit("pe", lambda e, f=f, c=c, bk=bank: e.matmul(
                    PS[bk][:, 0:512], wks[:, f * 128:(f + 1) * 128],
                    lnt_mv(f, c * 512, 512), start=(f == 0), stop=(f == NF - 1)),
                       reads=(b_wks, b_LNT), writes=(b_PS[bank],))
            k.emit("dve", lambda e, c=c, p=pr, bk=bank: e.tensor_scalar(
                kp[:, c * 512:(c + 1) * 512], PS[bk][:, 0:512],
                scalar1=smalls[:, 8 + p:9 + p], scalar2=None, op0=ALU.add),
                   reads=(b_PS[bank], b_smalls), writes=(b_kp,))
        for c in range(4):
            bank = c % 2
            qr, b_qr = qraw[c % 2], b_qraw[c % 2]
            for f in range(NF):
                k.emit("pe", lambda e, f=f, c=c, bk=bank: e.matmul(
                    PS[bk][:, 0:512], wvs[:, f * 128:(f + 1) * 128],
                    lnt_mv(f, c * 512, 512), start=(f == 0), stop=(f == NF - 1)),
                       reads=(b_wvs, b_LNT), writes=(b_PS[bank],))
            k.emit("dve", lambda e, bk=bank, qr=qr: e.tensor_copy(
                qr[:, :], PS[bk][:, 0:512]),
                   reads=(b_PS[bank],), writes=(b_qr,))
            vtb = 2 + (c % 4)
            for j in range(4):
                tt = c * 4 + j
                k.emit("pe", lambda e, j=j, vb=vtb, qr=qr: e.transpose(
                    PS[vb][:, 0:128], qr[:, j * 128:(j + 1) * 128], ident[:, :]),
                       reads=(b_qr, b_ident), writes=(b_PS[vtb],))
                k.emit("dve", lambda e, t=tt, vb=vtb: e.tensor_copy(
                    vp[:, t * 130:t * 130 + 64], PS[vb][:, 0:64]),
                       reads=(b_PS[vtb],), writes=(b_vp,))
                k.emit("dve", lambda e, t=tt, vb=vtb: e.tensor_copy(
                    vp[:, t * 130 + 65:t * 130 + 129], PS[vb][:, 64:128]),
                       reads=(b_PS[vtb],), writes=(b_vp,))

        for qc in range(4):
            q0 = qc * 512
            qb = qc % 2
            bank = qb
            for f in range(NF):
                k.emit("pe", lambda e, f=f, qc=qc, bk=bank: e.matmul(
                    PS[bk][:, 0:512], wqs[:, f * 128:(f + 1) * 128],
                    lnt_mv(f, qc * 512, 512), start=(f == 0), stop=(f == NF - 1)),
                       reads=(b_wqs, b_LNT), writes=(b_PS[bank],))
            k.emit("dve", lambda e, p=pr, bk=bank, qb=qb: e.tensor_scalar(
                qch[qb][:, :], PS[bk][:, 0:512], scalar1=smalls[:, 0 + p:1 + p],
                scalar2=None, op0=ALU.add),
                   reads=(b_PS[bank], b_smalls), writes=(b_qch[qb],))
            if qc == 3 and pr + 1 < NPAIR:
                load_qkv(pr + 1)

            prb = (ptil[0], ptil[1], qraw[0])
            b_prb = (b_ptil[0], b_ptil[1], b_qraw[0])

            def st_exp_b(side, kt, qb=qb, q0=q0, pr=pr):
                h = pr * 2 + side
                base = side * 64
                qh = qch[qb][base:base + 64, :]
                kh = kp[base:base + 64, :]
                sb = 2 + (kt % 4)
                k.emit("pe", lambda e, kt=kt, sb=sb, qh=qh, kh=kh: e.matmul(
                    PS[sb][:, 0:512], kh[:, kt * 128:(kt + 1) * 128],
                    qh[:, :], start=True, stop=True),
                       reads=(b_qch[qb], b_kp), writes=(b_PS[sb],))
                dt, bdt = dtmp[kt % 2], b_dtmp[kt % 2]
                dbi = (q0 - kt * 128 + 1920) // 128
                k.emit("act", lambda e, dt=dt, dbi=dbi: e.activation(
                    dt[:, :], sm0[:, :], AF.Abs, bias=dbt[:, dbi:dbi + 1]),
                       reads=(b_sm0, b_dbt), writes=(bdt,))
                k.emit("dve", lambda e, sb=sb, dt=dt: e.scalar_tensor_tensor(
                    PS[sb][:, 0:512], dt[:, :], smalls[:, 32:33],
                    PS[sb][:, 0:512], op0=ALU.mult, op1=ALU.add),
                       reads=(bdt, b_smalls, b_PS[sb]), writes=(b_PS[sb],))
                if kt == 0 and h > 0:
                    k.emit("dve", lambda e, sb=sb, h=h: e.tensor_scalar_add(
                        PS[sb][0:h, 0:512], PS[sb][0:h, 0:512], NEGM),
                           reads=(b_PS[sb],), writes=(b_PS[sb],))
                pt, bpt = prb[kt % 3], b_prb[kt % 3]
                k.emit("act", lambda e, sb=sb, pt=pt: e.activation(
                    pt[:, :], PS[sb][:, 0:512], AF.Exp, scale=0.125),
                       reads=(b_PS[sb],), writes=(bpt,))

            def pv_b(side, kt):
                pvb = 6 + side
                voff = 0 if side == 0 else 65
                pt, bpt = prb[kt % 3], b_prb[kt % 3]
                k.emit("pe", lambda e, kt=kt, pt=pt, vo=voff, pvb=pvb: e.matmul(
                    PS[pvb][0:65, 0:512], vp[:, kt * 130 + vo:kt * 130 + vo + 65],
                    pt[:, :], start=(kt == 0), stop=(kt == NT - 1)),
                       reads=(b_vp, bpt), writes=(b_PS[pvb],))

            def pipeline_b(side, LOOK=2):
                for kt in range(NT + LOOK):
                    if kt < NT:
                        st_exp_b(side, kt)
                    if kt >= LOOK:
                        pv_b(side, kt - LOOK)

            def o_proj_b(side, qc=qc, pr=pr):
                wo_sb = woa if side == 0 else wob
                b_wo = b_woa if side == 0 else b_wob
                last = (pr == NPAIR - 1 and side == 1)
                for j in range(4):
                    tt = qc * 4 + j
                    obk = j % 2
                    for oc in range(2):
                        k.emit("pe", lambda e, j=j, oc=oc, bk=obk, wo_sb=wo_sb,
                               side=side, lst=last: e.matmul(
                            PS[bk][:, 0:512], atn[side][:, j * 128:(j + 1) * 128],
                            wo_sb[:, oc * 512:(oc + 1) * 512],
                            start=True, stop=not lst),
                               reads=(b_atn[side], b_wo), writes=(b_PS[obk],))
                        if last:
                            rank1(obk, 0, oc * 512, 512, False, True,
                                  b_rowsrc=b_crow, rowsrc=crow)
                        k.emit("dve", lambda e, tt=tt, oc=oc, bk=obk: e.tensor_add(
                            TOK2[:, tt * D + oc * 512:tt * D + (oc + 1) * 512],
                            TOK2[:, tt * D + oc * 512:tt * D + (oc + 1) * 512],
                            PS[bk][:, 0:512]),
                               reads=(b_PS[obk], b_TOK2), writes=(b_TOK2,))

            pipeline_b(0)
            pv_recip(6)
            pipeline_b(1)
            pv_bcast_evac(atn[0][:, :], b_atn[0], 6, 2 + (NT % 4))
            pv_recip(7)
            o_proj_b(0)
            pv_bcast_evac(atn[1][:, :], b_atn[1], 7, 2 + (NT % 4))
            o_proj_b(1)
        if pr + 1 < NPAIR:
            load_wo(pr + 1)
    b1.close()
    k.barrier()

    ga2 = contextlib.ExitStack()
    g2r, b_g2r = mk(ga2, "g2r", [128, D], side="right")
    b2lr, b_b2lr = mk(ga2, "b2lr", [128, D], side="right")
    b1h, b_b1h = mk(ga2, "b1h", [128, 32], side="right")
    ga = contextlib.ExitStack()
    g1r, b_g1r = mk(ga, "g1r", [128, D], side="right")
    b1r, b_b1r = mk(ga, "b1r", [128, D], side="right")
    for t, d, b in ((g1r, g1r_d, b_g1r), (b1r, b1r_d, b_b1r),
                    (g2r, g2r_d, b_g2r), (b2lr, b2lr_d, b_b2lr),
                    (b1h, b1h_d, b_b1h)):
        k.emit("sd", lambda e, t=t, d=d: e.dma_start(out=t[:, :], in_=d),
               reads=(b,), writes=(b,), kind="d")
    k.barrier()
    layernorm(t2tile, b_TOK2, t2tile, b_TOK2, lnt=LNT, b_lnt=b_LNT,
              gain=g1r, b_gain=b_g1r, bias=b1r, b_bias=b_b1r)
    ga.close()
    k.barrier()

    m1 = contextlib.ExitStack()
    w1c, b_w1c = [None, None], [None, None]
    w2c, b_w2c = [None, None], [None, None]
    for i in range(2):
        w1c[i], b_w1c[i] = mk(m1, f"w1c{i}", [128, NF * 256], side="right")
        w2c[i], b_w2c[i] = mk(m1, f"w2c{i}", [128, 2 * D], side="right")
    hT, b_hT = mk(m1, "hT", [128, 2 * T], side="right")
    rows, b_rows = mk(m1, "rowsm", [128, D], side="right")
    k.emit("sd", lambda e: e.dma_start(out=rows[32:33, :], in_=b2r_d),
           reads=(b_rows,), writes=(b_rows,), kind="d")
    NHC = DFF // 256
    for hc in range(NHC):
        sl = hc % 2
        k.emit("sd", lambda e, hc=hc, sl=sl: e.dma_start(
            out=w1c[sl][:, :],
            in_=bass.AP(w1.tensor, hc * 256, [[DFF, 128], [128 * DFF, NF], [1, 256]])),
               reads=(b_w1c[sl],), writes=(b_w1c[sl],), kind="d", dstream=sl)
        k.emit("sd", lambda e, hc=hc, sl=sl: e.dma_start(
            out=w2c[sl][:, :],
            in_=bass.AP(w2.tensor, hc * 256 * D, [[D, 128], [128 * D, 2], [1, D]])),
               reads=(b_w2c[sl],), writes=(b_w2c[sl],), kind="d", dstream=sl)
        for ht in range(2):
            for c in range(4):
                bank = c % 2
                for f in range(NF):
                    k.emit("pe", lambda e, f=f, c=c, ht=ht, sl=sl, bk=bank: e.matmul(
                        PS[bk][:, 0:512],
                        w1c[sl][:, f * 256 + ht * 128:f * 256 + (ht + 1) * 128],
                        lnt_mv(f, c * 512, 512), start=(f == 0), stop=(f == NF - 1)),
                           reads=(b_w1c[sl], b_LNT), writes=(b_PS[bank],))
                hcol = hc * 2 + ht
                k.emit("act", lambda e, c=c, ht=ht, hcol=hcol, bk=bank: e.activation(
                    hT[:, ht * T + c * 512:ht * T + (c + 1) * 512], PS[bk][:, 0:512],
                    AF.Relu, bias=b1h[:, hcol:hcol + 1]),
                       reads=(b_PS[bank], b_b1h), writes=(b_hT,))
        for tt in range(NT):
            bank = 2 + (tt % 4)
            for oc in range(2):
                for ht in range(2):
                    lastmm = (ht == 1)
                    k.emit("pe", lambda e, tt=tt, oc=oc, ht=ht, sl=sl, bk=bank,
                           lm=lastmm, lh=(hc == NHC - 1): e.matmul(
                        PS[bk][:, 0:512],
                        hT[:, ht * T + tt * 128:ht * T + (tt + 1) * 128],
                        w2c[sl][:, ht * D + oc * 512:ht * D + (oc + 1) * 512],
                        start=(ht == 0), stop=(lm and not lh)),
                           reads=(b_hT, b_w2c[sl]), writes=(b_PS[bank],))
                if hc == NHC - 1:
                    rank1(bank, 32, oc * 512, 512, False, True, b_rows, rows)
                k.emit("dve", lambda e, tt=tt, oc=oc, bk=bank: e.tensor_add(
                    TOK2[:, tt * D + oc * 512:tt * D + (oc + 1) * 512],
                    TOK2[:, tt * D + oc * 512:tt * D + (oc + 1) * 512],
                    PS[bk][:, 0:512]),
                       reads=(b_PS[bank], b_TOK2), writes=(b_TOK2,))
    m1.close()
    k.barrier()

    layernorm(t2tile, b_TOK2, t2tile, b_TOK2,
              gain=g2r, b_gain=b_g2r, bias=b2lr, b_bias=b_b2lr)
    ga2.close()
    k.barrier()

    k.emit("gp", lambda e: e.dma_start(out=xsp_out, in_=TOK2[:, :],
                                       accum_op=ALU.add),
           reads=(b_TOK2,), kind="d")
    k.barrier()
    x1stk.close()
    X2, b_X2 = mk(xs, f"X2_{s}", [128, NT * D])
    xsp_in = bass.AP(xspill, 0, [[D, 128], [128 * D, NT], [1, D]])
    k.emit("sd", lambda e: e.dma_start(out=X2[:, :], in_=xsp_in),
           writes=(b_X2,), kind="d", dstream=2)
    k.barrier()
    return X2, b_X2




def _host_constants():
    c = {}
    c["identc"] = np.eye(128, dtype=np.float32)
    perm64 = np.array([2 * j for j in range(32)] + [2 * j + 1 for j in range(32)])
    perm2 = np.concatenate([perm64, perm64 + 64])
    p2 = np.zeros((128, 128), np.float32)
    p2[perm2, np.arange(128)] = 1.0
    c["p2c"] = p2
    c["onesc"] = np.ones((128, 128), np.float32)
    u = np.arange(1152)[None, :]
    p = np.arange(128)[:, None]
    c["cmc"] = np.where(u < p + 512, np.float32(NEGM), np.float32(0.0)).astype(np.float32)
    j = np.arange(512)[None, :]
    c["sm0c"] = (j - p).astype(np.float32)
    i = np.arange(32, dtype=np.float64)
    inv = 10000.0 ** (2.0 * i / 64.0)
    ang = np.arange(T, dtype=np.float64)[None, :] / inv[:, None]
    cos = np.cos(ang).astype(np.float32)
    sin = np.sin(ang).astype(np.float32)
    c["c64c"] = np.concatenate([cos, cos, cos, cos], axis=0)
    c["s64c"] = np.concatenate([-sin, sin, -sin, sin], axis=0)
    return c


def kernel(L0_attn_a_x, L0_attn_a_W_qkv, L0_attn_a_W_o,
           L0_attn_b_Wq, L0_attn_b_bq, L0_attn_b_Wk, L0_attn_b_bk,
           L0_attn_b_Wv, L0_attn_b_bv, L0_attn_b_Wo, L0_attn_b_bo,
           L0_attn_b_dist_scale, L0_attn_b_W1, L0_attn_b_b1,
           L0_attn_b_W2, L0_attn_b_b2, L0_attn_b_ln1_g, L0_attn_b_ln1_b,
           L0_attn_b_ln2_g, L0_attn_b_ln2_b, L0_ffn_CB, L0_ffn_SCB, L0_ffn_bias):
    f32 = np.float32
    asn = lambda a: np.ascontiguousarray(np.asarray(a, dtype=f32))
    x = asn(L0_attn_a_x)
    rep = lambda v: np.ascontiguousarray(
        np.broadcast_to(asn(v).reshape(1, D), (128, D)))
    shared = {
        "wqkv": asn(L0_attn_a_W_qkv), "wo_a": asn(L0_attn_a_W_o),
        "wq_b": asn(L0_attn_b_Wq), "wk_b": asn(L0_attn_b_Wk),
        "wv_b": asn(L0_attn_b_Wv), "wo_b": asn(L0_attn_b_Wo),
        "w1": asn(L0_attn_b_W1), "w2": asn(L0_attn_b_W2), "cb": asn(L0_ffn_CB),
        "bqhc": np.ascontiguousarray(asn(L0_attn_b_bq).reshape(8, 128).T),
        "bkhc": np.ascontiguousarray(asn(L0_attn_b_bk).reshape(8, 128).T),
        "bvfc": np.ascontiguousarray(asn(L0_attn_b_bv).reshape(16, 64).T),
        "b1hc": np.ascontiguousarray(asn(L0_attn_b_b1).reshape(32, 128).T),
        "b2rc": asn(L0_attn_b_b2).reshape(1, D),
        "borc": asn(L0_attn_b_bo).reshape(1, D),
        "fbrc": asn(L0_ffn_bias).reshape(1, D),
        "scbc": rep(L0_ffn_SCB),
        "g1rc": rep(L0_attn_b_ln1_g), "b1rc": rep(L0_attn_b_ln1_b),
        "g2rc": rep(L0_attn_b_ln2_g), "b2lrc": rep(L0_attn_b_ln2_b),
        "dscc": asn(L0_attn_b_dist_scale).reshape(1, 1),
        "epsc": np.full((128, 1), EPS, dtype=f32),
        "dbtc": np.ascontiguousarray(np.broadcast_to(
            (np.arange(28, dtype=f32) * 128 - 1920)[None, :], (128, 28))),
    }
    shared.update(_host_constants())
    n = 8
    in_maps = []
    for c in range(n):
        im = dict(shared)
        im["x"] = np.ascontiguousarray(x[c * B_LOC:(c + 1) * B_LOC])
        in_maps.append(im)
    nc, _ = build_program(STAGE)
    res = run_bass_kernel_spmd(nc, in_maps, core_ids=list(range(n)))
    return np.concatenate([r["y"] for r in res.results], axis=0)


# version 09
# speedup vs baseline: 1.3085x; 1.0167x over previous
import contextlib
import numpy as np
import concourse.bass as bass
import concourse.mybir as mybir
from concourse.bass_utils import run_bass_kernel_spmd

F32 = mybir.dt.float32
F16 = mybir.dt.float16
AF = mybir.ActivationFunctionType
ALU = mybir.AluOpType
AX = mybir.AxisListType

B_LOC, T, D, H, DH, DFF = 2, 2048, 1024, 16, 64, 4096
NT, NF = T // 128, D // 128
NPAIR = H // 2
EPS = 1e-5
NEGM = -1.0e9
STAGE = 4
PAIR_LIMIT = None
DBG = False



class Buf:
    __slots__ = ("ws", "rs", "name")

    def __init__(self, name=""):
        self.ws = []
        self.rs = []
        self.name = name


class Op:
    __slots__ = ("chan", "fn", "waits", "kind", "used", "val", "fwaits", "dstream")

    def __init__(self, chan, fn, waits, kind, dstream=0):
        self.chan = chan
        self.fn = fn
        self.waits = waits
        self.kind = kind
        self.used = False
        self.val = 0
        self.fwaits = ()
        self.dstream = dstream


NDSTREAM = 5


class Chan:
    def __init__(self, name):
        self.name = name
        self.last = None
        self.sem_c = None
        self.sem_d = [None] * NDSTREAM
        self.pending = []


class K:
    def __init__(self, nc):
        self.nc = nc
        self.ops = []
        self.chans = {n: Chan(n) for n in ("pe", "dve", "act", "gp", "sd")}

    def emit(self, chan, fn, reads=(), writes=(), kind="c", dstream=0,
             extra=()):
        ch = self.chans[chan]
        waits, seen = [], set()

        def add(op):
            if op is not None and id(op) not in seen:
                seen.add(id(op))
                waits.append(op)

        for o in ch.pending:
            add(o)
        ch.pending = []
        for o in extra:
            add(o)
        for b in reads:
            for w in b.ws:
                add(w)
        for b in writes:
            for r in b.rs:
                add(r)
            for w in b.ws:
                add(w)
        op = Op(ch, fn, waits, kind, dstream)
        self.ops.append(op)
        ch.last = op
        for b in reads:
            b.rs.append(op)
        for b in writes:
            b.ws = [op]
            b.rs = []
        return op

    def barrier(self):
        lasts = [c.last for c in self.chans.values() if c.last is not None]
        for ch in self.chans.values():
            for o in lasts:
                if o.chan is not ch and o not in ch.pending:
                    ch.pending.append(o)

    def finish(self):
        lasts = [c.last for c in self.chans.values() if c.last is not None]
        ldma = {}
        for op in self.ops:
            if op.kind == "d":
                ldma[(op.chan.name, op.dstream)] = op
        ch = self.chans["sd"]
        waits = [o for o in lasts if o.chan is not ch] + list(ldma.values())
        op = Op(ch, None, waits, "c")
        self.ops.append(op)

    def finalize(self):
        for op in self.ops:
            for w in op.waits:
                w.used = True
        for ch in self.chans.values():
            ch._cc = 0
            ch._cd = [0] * NDSTREAM
            ch._wmax = {}
        cd_snap = []
        run = {(n, s): 0 for n in self.chans for s in range(NDSTREAM)}
        for op in self.ops:
            cd_snap.append(dict(run))
            if op.kind == "d":
                run[(op.chan.name, op.dstream)] += 16
                op.chan._cd[op.dstream] += 16
                op.val = op.chan._cd[op.dstream]
            elif op.used:
                op.chan._cc += 1
                op.val = op.chan._cc
        for i, op in enumerate(self.ops):
            fw = []
            for w in op.waits:
                if w.chan is op.chan and w.kind == "c":
                    continue
                if w.kind == "d":
                    sem = w.chan.sem_d[w.dstream]
                    val = cd_snap[i][(w.chan.name, w.dstream)]
                    if val < w.val:
                        val = w.val
                else:
                    sem = w.chan.sem_c
                    val = w.val
                key = id(sem)
                if op.chan._wmax.get(key, -1) >= val:
                    continue
                op.chan._wmax[key] = val
                fw.append((sem, val))
            op.fwaits = fw

    def run(self, eng, name):
        ch = self.chans[name]
        for op in self.ops:
            if op.chan is not ch:
                continue
            for sem, val in op.fwaits:
                eng.wait_ge(sem, val)
            if op.fn is not None:
                ins = op.fn(eng)
                if ins is not None:
                    if op.kind == "d":
                        ins.then_inc(ch.sem_d[op.dstream], 16)
                    elif op.used:
                        ins.then_inc(ch.sem_c, 1)





def build_program(stage=STAGE):
    nc = bass.Bass(trn_type="TRN2", target_bir_lowering=False,
                   detect_race_conditions=False, dynamic_dma_scratch_size=8192)
    k = K(nc)

    def din(name, shape):
        return nc.dram_tensor(name, shape, F32, kind="ExternalInput").ap()

    x_in = din("x", [B_LOC, T, D])
    wqkv = din("wqkv", [3 * D, D])
    wo_a = din("wo_a", [D, D])
    wq_b, wk_b, wv_b, wo_b = (din(n, [D, D]) for n in ("wq_b", "wk_b", "wv_b", "wo_b"))
    w1 = din("w1", [D, DFF])
    w2 = din("w2", [DFF, D])
    cb = din("cb", [D, D])
    ident_d = din("identc", [128, 128])
    p2_d = din("p2c", [128, 128])
    ones_d = din("onesc", [128, 128])
    cm_d = din("cmc", [128, 1152])
    sm0_d = din("sm0c", [128, 512])
    c64_d = din("c64c", [128, T])
    s64_d = din("s64c", [128, T])
    bqh_d = din("bqhc", [128, 8])
    bkh_d = din("bkhc", [128, 8])
    bvf_d = din("bvfc", [64, 16])
    b1h_d = din("b1hc", [128, 32])
    b2r_d = din("b2rc", [1, D])
    bor_d = din("borc", [1, D])
    fbr_d = din("fbrc", [1, D])
    scb_d = din("scbc", [128, D])
    g1r_d = din("g1rc", [128, D])
    b1r_d = din("b1rc", [128, D])
    g2r_d = din("g2rc", [128, D])
    b2lr_d = din("b2lrc", [128, D])
    dsc_d = din("dscc", [1, 1])
    eps_d = din("epsc", [128, 1])
    dbt_d = din("dbtc", [128, 28])

    y_out = nc.dram_tensor("y", [B_LOC, T, D], F32, kind="ExternalOutput").ap()
    dbg_out = nc.dram_tensor("dbg", [128, 4 * 512], F32, kind="ExternalOutput").ap() if DBG \
        else None
    dbg2_out = nc.dram_tensor("dbg2", [64, 2 * 32 * 512], F32,
                              kind="ExternalOutput").ap() if DBG else None
    dbg3_out = nc.dram_tensor("dbg3", [128, 8192], F32,
                              kind="ExternalOutput").ap() if DBG else None
    k.dbg3 = dbg3_out
    xspill = nc.dram_tensor("xspill", [T, D], F32, kind="Internal")

    gstack = contextlib.ExitStack()
    _uid = [0]

    def mk(stk, name, shape, dt=F32, side="left"):
        _uid[0] += 1
        h = stk.enter_context(nc.sbuf_tensor(f"{name}_u{_uid[0]}", shape, dt,
                                             side=side))
        return h, Buf(name)

    ident, b_ident = mk(gstack, "identc_s", [128, 128])
    ones, b_ones = mk(gstack, "onesc_s", [128, 128])
    smalls, b_smalls = mk(gstack, "smalls", [128, 64])
    stats, b_stats = mk(gstack, "stats", [128, 4 * NT])
    nrm, b_nrm = mk(gstack, "nrm", [1, 16])
    sqs, b_sqs = mk(gstack, "sqs", [128, 512])
    ptil, b_ptil = [None, None], [None, None]
    for i in range(2):
        ptil[i], b_ptil[i] = mk(gstack, f"ptil{i}", [128, 512])
    recbc, b_recbc = mk(gstack, "recbc", [128, 512])
    rcw, b_rcw = mk(gstack, "rcw", [128, 256])

    PS, b_PS = [], []
    for i in range(8):
        h = gstack.enter_context(nc.psum_tensor(f"ps{i}", [128, 512], F32))
        PS.append(h)
        b_PS.append(Buf(f"ps{i}"))

    def ld(dst, src, b):
        k.emit("sd", lambda e, d=dst, s=src: e.dma_start(out=d, in_=s),
               reads=(b,), writes=(b,), kind="d")

    ld(ident[:, :], ident_d, b_ident)
    ld(ones[:, :], ones_d, b_ones)
    ld(smalls[:, 0:8], bqh_d, b_smalls)
    ld(smalls[:, 8:16], bkh_d, b_smalls)
    ld(smalls[0:64, 16:32], bvf_d, b_smalls)
    ld(smalls[0:1, 33:34], dsc_d, b_smalls)
    ld(smalls[:, 35:36], eps_d, b_smalls)
    k.emit("dve", lambda e: e.memset(rcw[:, 0:128], 1.0), writes=(b_rcw,))
    k.emit("pe", lambda e: e.matmul(PS[7][:, 0:1], ones[0:1, 0:128],
                                    smalls[0:1, 33:34], start=True, stop=True),
           reads=(b_ones, b_smalls), writes=(b_PS[7],))
    k.emit("dve", lambda e: e.tensor_scalar_mul(smalls[:, 32:33], PS[7][:, 0:1], -1.0),
           reads=(b_PS[7],), writes=(b_smalls,))
    k.barrier()

    def layernorm(xt, b_xt, out_tiles, b_out, lnt=None, b_lnt=None,
                  gain=None, b_gain=None, bias=None, b_bias=None):
        for tt in range(NT):
            src = xt(tt)
            k.emit("dve", lambda e, s=src, c=tt: e.reduce_sum(
                stats[:, c:c + 1], s, axis=AX.X), reads=(b_xt,), writes=(b_stats,))
            k.emit("act", lambda e, s=src, c=tt: e.activation(
                PS[7][:, 0:512], s[:, 0:512], AF.Square,
                accum_out=stats[:, NT + c:NT + c + 1]),
                   reads=(b_xt,), writes=(b_PS[7], b_stats))
            k.emit("act", lambda e, s=src, c=tt: e.activation(
                PS[7][:, 0:512], s[:, 512:1024], AF.Square,
                accum_out=stats[:, 2 * NT + c:2 * NT + c + 1]),
                   reads=(b_xt,), writes=(b_PS[7], b_stats))
        k.emit("dve", lambda e: e.tensor_add(stats[:, NT:2 * NT], stats[:, NT:2 * NT],
                                             stats[:, 2 * NT:3 * NT]),
               reads=(b_stats,), writes=(b_stats,))
        k.emit("gp", lambda e: e.tensor_scalar_mul(stats[:, 0:NT], stats[:, 0:NT],
                                                   1.0 / D),
               reads=(b_stats,), writes=(b_stats,))
        k.emit("dve", lambda e: e.tensor_scalar_mul(stats[:, NT:2 * NT],
                                                    stats[:, NT:2 * NT], 1.0 / D),
               reads=(b_stats,), writes=(b_stats,))
        k.emit("gp", lambda e: e.tensor_mul(stats[:, 3 * NT:4 * NT],
                                            stats[:, 0:NT], stats[:, 0:NT]),
               reads=(b_stats,), writes=(b_stats,))
        k.emit("dve", lambda e: e.tensor_sub(stats[:, NT:2 * NT],
                                             stats[:, NT:2 * NT],
                                             stats[:, 3 * NT:4 * NT]),
               reads=(b_stats,), writes=(b_stats,))
        k.emit("gp", lambda e: e.tensor_scalar_add(rcw[:, 0:NT],
                                                    stats[:, NT:2 * NT], EPS),
               reads=(b_stats, b_rcw), writes=(b_rcw,))
        k.emit("act", lambda e: e.activation(rcw[:, 128:256], rcw[:, 0:128], AF.Ln),
               reads=(b_rcw,), writes=(b_rcw,))
        k.emit("act", lambda e: e.activation(rcw[:, 128:256], rcw[:, 128:256],
                                             AF.Exp, scale=-0.5),
               reads=(b_rcw,), writes=(b_rcw,))
        for _nr in range(2):
            k.emit("dve", lambda e: e.tensor_mul(sqs[:, 0:128], rcw[:, 128:256],
                                                 rcw[:, 128:256]),
                   reads=(b_rcw,), writes=(b_sqs,))
            k.emit("dve", lambda e: e.tensor_mul(sqs[:, 0:128], sqs[:, 0:128],
                                                 rcw[:, 0:128]),
                   reads=(b_rcw, b_sqs), writes=(b_sqs,))
            k.emit("dve", lambda e: e.tensor_scalar(sqs[:, 0:128], sqs[:, 0:128],
                                                    scalar1=-0.5, scalar2=1.5,
                                                    op0=ALU.mult, op1=ALU.add),
                   reads=(b_sqs,), writes=(b_sqs,))
            k.emit("dve", lambda e: e.tensor_mul(rcw[:, 128:256], rcw[:, 128:256],
                                                 sqs[:, 0:128]),
                   reads=(b_sqs, b_rcw), writes=(b_rcw,))
        for tt in range(NT):
            src = xt(tt)
            for half in range(2):
                if out_tiles is not None:
                    dsub = out_tiles(tt)[:, half * 512:(half + 1) * 512]
                    b_d = b_out
                else:
                    dsub = ptil[half][:, :]
                    b_d = b_ptil[half]
                k.emit("dve", lambda e, s=src, d=dsub, c=tt, hh=half: e.tensor_scalar(
                    d, s[:, hh * 512:(hh + 1) * 512], scalar1=stats[:, c:c + 1],
                    scalar2=rcw[:, 128 + c:129 + c],
                    op0=ALU.subtract, op1=ALU.mult),
                       reads=(b_xt, b_stats, b_rcw), writes=(b_d,))
                if gain is not None:
                    k.emit("dve", lambda e, d=dsub, hh=half: e.tensor_mul(
                        d, d, gain[:, hh * 512:(hh + 1) * 512]),
                           reads=(b_d, b_gain), writes=(b_d,))
                    k.emit("dve", lambda e, d=dsub, hh=half: e.tensor_add(
                        d, d, bias[:, hh * 512:(hh + 1) * 512]),
                           reads=(b_d, b_bias), writes=(b_d,))
            if lnt is None:
                continue
            for half in range(2):
                bank = half
                if out_tiles is not None:
                    srcs = [out_tiles(tt)[:, (half * 4 + j) * 128:
                                          (half * 4 + j + 1) * 128] for j in range(4)]
                    b_s = b_out
                else:
                    srcs = [ptil[half][:, j * 128:(j + 1) * 128] for j in range(4)]
                    b_s = b_ptil[half]
                for j in range(4):
                    k.emit("pe", lambda e, s=srcs[j], bk=bank, j=j: e.transpose(
                        PS[bk][:, j * 128:(j + 1) * 128], s, ident[:, :]),
                           reads=(b_s, b_ident), writes=(b_PS[bank],))
                out_ap = bass.AP(lnt, (half * 4) * T + tt * 128,
                                 [[NF * T, 128], [T, 4], [1, 128]])
                k.emit("dve", lambda e, o=out_ap, bk=bank: e.tensor_copy(
                    o, PS[bk][:, 0:512]),
                       reads=(b_PS[bank],), writes=(b_lnt,))

    def rank1(bank, row_base, col0, w, start, stop, b_rowsrc, rowsrc):
        src = rowsrc
        b_src = b_rowsrc
        k.emit("pe", lambda e: e.matmul(
            PS[bank][:, 0:w], ones[row_base:row_base + 1, 0:128],
            src[row_base:row_base + 1, col0:col0 + w], start=start, stop=stop),
               reads=(b_ones, b_src), writes=(b_PS[bank],))

    def bound_bcast(qsb, b_qsb, qbase, ksb, b_ksb, kcol):
        k.emit("act", lambda e: e.activation(sqs[qbase:qbase + 64, :],
                                             qsb[qbase:qbase + 64, 0:512], AF.Square),
               reads=(b_qsb,), writes=(b_sqs,))
        k.emit("pe", lambda e: e.matmul(PS[5][0:1, 0:512],
                                        ones[qbase:qbase + 64, 0:1],
                                        sqs[qbase:qbase + 64, :],
                                        start=True, stop=True),
               reads=(b_sqs, b_ones), writes=(b_PS[5],))
        k.emit("dve", lambda e: e.reduce_max(nrm[0:1, 10:11], PS[5][0:1, 0:512],
                                             axis=AX.X),
               reads=(b_PS[5],), writes=(b_nrm,))
        k.emit("dve", lambda e, kc=kcol: e.tensor_add(
            nrm[0:1, 11:12], nrm[0:1, 10:11], nrm[0:1, kc:kc + 1]),
               reads=(b_nrm,), writes=(b_nrm,))
        k.emit("pe", lambda e: e.matmul(PS[5][:, 0:1], ones[0:1, 0:128],
                                        nrm[0:1, 11:12], start=True, stop=True),
               reads=(b_ones, b_nrm), writes=(b_PS[5],))
        k.emit("dve", lambda e: e.tensor_scalar_mul(smalls[:, 34:35],
                                                    PS[5][:, 0:1], -1.0 / 16.0),
               reads=(b_PS[5],), writes=(b_smalls,))

    def knorm_pass(ksb, b_ksb, cols_a, cols_b):
        for c in range(4):
            k.emit("act", lambda e, c=c: e.activation(
                sqs[:, :], ksb[:, c * 512:(c + 1) * 512], AF.Square),
                   reads=(b_ksb,), writes=(b_sqs,))
            for base, col in ((0, c), (64, 4 + c)):
                k.emit("pe", lambda e, b=base: e.matmul(
                    PS[5][0:1, 0:512], ones[b:b + 64, 0:1],
                    sqs[b:b + 64, :], start=True, stop=True),
                       reads=(b_sqs, b_ones), writes=(b_PS[5],))
                k.emit("dve", lambda e, col=col: e.reduce_max(
                    nrm[0:1, col:col + 1], PS[5][0:1, 0:512], axis=AX.X),
                       reads=(b_PS[5],), writes=(b_nrm,))
        k.emit("dve", lambda e, ca=cols_a: e.reduce_max(
            nrm[0:1, ca:ca + 1], nrm[0:1, 0:4], axis=AX.X),
               reads=(b_nrm,), writes=(b_nrm,))
        k.emit("dve", lambda e, cb=cols_b: e.reduce_max(
            nrm[0:1, cb:cb + 1], nrm[0:1, 4:8], axis=AX.X),
               reads=(b_nrm,), writes=(b_nrm,))

    def pv_recip(pvb):
        k.emit("act", lambda e: e.activation(recbc[64:65, 0:512],
                                             PS[pvb][64:65, 0:512], AF.Ln),
               reads=(b_PS[pvb],), writes=(b_recbc,))
        k.emit("act", lambda e: e.activation(recbc[64:65, 0:512],
                                             recbc[64:65, 0:512],
                                             AF.Exp, scale=-1.0),
               reads=(b_recbc,), writes=(b_recbc,))
        k.emit("dve", lambda e: e.tensor_mul(sqs[64:65, 0:512],
                                             recbc[64:65, 0:512],
                                             PS[pvb][64:65, 0:512]),
               reads=(b_recbc, b_PS[pvb]), writes=(b_sqs,))
        k.emit("gp", lambda e: e.tensor_scalar(sqs[64:65, 0:512],
                                               sqs[64:65, 0:512],
                                               scalar1=-1.0, scalar2=2.0,
                                               op0=ALU.mult, op1=ALU.add),
               reads=(b_sqs,), writes=(b_sqs,))
        k.emit("dve", lambda e: e.tensor_mul(recbc[64:65, 0:512],
                                             recbc[64:65, 0:512],
                                             sqs[64:65, 0:512]),
               reads=(b_sqs, b_recbc), writes=(b_recbc,))

    def pv_bcast_evac(atn_ap, b_atn, pvb, bcb):
        k.emit("pe", lambda e: e.matmul(PS[bcb][0:64, 0:512], ones[64:65, 0:64],
                                        recbc[64:65, 0:512], start=True, stop=True),
               reads=(b_ones, b_recbc), writes=(b_PS[bcb],))
        k.emit("dve", lambda e: e.tensor_copy(recbc[0:64, 0:512], PS[bcb][0:64, 0:512]),
               reads=(b_PS[bcb],), writes=(b_recbc,))
        k.emit("dve", lambda e: e.tensor_mul(atn_ap, PS[pvb][0:64, 0:512],
                                             recbc[0:64, 0:512]),
               reads=(b_PS[pvb], b_recbc), writes=(b_atn,))

    def pv_norm_evac(atn_ap, b_atn, pvb, bcb):
        pv_recip(pvb)
        pv_bcast_evac(atn_ap, b_atn, pvb, bcb)

    def do_seq(s):
        xs = contextlib.ExitStack()
        x1stk = contextlib.ExitStack()
        X, b_X = mk(x1stk, f"X{s}", [128, NT * D], side="right")

        def xtile(tt, X=X):
            return X[:, tt * D:(tt + 1) * D]

        xv = bass.AP(x_in.tensor, s * T * D, [[D, 128], [128 * D, NT], [1, D]])
        k.emit("sd", lambda e, xv=xv, X=X: e.dma_start(out=X[:, :], in_=xv),
               writes=(b_X,), kind="d", dstream=2)
        k.barrier()

        LNT, b_LNT = mk(xs, f"LNT{s}", [128, NF * T])

        def lnt_mv(f, c0, w, LNT=LNT):
            return LNT[:, f * T + c0:f * T + c0 + w]

        layernorm(xtile, b_X, None, None, lnt=LNT, b_lnt=b_LNT)
        k.barrier()
        a1 = contextlib.ExitStack()
        p2m, b_p2m = mk(a1, "p2m", [128, 128], side="right")
        cm, b_cm = mk(a1, "cm", [128, 896], side="right")
        rtc, b_rtc = zip(*[mk(a1, f"rtc{i}", [128, 512], side="right") for i in range(2)])
        rts, b_rts = zip(*[mk(a1, f"rts{i}", [128, 512], side="right") for i in range(2)])
        khl, b_khl = zip(*[mk(a1, f"khl{i}", [128, T], dt=F16, side="right")
                           for i in range(2)])
        vp, b_vp = mk(a1, "vp", [128, NT * 130], side="right")
        qs16, b_qs16 = mk(a1, "qs16", [128, 512], dt=F16, side="right")
        qsx16, b_qsx16 = mk(a1, "qsx16", [128, 512], dt=F16, side="right")
        qraw, b_qraw = zip(*[mk(a1, f"qraw{i}", [128, 512], side="right") for i in range(2)])
        qswp, b_qswp = zip(*[mk(a1, f"qswp{i}", [128, 512], side="right") for i in range(2)])
        atn, b_atn = zip(*[mk(a1, f"atn{i}", [64, 512], side="right") for i in range(2)])
        wst, b_wst = zip(*[mk(a1, f"wst{i}", [128, D], side="right") for i in range(2)])
        wqkT, b_wqkT = mk(a1, "wqkT", [128, NF * 256], side="right")
        wvT, b_wvT = mk(a1, "wvT", [128, NF * 128], side="right")
        woTa, b_woTa = mk(a1, "woTa", [64, D], side="right")
        woTb, b_woTb = mk(a1, "woTb", [64, D], side="right")

        ld(p2m[:, :], p2_d, b_p2m)
        ld(cm[:, :], cm_d[:, 128:1024], b_cm)
        for tt in range(NT):
            k.emit("gp", lambda e, c=tt: e.memset(
                vp[:, c * 130 + 64:c * 130 + 65], 1.0), writes=(b_vp,))
            k.emit("gp", lambda e, c=tt: e.memset(
                vp[:, c * 130 + 129:c * 130 + 130], 1.0), writes=(b_vp,))

        ruse = [0]
        rlast = [None]

        def rope_chunk(dst_ap, b_dst, bank, c, wsi=None):
            ri = ruse[0] % 2
            ruse[0] += 1
            wi = ri if wsi is None else wsi
            qr, b_qr = qraw[wi], b_qraw[wi]
            qs, b_qs = qswp[wi], b_qswp[wi]
            k.emit("dve", lambda e, bk=bank, qr=qr: e.tensor_copy(
                qr[:, :], PS[bk][:, 0:512]),
                   reads=(b_PS[bank],), writes=(b_qr,))
            for sl in range(4):
                srcl = [32, 0, 96, 64][sl]
                k.emit("sd", lambda e, sl=sl, srcl=srcl, qr=qr, qs=qs: e.dma_start(
                    out=qs[sl * 32:(sl + 1) * 32, :], in_=qr[srcl:srcl + 32, :]),
                       reads=(b_qr,), writes=(b_qs,), kind="d", dstream=4)
            k.emit("dve", lambda e, qr=qr, ri=ri: e.tensor_mul(
                qr[:, :], qr[:, :], rtc[ri][:, :]),
                   reads=(b_qr, b_rtc[ri]), writes=(b_qr,))
            k.emit("dve", lambda e, qs=qs, ri=ri: e.tensor_mul(
                qs[:, :], qs[:, :], rts[ri][:, :]),
                   reads=(b_qs, b_rts[ri]), writes=(b_qs,))
            if dst_ap is None:
                rlast[0] = k.emit("dve", lambda e, qr=qr, qs=qs: e.tensor_add(
                    qr[:, :], qr[:, :], qs[:, :]),
                       reads=(b_qr, b_qs), writes=(b_qr,))
            else:
                rlast[0] = k.emit("dve", lambda e, d=dst_ap, qr=qr, qs=qs:
                       e.tensor_add(d, qr[:, :], qs[:, :]),
                       reads=(b_qr, b_qs), writes=(b_dst,))

        def rope_tables(c):
            ri = ruse[0] % 2
            ex = (rlast[0],) if rlast[0] is not None else ()
            k.emit("sd", lambda e, c=c, ri=ri: e.dma_start(
                out=rtc[ri][:, :], in_=c64_d[:, c * 512:(c + 1) * 512]),
                   reads=(b_rtc[ri],), writes=(b_rtc[ri],), kind="d", dstream=3,
                   extra=ex)
            k.emit("sd", lambda e, c=c, ri=ri: e.dma_start(
                out=rts[ri][:, :], in_=s64_d[:, c * 512:(c + 1) * 512]),
                   reads=(b_rts[ri],), writes=(b_rts[ri],), kind="d", dstream=3,
                   extra=ex)

        nplim = PAIR_LIMIT if PAIR_LIMIT is not None else NPAIR
        wring = [0]

        for pr in range(nplim):
            def wload(off, p=pr):
                w = wring[0] % 2
                wring[0] += 1
                k.emit("sd", lambda e, p=p, off=off, w=w: e.dma_start(
                    out=wst[w][:, :],
                    in_=(wqkv[off + p * 128:off + (p + 1) * 128, :] if off >= 0 else
                         bass.AP(wo_a.tensor, p * 128,
                                 [[D, 128], [128 * D, 8], [1, 128]]))),
                       reads=(b_wst[w],), writes=(b_wst[w],), kind="d", dstream=w)
                return w
            w = wload(0)
            for f in range(NF):
                k.emit("pe", lambda e, f=f, w=w: e.transpose(
                    PS[0][:, 0:128], wst[w][:, f * 128:(f + 1) * 128], p2m[:, :]),
                       reads=(b_wst[w], b_p2m), writes=(b_PS[0],))
                k.emit("dve", lambda e, f=f: e.tensor_copy(
                    wqkT[:, f * 256:f * 256 + 128], PS[0][:, 0:128]),
                       reads=(b_PS[0],), writes=(b_wqkT,))
            w = wload(D)
            for f in range(NF):
                k.emit("pe", lambda e, f=f, w=w: e.transpose(
                    PS[1][:, 0:128], wst[w][:, f * 128:(f + 1) * 128], p2m[:, :]),
                       reads=(b_wst[w], b_p2m), writes=(b_PS[1],))
                k.emit("dve", lambda e, f=f: e.tensor_copy(
                    wqkT[:, f * 256 + 128:f * 256 + 256], PS[1][:, 0:128]),
                       reads=(b_PS[1],), writes=(b_wqkT,))
            w = wload(2 * D)
            for f in range(NF):
                k.emit("pe", lambda e, f=f, w=w: e.transpose(
                    PS[0][:, 0:128], wst[w][:, f * 128:(f + 1) * 128], ident[:, :]),
                       reads=(b_wst[w], b_ident), writes=(b_PS[0],))
                k.emit("dve", lambda e, f=f: e.tensor_copy(
                    wvT[:, f * 128:(f + 1) * 128], PS[0][:, 0:128]),
                       reads=(b_PS[0],), writes=(b_wvT,))
            w = wload(-1)
            for c in range(NF):
                k.emit("pe", lambda e, c=c, w=w: e.transpose(
                    PS[1][0:64, 0:128], wst[w][:, c * 128:c * 128 + 64], ident[:, :]),
                       reads=(b_wst[w], b_ident), writes=(b_PS[1],))
                k.emit("dve", lambda e, c=c: e.tensor_copy(
                    woTa[:, c * 128:(c + 1) * 128], PS[1][0:64, 0:128]),
                       reads=(b_PS[1],), writes=(b_woTa,))
                k.emit("pe", lambda e, c=c, w=w: e.transpose(
                    PS[1][0:64, 0:128], wst[w][:, c * 128 + 64:(c + 1) * 128],
                    ident[:, :]),
                       reads=(b_wst[w], b_ident), writes=(b_PS[1],))
                k.emit("dve", lambda e, c=c: e.tensor_copy(
                    woTb[:, c * 128:(c + 1) * 128], PS[1][0:64, 0:128]),
                       reads=(b_PS[1],), writes=(b_woTb,))

            for c in range(4):
                bank = c % 2
                rope_tables(c)
                for f in range(NF):
                    k.emit("pe", lambda e, f=f, c=c, bk=bank, w=w: e.matmul(
                        PS[bk][:, 0:512], wqkT[:, f * 256 + 128:f * 256 + 256],
                        lnt_mv(f, c * 512, 512),
                        start=(f == 0), stop=(f == NF - 1)),
                           reads=(b_wqkT, b_LNT), writes=(b_PS[bank],))
                rope_chunk(None, None, bank, c)
                t, b_t = qraw[c % 2], b_qraw[c % 2]
                cc0, cc1 = c * 512, (c + 1) * 512
                k.emit("dve", lambda e, t=t, cc0=cc0, cc1=cc1: e.tensor_copy(
                    khl[0][0:64, cc0:cc1], t[0:64, :]),
                       reads=(b_t,), writes=(b_khl[0],))
                k.emit("dve", lambda e, t=t, cc0=cc0, cc1=cc1: e.tensor_sub(
                    qsx16[0:64, :], t[0:64, :], khl[0][0:64, cc0:cc1]),
                       reads=(b_t, b_khl[0]), writes=(b_qsx16,))
                k.emit("sd", lambda e, cc0=cc0, cc1=cc1: e.dma_start(
                    out=khl[0][64:128, cc0:cc1], in_=qsx16[0:64, :]),
                       reads=(b_qsx16,), writes=(b_khl[0],), kind="d", dstream=4)
                k.emit("dve", lambda e, t=t, cc0=cc0, cc1=cc1: e.tensor_copy(
                    khl[1][64:128, cc0:cc1], t[64:128, :]),
                       reads=(b_t,), writes=(b_khl[1],))
                k.emit("dve", lambda e, t=t, cc0=cc0, cc1=cc1: e.tensor_sub(
                    qs16[64:128, :], t[64:128, :], khl[1][64:128, cc0:cc1]),
                       reads=(b_t, b_khl[1]), writes=(b_qs16,))
                k.emit("sd", lambda e, cc0=cc0, cc1=cc1: e.dma_start(
                    out=khl[1][0:64, cc0:cc1], in_=qs16[64:128, :]),
                       reads=(b_qs16,), writes=(b_khl[1],), kind="d", dstream=4)
            for c in range(4):
                bank = c % 2
                for f in range(NF):
                    k.emit("pe", lambda e, f=f, c=c, bk=bank, w=w: e.matmul(
                        PS[bk][:, 0:512], wvT[:, f * 128:(f + 1) * 128],
                        lnt_mv(f, c * 512, 512),
                        start=(f == 0), stop=(f == NF - 1)),
                           reads=(b_wvT, b_LNT), writes=(b_PS[bank],))
                vtb = 2 + (c % 4)
                k.emit("dve", lambda e, bk=bank, c=c: e.tensor_copy(
                    qraw[c % 2][:, :], PS[bk][:, 0:512]),
                       reads=(b_PS[bank],), writes=(b_qraw[c % 2],))
                for j in range(4):
                    tt = c * 4 + j
                    k.emit("pe", lambda e, j=j, c=c, vb=vtb: e.transpose(
                        PS[vb][:, 0:128], qraw[c % 2][:, j * 128:(j + 1) * 128],
                        ident[:, :]),
                           reads=(b_qraw[c % 2], b_ident), writes=(b_PS[vtb],))
                    k.emit("dve", lambda e, t=tt, vb=vtb: e.tensor_copy(
                        vp[:, t * 130:t * 130 + 64], PS[vb][:, 0:64]),
                           reads=(b_PS[vtb],), writes=(b_vp,))
                    k.emit("dve", lambda e, t=tt, vb=vtb: e.tensor_copy(
                        vp[:, t * 130 + 65:t * 130 + 129], PS[vb][:, 64:128]),
                           reads=(b_PS[vtb],), writes=(b_vp,))

            for qc in range(4):
                q0 = qc * 512
                qb = qc % 2
                bank = qb
                rope_tables(qc)
                for f in range(NF):
                    k.emit("pe", lambda e, f=f, qc=qc, bk=bank, w=w: e.matmul(
                        PS[bk][:, 0:512], wqkT[:, f * 256:f * 256 + 128],
                        lnt_mv(f, qc * 512, 512),
                        start=(f == 0), stop=(f == NF - 1)),
                           reads=(b_wqkT, b_LNT), writes=(b_PS[bank],))
                rope_chunk(None, None, bank, qc, wsi=1)
                ktmax = 4 * (qc + 1)

                def qbuild(side):
                    t, b_t = qraw[1], b_qraw[1]
                    a0, a1r = (0, 64) if side == 0 else (64, 128)
                    b0, b1r = (64, 128) if side == 0 else (0, 64)
                    k.emit("dve", lambda e, t=t, a0=a0, a1r=a1r: e.tensor_copy(
                        qs16[a0:a1r, :], t[a0:a1r, :]),
                           reads=(b_t,), writes=(b_qs16,))
                    k.emit("dve", lambda e, t=t, a0=a0, a1r=a1r: e.tensor_sub(
                        qsx16[a0:a1r, :], t[a0:a1r, :], qs16[a0:a1r, :]),
                           reads=(b_t, b_qs16), writes=(b_qsx16,))
                    k.emit("sd", lambda e, a0=a0, a1r=a1r, b0=b0, b1r=b1r:
                           e.dma_start(out=qs16[b0:b1r, :], in_=qsx16[a0:a1r, :]),
                           reads=(b_qsx16,), writes=(b_qs16,), kind="d", dstream=4)
                    k.emit("sd", lambda e, a0=a0, a1r=a1r, b0=b0, b1r=b1r:
                           e.dma_start(out=qsx16[b0:b1r, :], in_=qs16[a0:a1r, :]),
                           reads=(b_qs16,), writes=(b_qsx16,), kind="d", dstream=4)

                qbuild(0)

                prb = (ptil[0], ptil[1], qraw[0])
                b_prb = (b_ptil[0], b_ptil[1], b_qraw[0])

                def st_exp(side, kt, q0=q0):
                    kh, b_kh = khl[side], b_khl[side]
                    sb = 2 + (kt % 4)
                    k.emit("pe", lambda e, kt=kt, sb=sb, kh=kh: e.matmul(
                        PS[sb][:, 0:512], kh[:, kt * 128:(kt + 1) * 128],
                        qs16[:, :], start=True, stop=False),
                           reads=(b_kh, b_qs16), writes=(b_PS[sb],))
                    k.emit("pe", lambda e, kt=kt, sb=sb, kh=kh: e.matmul(
                        PS[sb][:, 0:512], kh[:, kt * 128:(kt + 1) * 128],
                        qsx16[:, :], start=False, stop=True),
                           reads=(b_kh, b_qsx16), writes=(b_PS[sb],))
                    dbase = q0 - kt * 128
                    if dbase < 127:
                        u0 = 384 + dbase
                        k.emit("dve", lambda e, sb=sb, u0=u0: e.tensor_add(
                            PS[sb][:, 0:512], PS[sb][:, 0:512], cm[:, u0:u0 + 512]),
                               reads=(b_cm, b_PS[sb]), writes=(b_PS[sb],))
                    pt, bpt = prb[kt % 3], b_prb[kt % 3]
                    k.emit("act", lambda e, sb=sb, pt=pt: e.activation(
                        pt[:, :], PS[sb][:, 0:512], AF.Exp, scale=0.125),
                           reads=(b_PS[sb],), writes=(bpt,))

                def pv(side, kt, ktmax=ktmax):
                    pvb = 6 + side
                    voff = 0 if side == 0 else 65
                    pt, bpt = prb[kt % 3], b_prb[kt % 3]
                    k.emit("pe", lambda e, kt=kt, pt=pt, vo=voff, pvb=pvb,
                           fr=(kt == 0), lastk=(kt == ktmax - 1): e.matmul(
                        PS[pvb][0:65, 0:512],
                        vp[:, kt * 130 + vo:kt * 130 + vo + 65],
                        pt[:, :], start=fr, stop=lastk),
                           reads=(b_vp, bpt), writes=(b_PS[pvb],))

                def pipeline(side, ktmax=ktmax, LOOK=2):
                    for kt in range(ktmax + LOOK):
                        if kt < ktmax:
                            st_exp(side, kt)
                        if kt >= LOOK:
                            pv(side, kt - LOOK)

                def o_proj(side, qc=qc):
                    woT = woTa if side == 0 else woTb
                    b_woT = b_woTa if side == 0 else b_woTb
                    for j in range(4):
                        tt = qc * 4 + j
                        obk = j % 2
                        for oc in range(2):
                            k.emit("pe", lambda e, j=j, oc=oc, bk=obk, woT=woT,
                                   side=side: e.matmul(
                                PS[bk][:, 0:512], atn[side][:, j * 128:(j + 1) * 128],
                                woT[:, oc * 512:(oc + 1) * 512],
                                start=True, stop=True),
                                   reads=(b_atn[side], b_woT), writes=(b_PS[obk],))
                            k.emit("dve", lambda e, tt=tt, oc=oc, bk=obk, X=X:
                                   e.tensor_add(
                                       X[:, tt * D + oc * 512:tt * D + (oc + 1) * 512],
                                       X[:, tt * D + oc * 512:tt * D + (oc + 1) * 512],
                                       PS[bk][:, 0:512]),
                                   reads=(b_PS[obk], b_X), writes=(b_X,))

                pipeline(0)
                pv_recip(6)
                qbuild(1)
                pipeline(1)
                pv_bcast_evac(atn[0][:, :], b_atn[0], 6, 2 + (ktmax % 4))
                pv_recip(7)
                o_proj(0)
                pv_bcast_evac(atn[1][:, :], b_atn[1], 7, 2 + (ktmax % 4))
                o_proj(1)
        a1.close()
        k.barrier()

        if stage >= 2:
            emit_ffn(k, nc, mk, xs, X, b_X, xtile, LNT, b_LNT, lnt_mv, layernorm,
                     PS, b_PS, cb, scb_d, ident, b_ident, fbr_d, rank1)
            k.barrier()

        if stage >= 3:
            X, b_X = emit_attn_b(
                k, nc, mk, xs, s, X, b_X, x1stk, xtile, LNT, b_LNT, lnt_mv,
                layernorm, PS, b_PS, ident, b_ident, ones, b_ones,
                smalls, b_smalls, b1h_d, b2r_d, sm0_d, dbt_d,
                wq_b, wk_b, wv_b, wo_b, w1, w2, bor_d,
                g1r_d, b1r_d, g2r_d, b2lr_d,
                ptil, b_ptil, recbc, b_recbc, sqs, b_sqs,
                nrm, b_nrm, xspill, rank1, knorm_pass, bound_bcast,
                pv_norm_evac, pv_recip, pv_bcast_evac, ld)

            def xtile(tt, X=X):
                return X[:, tt * D:(tt + 1) * D]
            k.barrier()

        if stage >= 4:
            emit_ffn(k, nc, mk, xs, X, b_X, xtile, LNT, b_LNT, lnt_mv, layernorm,
                     PS, b_PS, cb, scb_d, ident, b_ident, fbr_d, rank1)
            k.barrier()

        yv = bass.AP(y_out.tensor, s * T * D, [[D, 128], [128 * D, NT], [1, D]])
        k.emit("sd", lambda e, yv=yv, X=X: e.dma_start(out=yv, in_=X[:, :]),
               reads=(b_X,), kind="d", dstream=2)
        k.barrier()
        if stage < 3:
            x1stk.close()
        xs.close()

    for s in range(B_LOC):
        do_seq(s)

    k.finish()
    nops = len(k.ops)

    with contextlib.ExitStack() as semstk:
        for name, ch in k.chans.items():
            ch.sem_c = semstk.enter_context(nc.semaphore(f"sc_{name}"))
            ch.sem_d = [semstk.enter_context(nc.semaphore(f"sd_{name}{i}"))
                        for i in range(NDSTREAM)]
        k.finalize()
        with nc.Block() as block:
            @block.tensor
            def _(eng):
                k.run(eng, "pe")

            @block.vector
            def _(eng):
                k.run(eng, "dve")

            @block.scalar
            def _(eng):
                k.run(eng, "act")

            @block.gpsimd
            def _(eng):
                k.run(eng, "gp")

            @block.sync
            def _(eng):
                k.run(eng, "sd")

    return nc, nops


def emit_ffn(k, nc, mk, xs, X, b_X, xtile, LNT, b_LNT, lnt_mv, layernorm,
             PS, b_PS, cb, scb_d, ident, b_ident, fbr_d, rank1,
             dbgbufs=None):
    layernorm(xtile, b_X, None, None, lnt=LNT, b_lnt=b_LNT)
    if DBG and dbgbufs is not None:
        stats, b_stats, rcw, b_rcw, Xb = dbgbufs
        k.emit("gp", lambda e: e.dma_start(
            out=k.dbg3[0:128, 2048:2112], in_=stats[:, :]),
               reads=(b_stats,), kind="d")
        k.emit("gp", lambda e: e.dma_start(
            out=k.dbg3[0:128, 2112:2240], in_=rcw[:, 128:256]),
               reads=(b_rcw,), kind="d")
        k.emit("gp", lambda e: e.dma_start(
            out=k.dbg3[0:128, 6144:7168], in_=Xb[:, 3 * D:4 * D]),
               reads=(b_X,), kind="d")
        k.emit("gp", lambda e: e.dma_start(
            out=k.dbg3[0:128, 4096:5120], in_=LNT[:, 0:1024]),
               reads=(b_LNT,), kind="d")
        k.barrier()
    k.barrier()
    f1 = contextlib.ExitStack()
    cbt, b_cbt = mk(f1, "cbt", [128, NF * D], side="right")
    cst, b_cst = mk(f1, "cst", [128, D], side="right")
    scb, b_scb = mk(f1, "scb", [128, D], side="right")
    rows, b_rows = mk(f1, "rowsf", [128, D], side="right")
    k.emit("sd", lambda e: e.dma_start(out=rows[64:65, :], in_=fbr_d),
           reads=(b_rows,), writes=(b_rows,), kind="d")
    k.emit("sd", lambda e: e.dma_start(out=scb[:, :], in_=scb_d),
           reads=(b_scb,), writes=(b_scb,), kind="d")
    k.emit("dve", lambda e: e.tensor_scalar_mul(scb[:, :], scb[:, :], 1.0 / 127.0),
           reads=(b_scb,), writes=(b_scb,))
    for ntile in range(NF):
        k.emit("sd", lambda e, n=ntile: e.dma_start(
            out=cst[:, :], in_=cb[n * 128:(n + 1) * 128, :]),
               reads=(b_cst,), writes=(b_cst,), kind="d")
        for f in range(NF):
            bank = f % 2
            k.emit("pe", lambda e, f=f, bk=bank: e.transpose(
                PS[bk][:, 0:128], cst[:, f * 128:(f + 1) * 128], ident[:, :]),
                   reads=(b_cst, b_ident), writes=(b_PS[bank],))
            k.emit("dve", lambda e, f=f, n=ntile, bk=bank: e.tensor_mul(
                cbt[:, f * D + n * 128:f * D + (n + 1) * 128], PS[bk][:, 0:128],
                scb[:, n * 128:(n + 1) * 128]),
                   reads=(b_PS[bank], b_scb), writes=(b_cbt,))
    for tt in range(NT):
        for oc in range(2):
            bank = 2 + ((tt * 2 + oc) % 2)
            for f in range(NF):
                k.emit("pe", lambda e, tt=tt, oc=oc, f=f, bk=bank: e.matmul(
                    PS[bk][:, 0:512], lnt_mv(f, tt * 128, 128),
                    cbt[:, f * D + oc * 512:f * D + (oc + 1) * 512],
                    start=(f == 0), stop=False),
                       reads=(b_LNT, b_cbt), writes=(b_PS[bank],))
            rank1(bank, 64, oc * 512, 512, False, True, b_rows, rows)
            k.emit("dve", lambda e, tt=tt, oc=oc, bk=bank: e.tensor_add(
                X[:, tt * D + oc * 512:tt * D + (oc + 1) * 512],
                X[:, tt * D + oc * 512:tt * D + (oc + 1) * 512], PS[bk][:, 0:512]),
                   reads=(b_PS[bank], b_X), writes=(b_X,))
    f1.close()


def emit_attn_b(k, nc, mk, xs, s, X, b_X, x1stk, xtile, LNT, b_LNT, lnt_mv,
                layernorm, PS, b_PS, ident, b_ident, ones, b_ones,
                smalls, b_smalls, b1h_d, b2r_d, sm0_d, dbt_d,
                wq_b, wk_b, wv_b, wo_b, w1, w2, bor_d,
                g1r_d, b1r_d, g2r_d, b2lr_d,
                ptil, b_ptil, recbc, b_recbc, sqs, b_sqs,
                nrm, b_nrm, xspill, rank1, knorm_pass, bound_bcast,
                pv_norm_evac, pv_recip, pv_bcast_evac, ld):
    xsp_out = bass.AP(xspill, 0, [[D, 128], [128 * D, NT], [1, D]])
    k.emit("sd", lambda e: e.dma_start(out=xsp_out, in_=X[:, :]),
           reads=(b_X,), kind="d", dstream=2)
    layernorm(xtile, b_X, xtile, b_X, lnt=LNT, b_lnt=b_LNT)
    TOK2, b_TOK2 = X, b_X

    def t2tile(tt, TOK2=TOK2):
        return TOK2[:, tt * D:(tt + 1) * D]

    k.barrier()

    b1 = contextlib.ExitStack()
    sm0, b_sm0 = mk(b1, "sm0", [128, 512], side="right")
    dbt, b_dbt = mk(b1, "dbt", [128, 28], side="right")
    ld(dbt[:, :], dbt_d, b_dbt)
    dtmp, b_dtmp = zip(*[mk(b1, f"dtmp{i}", [128, 512], side="right") for i in range(2)])
    khl, b_khl = zip(*[mk(b1, f"khlb{i}", [128, T], dt=F16, side="right")
                       for i in range(2)])
    vp, b_vp = mk(b1, "vpb", [128, NT * 130], side="right")
    qs16, b_qs16 = mk(b1, "qs16b", [128, 512], dt=F16, side="right")
    qsx16, b_qsx16 = mk(b1, "qsx16b", [128, 512], dt=F16, side="right")
    atn, b_atn = zip(*[mk(b1, f"atnb{i}", [64, 512], side="right") for i in range(2)])
    wqs, b_wqs = mk(b1, "wqs", [128, D], side="right")
    wks, b_wks = mk(b1, "wks", [128, D], side="right")
    wvs, b_wvs = mk(b1, "wvs", [128, D], side="right")
    woa, b_woa = mk(b1, "woab", [64, D], side="right")
    wob, b_wob = mk(b1, "wobb", [64, D], side="right")
    crow, b_crow = mk(b1, "crow", [1, D], side="right")
    qraw, b_qraw = zip(*[mk(b1, f"qrawb{i}", [128, 512], side="right") for i in range(2)])

    ld(sm0[:, :], sm0_d, b_sm0)
    k.emit("sd", lambda e: e.dma_start(out=crow[0:1, :], in_=bor_d),
           reads=(b_crow,), writes=(b_crow,), kind="d")
    for tt in range(NT):
        k.emit("gp", lambda e, c=tt: e.memset(vp[:, c * 130 + 64:c * 130 + 65], 1.0),
               writes=(b_vp,))
        k.emit("gp", lambda e, c=tt: e.memset(vp[:, c * 130 + 129:c * 130 + 130], 1.0),
               writes=(b_vp,))

    def load_qkv(p):
        for wsb, b_wsb, wd in ((wqs, b_wqs, wq_b), (wks, b_wks, wk_b),
                               (wvs, b_wvs, wv_b)):
            k.emit("sd", lambda e, wsb=wsb, wd=wd, p=p: e.dma_start(
                out=wsb[:, :],
                in_=bass.AP(wd.tensor, p * 128, [[D, 128], [128 * D, 8], [1, 128]])),
                   reads=(b_wsb,), writes=(b_wsb,), kind="d", dstream=p % 2)

    def load_wo(p):
        k.emit("sd", lambda e, p=p: e.dma_start(
            out=woa[:, :], in_=wo_b[p * 128:p * 128 + 64, :]),
               reads=(b_woa,), writes=(b_woa,), kind="d", dstream=p % 2)
        k.emit("sd", lambda e, p=p: e.dma_start(
            out=wob[:, :], in_=wo_b[p * 128 + 64:p * 128 + 128, :]),
               reads=(b_wob,), writes=(b_wob,), kind="d", dstream=p % 2)

    load_qkv(0)
    load_wo(0)
    for pr in range(NPAIR):
        for side in range(2):
            h = pr * 2 + side
            wo_sb = woa if side == 0 else wob
            b_wo = b_woa if side == 0 else b_wob
            for oc in range(2):
                k.emit("pe", lambda e, h=h, oc=oc, wo_sb=wo_sb: e.matmul(
                    PS[0][0:1, 0:512], smalls[0:64, 16 + h:17 + h],
                    wo_sb[:, oc * 512:(oc + 1) * 512], start=True, stop=True),
                       reads=(b_smalls, b_wo), writes=(b_PS[0],))
                k.emit("dve", lambda e, oc=oc: e.tensor_add(
                    crow[0:1, oc * 512:(oc + 1) * 512],
                    crow[0:1, oc * 512:(oc + 1) * 512], PS[0][0:1, 0:512]),
                       reads=(b_PS[0],), writes=(b_crow,))

        for c in range(4):
            bank = c % 2
            for f in range(NF):
                k.emit("pe", lambda e, f=f, c=c, bk=bank: e.matmul(
                    PS[bk][:, 0:512], wks[:, f * 128:(f + 1) * 128],
                    lnt_mv(f, c * 512, 512), start=(f == 0), stop=(f == NF - 1)),
                       reads=(b_wks, b_LNT), writes=(b_PS[bank],))
            t, b_t = qraw[c % 2], b_qraw[c % 2]
            k.emit("dve", lambda e, t=t, p=pr, bk=bank: e.tensor_scalar(
                t[:, :], PS[bk][:, 0:512],
                scalar1=smalls[:, 8 + p:9 + p], scalar2=None, op0=ALU.add),
                   reads=(b_PS[bank], b_smalls), writes=(b_t,))
            cc0, cc1 = c * 512, (c + 1) * 512
            k.emit("dve", lambda e, t=t, cc0=cc0, cc1=cc1: e.tensor_copy(
                khl[0][0:64, cc0:cc1], t[0:64, :]),
                   reads=(b_t,), writes=(b_khl[0],))
            k.emit("dve", lambda e, t=t, cc0=cc0, cc1=cc1: e.tensor_sub(
                qsx16[0:64, :], t[0:64, :], khl[0][0:64, cc0:cc1]),
                   reads=(b_t, b_khl[0]), writes=(b_qsx16,))
            k.emit("sd", lambda e, cc0=cc0, cc1=cc1: e.dma_start(
                out=khl[0][64:128, cc0:cc1], in_=qsx16[0:64, :]),
                   reads=(b_qsx16,), writes=(b_khl[0],), kind="d", dstream=4)
            k.emit("dve", lambda e, t=t, cc0=cc0, cc1=cc1: e.tensor_copy(
                khl[1][64:128, cc0:cc1], t[64:128, :]),
                   reads=(b_t,), writes=(b_khl[1],))
            k.emit("dve", lambda e, t=t, cc0=cc0, cc1=cc1: e.tensor_sub(
                qs16[64:128, :], t[64:128, :], khl[1][64:128, cc0:cc1]),
                   reads=(b_t, b_khl[1]), writes=(b_qs16,))
            k.emit("sd", lambda e, cc0=cc0, cc1=cc1: e.dma_start(
                out=khl[1][0:64, cc0:cc1], in_=qs16[64:128, :]),
                   reads=(b_qs16,), writes=(b_khl[1],), kind="d", dstream=4)
        for c in range(4):
            bank = c % 2
            qr, b_qr = qraw[c % 2], b_qraw[c % 2]
            for f in range(NF):
                k.emit("pe", lambda e, f=f, c=c, bk=bank: e.matmul(
                    PS[bk][:, 0:512], wvs[:, f * 128:(f + 1) * 128],
                    lnt_mv(f, c * 512, 512), start=(f == 0), stop=(f == NF - 1)),
                       reads=(b_wvs, b_LNT), writes=(b_PS[bank],))
            k.emit("dve", lambda e, bk=bank, qr=qr: e.tensor_copy(
                qr[:, :], PS[bk][:, 0:512]),
                   reads=(b_PS[bank],), writes=(b_qr,))
            vtb = 2 + (c % 4)
            for j in range(4):
                tt = c * 4 + j
                k.emit("pe", lambda e, j=j, vb=vtb, qr=qr: e.transpose(
                    PS[vb][:, 0:128], qr[:, j * 128:(j + 1) * 128], ident[:, :]),
                       reads=(b_qr, b_ident), writes=(b_PS[vtb],))
                k.emit("dve", lambda e, t=tt, vb=vtb: e.tensor_copy(
                    vp[:, t * 130:t * 130 + 64], PS[vb][:, 0:64]),
                       reads=(b_PS[vtb],), writes=(b_vp,))
                k.emit("dve", lambda e, t=tt, vb=vtb: e.tensor_copy(
                    vp[:, t * 130 + 65:t * 130 + 129], PS[vb][:, 64:128]),
                       reads=(b_PS[vtb],), writes=(b_vp,))

        for qc in range(4):
            q0 = qc * 512
            qb = qc % 2
            bank = qb
            for f in range(NF):
                k.emit("pe", lambda e, f=f, qc=qc, bk=bank: e.matmul(
                    PS[bk][:, 0:512], wqs[:, f * 128:(f + 1) * 128],
                    lnt_mv(f, qc * 512, 512), start=(f == 0), stop=(f == NF - 1)),
                       reads=(b_wqs, b_LNT), writes=(b_PS[bank],))
            k.emit("dve", lambda e, p=pr, bk=bank: e.tensor_scalar(
                qraw[1][:, :], PS[bk][:, 0:512], scalar1=smalls[:, 0 + p:1 + p],
                scalar2=None, op0=ALU.add),
                   reads=(b_PS[bank], b_smalls), writes=(b_qraw[1],))
            if qc == 3 and pr + 1 < NPAIR:
                load_qkv(pr + 1)

            def qbuild_b(side):
                t, b_t = qraw[1], b_qraw[1]
                a0, a1r = (0, 64) if side == 0 else (64, 128)
                b0, b1r = (64, 128) if side == 0 else (0, 64)
                k.emit("dve", lambda e, t=t, a0=a0, a1r=a1r: e.tensor_copy(
                    qs16[a0:a1r, :], t[a0:a1r, :]),
                       reads=(b_t,), writes=(b_qs16,))
                k.emit("dve", lambda e, t=t, a0=a0, a1r=a1r: e.tensor_sub(
                    qsx16[a0:a1r, :], t[a0:a1r, :], qs16[a0:a1r, :]),
                       reads=(b_t, b_qs16), writes=(b_qsx16,))
                k.emit("sd", lambda e, a0=a0, a1r=a1r, b0=b0, b1r=b1r:
                       e.dma_start(out=qs16[b0:b1r, :], in_=qsx16[a0:a1r, :]),
                       reads=(b_qsx16,), writes=(b_qs16,), kind="d", dstream=4)
                k.emit("sd", lambda e, a0=a0, a1r=a1r, b0=b0, b1r=b1r:
                       e.dma_start(out=qsx16[b0:b1r, :], in_=qs16[a0:a1r, :]),
                       reads=(b_qs16,), writes=(b_qsx16,), kind="d", dstream=4)

            qbuild_b(0)

            prb = (ptil[0], ptil[1], qraw[0])
            b_prb = (b_ptil[0], b_ptil[1], b_qraw[0])

            def st_exp_b(side, kt, q0=q0, pr=pr):
                h = pr * 2 + side
                kh, b_kh = khl[side], b_khl[side]
                sb = 2 + (kt % 4)
                k.emit("pe", lambda e, kt=kt, sb=sb, kh=kh: e.matmul(
                    PS[sb][:, 0:512], kh[:, kt * 128:(kt + 1) * 128],
                    qs16[:, :], start=True, stop=False),
                       reads=(b_kh, b_qs16), writes=(b_PS[sb],))
                k.emit("pe", lambda e, kt=kt, sb=sb, kh=kh: e.matmul(
                    PS[sb][:, 0:512], kh[:, kt * 128:(kt + 1) * 128],
                    qsx16[:, :], start=False, stop=True),
                       reads=(b_kh, b_qsx16), writes=(b_PS[sb],))
                dt, bdt = dtmp[kt % 2], b_dtmp[kt % 2]
                dbi = (q0 - kt * 128 + 1920) // 128
                k.emit("act", lambda e, dt=dt, dbi=dbi: e.activation(
                    dt[:, :], sm0[:, :], AF.Abs, bias=dbt[:, dbi:dbi + 1]),
                       reads=(b_sm0, b_dbt), writes=(bdt,))
                k.emit("dve", lambda e, sb=sb, dt=dt: e.scalar_tensor_tensor(
                    PS[sb][:, 0:512], dt[:, :], smalls[:, 32:33],
                    PS[sb][:, 0:512], op0=ALU.mult, op1=ALU.add),
                       reads=(bdt, b_smalls, b_PS[sb]), writes=(b_PS[sb],))
                if kt == 0 and h > 0:
                    k.emit("dve", lambda e, sb=sb, h=h: e.tensor_scalar_add(
                        PS[sb][0:h, 0:512], PS[sb][0:h, 0:512], NEGM),
                           reads=(b_PS[sb],), writes=(b_PS[sb],))
                pt, bpt = prb[kt % 3], b_prb[kt % 3]
                k.emit("act", lambda e, sb=sb, pt=pt: e.activation(
                    pt[:, :], PS[sb][:, 0:512], AF.Exp, scale=0.125),
                       reads=(b_PS[sb],), writes=(bpt,))

            def pv_b(side, kt):
                pvb = 6 + side
                voff = 0 if side == 0 else 65
                pt, bpt = prb[kt % 3], b_prb[kt % 3]
                k.emit("pe", lambda e, kt=kt, pt=pt, vo=voff, pvb=pvb: e.matmul(
                    PS[pvb][0:65, 0:512], vp[:, kt * 130 + vo:kt * 130 + vo + 65],
                    pt[:, :], start=(kt == 0), stop=(kt == NT - 1)),
                       reads=(b_vp, bpt), writes=(b_PS[pvb],))

            def pipeline_b(side, LOOK=2):
                for kt in range(NT + LOOK):
                    if kt < NT:
                        st_exp_b(side, kt)
                    if kt >= LOOK:
                        pv_b(side, kt - LOOK)

            def o_proj_b(side, qc=qc, pr=pr):
                wo_sb = woa if side == 0 else wob
                b_wo = b_woa if side == 0 else b_wob
                last = (pr == NPAIR - 1 and side == 1)
                for j in range(4):
                    tt = qc * 4 + j
                    obk = j % 2
                    for oc in range(2):
                        k.emit("pe", lambda e, j=j, oc=oc, bk=obk, wo_sb=wo_sb,
                               side=side, lst=last: e.matmul(
                            PS[bk][:, 0:512], atn[side][:, j * 128:(j + 1) * 128],
                            wo_sb[:, oc * 512:(oc + 1) * 512],
                            start=True, stop=not lst),
                               reads=(b_atn[side], b_wo), writes=(b_PS[obk],))
                        if last:
                            rank1(obk, 0, oc * 512, 512, False, True,
                                  b_rowsrc=b_crow, rowsrc=crow)
                        k.emit("dve", lambda e, tt=tt, oc=oc, bk=obk: e.tensor_add(
                            TOK2[:, tt * D + oc * 512:tt * D + (oc + 1) * 512],
                            TOK2[:, tt * D + oc * 512:tt * D + (oc + 1) * 512],
                            PS[bk][:, 0:512]),
                               reads=(b_PS[obk], b_TOK2), writes=(b_TOK2,))

            pipeline_b(0)
            pv_recip(6)
            qbuild_b(1)
            pipeline_b(1)
            pv_bcast_evac(atn[0][:, :], b_atn[0], 6, 2 + (NT % 4))
            pv_recip(7)
            o_proj_b(0)
            pv_bcast_evac(atn[1][:, :], b_atn[1], 7, 2 + (NT % 4))
            o_proj_b(1)
        if pr + 1 < NPAIR:
            load_wo(pr + 1)
    b1.close()
    k.barrier()

    ga2 = contextlib.ExitStack()
    g2r, b_g2r = mk(ga2, "g2r", [128, D], side="right")
    b2lr, b_b2lr = mk(ga2, "b2lr", [128, D], side="right")
    b1h, b_b1h = mk(ga2, "b1h", [128, 32], side="right")
    ga = contextlib.ExitStack()
    g1r, b_g1r = mk(ga, "g1r", [128, D], side="right")
    b1r, b_b1r = mk(ga, "b1r", [128, D], side="right")
    for t, d, b in ((g1r, g1r_d, b_g1r), (b1r, b1r_d, b_b1r),
                    (g2r, g2r_d, b_g2r), (b2lr, b2lr_d, b_b2lr),
                    (b1h, b1h_d, b_b1h)):
        k.emit("sd", lambda e, t=t, d=d: e.dma_start(out=t[:, :], in_=d),
               reads=(b,), writes=(b,), kind="d")
    k.barrier()
    layernorm(t2tile, b_TOK2, t2tile, b_TOK2, lnt=LNT, b_lnt=b_LNT,
              gain=g1r, b_gain=b_g1r, bias=b1r, b_bias=b_b1r)
    ga.close()
    k.barrier()

    m1 = contextlib.ExitStack()
    w1c, b_w1c = [None, None], [None, None]
    w2c, b_w2c = [None, None], [None, None]
    for i in range(2):
        w1c[i], b_w1c[i] = mk(m1, f"w1c{i}", [128, NF * 256], side="right")
        w2c[i], b_w2c[i] = mk(m1, f"w2c{i}", [128, 2 * D], side="right")
    hT, b_hT = mk(m1, "hT", [128, 2 * T], side="right")
    rows, b_rows = mk(m1, "rowsm", [128, D], side="right")
    k.emit("sd", lambda e: e.dma_start(out=rows[32:33, :], in_=b2r_d),
           reads=(b_rows,), writes=(b_rows,), kind="d")
    NHC = DFF // 256
    for hc in range(NHC):
        sl = hc % 2
        k.emit("sd", lambda e, hc=hc, sl=sl: e.dma_start(
            out=w1c[sl][:, :],
            in_=bass.AP(w1.tensor, hc * 256, [[DFF, 128], [128 * DFF, NF], [1, 256]])),
               reads=(b_w1c[sl],), writes=(b_w1c[sl],), kind="d", dstream=sl)
        k.emit("sd", lambda e, hc=hc, sl=sl: e.dma_start(
            out=w2c[sl][:, :],
            in_=bass.AP(w2.tensor, hc * 256 * D, [[D, 128], [128 * D, 2], [1, D]])),
               reads=(b_w2c[sl],), writes=(b_w2c[sl],), kind="d", dstream=sl)
        for ht in range(2):
            for c in range(4):
                bank = c % 2
                for f in range(NF):
                    k.emit("pe", lambda e, f=f, c=c, ht=ht, sl=sl, bk=bank: e.matmul(
                        PS[bk][:, 0:512],
                        w1c[sl][:, f * 256 + ht * 128:f * 256 + (ht + 1) * 128],
                        lnt_mv(f, c * 512, 512), start=(f == 0), stop=(f == NF - 1)),
                           reads=(b_w1c[sl], b_LNT), writes=(b_PS[bank],))
                hcol = hc * 2 + ht
                k.emit("act", lambda e, c=c, ht=ht, hcol=hcol, bk=bank: e.activation(
                    hT[:, ht * T + c * 512:ht * T + (c + 1) * 512], PS[bk][:, 0:512],
                    AF.Relu, bias=b1h[:, hcol:hcol + 1]),
                       reads=(b_PS[bank], b_b1h), writes=(b_hT,))
        for tt in range(NT):
            bank = 2 + (tt % 4)
            for oc in range(2):
                for ht in range(2):
                    lastmm = (ht == 1)
                    k.emit("pe", lambda e, tt=tt, oc=oc, ht=ht, sl=sl, bk=bank,
                           lm=lastmm, lh=(hc == NHC - 1): e.matmul(
                        PS[bk][:, 0:512],
                        hT[:, ht * T + tt * 128:ht * T + (tt + 1) * 128],
                        w2c[sl][:, ht * D + oc * 512:ht * D + (oc + 1) * 512],
                        start=(ht == 0), stop=(lm and not lh)),
                           reads=(b_hT, b_w2c[sl]), writes=(b_PS[bank],))
                if hc == NHC - 1:
                    rank1(bank, 32, oc * 512, 512, False, True, b_rows, rows)
                k.emit("dve", lambda e, tt=tt, oc=oc, bk=bank: e.tensor_add(
                    TOK2[:, tt * D + oc * 512:tt * D + (oc + 1) * 512],
                    TOK2[:, tt * D + oc * 512:tt * D + (oc + 1) * 512],
                    PS[bk][:, 0:512]),
                       reads=(b_PS[bank], b_TOK2), writes=(b_TOK2,))
    m1.close()
    k.barrier()

    layernorm(t2tile, b_TOK2, t2tile, b_TOK2,
              gain=g2r, b_gain=b_g2r, bias=b2lr, b_bias=b_b2lr)
    ga2.close()
    k.barrier()

    k.emit("gp", lambda e: e.dma_start(out=xsp_out, in_=TOK2[:, :],
                                       accum_op=ALU.add),
           reads=(b_TOK2,), kind="d")
    k.barrier()
    x1stk.close()
    X2, b_X2 = mk(xs, f"X2_{s}", [128, NT * D])
    xsp_in = bass.AP(xspill, 0, [[D, 128], [128 * D, NT], [1, D]])
    k.emit("sd", lambda e: e.dma_start(out=X2[:, :], in_=xsp_in),
           writes=(b_X2,), kind="d", dstream=2)
    k.barrier()
    return X2, b_X2




def _host_constants():
    c = {}
    c["identc"] = np.eye(128, dtype=np.float32)
    perm64 = np.array([2 * j for j in range(32)] + [2 * j + 1 for j in range(32)])
    perm2 = np.concatenate([perm64, perm64 + 64])
    p2 = np.zeros((128, 128), np.float32)
    p2[perm2, np.arange(128)] = 1.0
    c["p2c"] = p2
    c["onesc"] = np.ones((128, 128), np.float32)
    u = np.arange(1152)[None, :]
    p = np.arange(128)[:, None]
    c["cmc"] = np.where(u < p + 512, np.float32(NEGM), np.float32(0.0)).astype(np.float32)
    j = np.arange(512)[None, :]
    c["sm0c"] = (j - p).astype(np.float32)
    i = np.arange(32, dtype=np.float64)
    inv = 10000.0 ** (2.0 * i / 64.0)
    ang = np.arange(T, dtype=np.float64)[None, :] / inv[:, None]
    cos = np.cos(ang).astype(np.float32)
    sin = np.sin(ang).astype(np.float32)
    c["c64c"] = np.concatenate([cos, cos, cos, cos], axis=0)
    c["s64c"] = np.concatenate([-sin, sin, -sin, sin], axis=0)
    return c


def kernel(L0_attn_a_x, L0_attn_a_W_qkv, L0_attn_a_W_o,
           L0_attn_b_Wq, L0_attn_b_bq, L0_attn_b_Wk, L0_attn_b_bk,
           L0_attn_b_Wv, L0_attn_b_bv, L0_attn_b_Wo, L0_attn_b_bo,
           L0_attn_b_dist_scale, L0_attn_b_W1, L0_attn_b_b1,
           L0_attn_b_W2, L0_attn_b_b2, L0_attn_b_ln1_g, L0_attn_b_ln1_b,
           L0_attn_b_ln2_g, L0_attn_b_ln2_b, L0_ffn_CB, L0_ffn_SCB, L0_ffn_bias):
    f32 = np.float32
    asn = lambda a: np.ascontiguousarray(np.asarray(a, dtype=f32))
    x = asn(L0_attn_a_x)
    rep = lambda v: np.ascontiguousarray(
        np.broadcast_to(asn(v).reshape(1, D), (128, D)))
    shared = {
        "wqkv": asn(L0_attn_a_W_qkv), "wo_a": asn(L0_attn_a_W_o),
        "wq_b": asn(L0_attn_b_Wq), "wk_b": asn(L0_attn_b_Wk),
        "wv_b": asn(L0_attn_b_Wv), "wo_b": asn(L0_attn_b_Wo),
        "w1": asn(L0_attn_b_W1), "w2": asn(L0_attn_b_W2), "cb": asn(L0_ffn_CB),
        "bqhc": np.ascontiguousarray(asn(L0_attn_b_bq).reshape(8, 128).T),
        "bkhc": np.ascontiguousarray(asn(L0_attn_b_bk).reshape(8, 128).T),
        "bvfc": np.ascontiguousarray(asn(L0_attn_b_bv).reshape(16, 64).T),
        "b1hc": np.ascontiguousarray(asn(L0_attn_b_b1).reshape(32, 128).T),
        "b2rc": asn(L0_attn_b_b2).reshape(1, D),
        "borc": asn(L0_attn_b_bo).reshape(1, D),
        "fbrc": asn(L0_ffn_bias).reshape(1, D),
        "scbc": rep(L0_ffn_SCB),
        "g1rc": rep(L0_attn_b_ln1_g), "b1rc": rep(L0_attn_b_ln1_b),
        "g2rc": rep(L0_attn_b_ln2_g), "b2lrc": rep(L0_attn_b_ln2_b),
        "dscc": asn(L0_attn_b_dist_scale).reshape(1, 1),
        "epsc": np.full((128, 1), EPS, dtype=f32),
        "dbtc": np.ascontiguousarray(np.broadcast_to(
            (np.arange(28, dtype=f32) * 128 - 1920)[None, :], (128, 28))),
    }
    shared.update(_host_constants())
    n = 8
    in_maps = []
    for c in range(n):
        im = dict(shared)
        im["x"] = np.ascontiguousarray(x[c * B_LOC:(c + 1) * B_LOC])
        in_maps.append(im)
    nc, _ = build_program(STAGE)
    res = run_bass_kernel_spmd(nc, in_maps, core_ids=list(range(n)))
    return np.concatenate([r["y"] for r in res.results], axis=0)
